# version 02
# baseline (speedup 1.0000x reference)
import os
import contextlib
import numpy as np
import concourse.bass as bass
import concourse.mybir as mybir
from concourse.bass_utils import run_bass_kernel_spmd

F32 = mybir.dt.float32
BF16 = mybir.dt.bfloat16
I32 = mybir.dt.int32
AF = mybir.ActivationFunctionType
ALU = mybir.AluOpType
AX = mybir.AxisListType

NCORES = 8
B, S, D = 2, 8192, 1024
NTOK = B * S
TOK = NTOK // NCORES
NT = TOK // 128


_ENGS = ("y", "v", "s", "t", "g")


def _apkey(ap):
    t = getattr(ap, "tensor", None)
    if t is not None:
        return t.name
    return getattr(ap, "name", None)


def _asap(x):
    if x is None or isinstance(x, (int, float, bass.AP)):
        return x
    if hasattr(x, "ap"):
        return x.ap()
    return x



def _keys(custom, aps):
    out = list(custom)
    for a in aps:
        if a is None or isinstance(a, (int, float)):
            continue
        k = _apkey(_asap(a))
        if k and k not in out:
            out.append(k)
    return out

NDMASLOT = {"y": 12, "v": 4, "s": 8, "g": 8, "t": 2}


class Prog:
    def __init__(self, nc, es):
        self.nc = nc
        self.es = es
        self.sem = {}
        self.cnt = {}
        for tag in ("y", "v", "s", "t", "g", "cc"):
            self.sem[tag] = es.enter_context(nc.semaphore("sem_" + tag))
            self.cnt[tag] = 0
        for e, n in NDMASLOT.items():
            for sl in range(n):
                tag = (e, sl)
                self.sem[tag] = es.enter_context(nc.semaphore(f"dma_{e}{sl}"))
                self.cnt[tag] = 0
        self.dma_idx = {e: 0 for e in NDMASLOT}
        self.q = {e: [] for e in _ENGS}
        self.last_w = {}
        self.readers = {}
        self.known = {e: {} for e in _ENGS}
        self.snaps = {tag: {} for tag in self.sem}
        self.nwaits = 0
        self.nops = {e: 0 for e in _ENGS}

    def _need_for(self, reads, writes, skip_tags=()):
        need = {}
        for k in reads:
            lw = self.last_w.get(k)
            if lw and lw[0] not in skip_tags:
                need[lw[0]] = max(need.get(lw[0], 0), lw[1])
        for k in writes:
            lw = self.last_w.get(k)
            if lw and lw[0] not in skip_tags:
                need[lw[0]] = max(need.get(lw[0], 0), lw[1])
            for rt, rc in self.readers.get(k, {}).items():
                if rt not in skip_tags:
                    need[rt] = max(need.get(rt, 0), rc)
        return need

    def _emit_waits(self, e, need):
        kn = self.known[e]
        for t_, c_ in sorted(need.items(), key=lambda kv: str(kv[0])):
            if kn.get(t_, 0) >= c_:
                continue
            sem = self.sem[t_]
            self.q[e].append(lambda eng, sem=sem, c=c_: eng.wait_ge(sem, c))
            self.nwaits += 1
            snap = self.snaps[t_].get(c_)
            if snap:
                for st_, sc_ in snap.items():
                    if kn.get(st_, 0) < sc_:
                        kn[st_] = sc_
            if kn.get(t_, 0) < c_:
                kn[t_] = c_

    def op(self, e, fn, reads=(), writes=(), dma=False, chain=None, tag_override=None):
        selfser = True
        if tag_override is not None:
            tag = tag_override
            inc = 1
        elif dma:
            slot = self.dma_idx[e] % NDMASLOT[e]
            self.dma_idx[e] += 1
            tag = (e, slot)
            inc = 16
        else:
            tag = e
            inc = 1
            selfser = False
        if chain in ("mid", "end"):
            need = self._need_for(reads, ())
        else:
            need = self._need_for(reads, writes)
        if selfser and self.cnt[tag] > 0:
            need[tag] = max(need.get(tag, 0), self.cnt[tag])
        self._emit_waits(e, need)
        self.cnt[tag] += inc
        c = self.cnt[tag]
        sem = self.sem[tag]
        import sys as _sys
        _ln = []
        _f = _sys._getframe(1)
        for _ in range(4):
            if _f is None:
                break
            _ln.append(_f.f_lineno)
            _f = _f.f_back

        def _run(eng, fn=fn, sem=sem, inc=inc, _ln=_ln):
            try:
                return fn(eng).then_inc(sem, inc)
            except Exception as ex:
                raise RuntimeError(f"emit site lines {_ln}: {ex}") from ex
        self.q[e].append(_run)
        self.nops[e] += 1
        kn = dict(self.known[e])
        kn[tag] = c
        self.snaps[tag][c] = kn
        for k in reads:
            self.readers.setdefault(k, {})[tag] = c
        if chain in ("start", "mid"):
            pass
        else:
            for k in writes:
                self.last_w[k] = (tag, c)
                self.readers[k] = {}
        return c

    def dma(self, e, out, in_, rk=(), wk=(), **kw):
        out = _asap(out); in_ = _asap(in_)
        reads = _keys(rk, [in_])
        writes = _keys(wk, [out])
        return self.op(e, lambda eng: eng.dma_start(out=out, in_=in_, **kw),
                       reads=reads, writes=writes, dma=True)

    def cp(self, e, out, in_, rk=(), wk=()):
        out = _asap(out); in_ = _asap(in_)
        reads = _keys(rk, [in_])
        writes = _keys(wk, [out])
        if e == "s":
            return self.op(e, lambda eng: eng.copy(out, in_), reads, writes)
        return self.op(e, lambda eng: eng.tensor_copy(out, in_), reads, writes)

    def act(self, out, in_, func, bias=0.0, scale=1.0, accum_out=None,
            rk=(), wk=(), e="s"):
        out = _asap(out); in_ = _asap(in_); bias = _asap(bias); scale = _asap(scale); accum_out = _asap(accum_out)
        reads = _keys(rk, [in_, bias if isinstance(bias, bass.AP) else None,
                           scale if isinstance(scale, bass.AP) else None])
        writes = _keys(wk, [out, accum_out])
        return self.op(e, lambda eng: eng.activation(out, in_, func, bias=bias, scale=scale,
                                                     accum_out=accum_out), reads, writes)

    def tt(self, e, out, in0, in1, alu, rk=(), wk=()):
        out = _asap(out); in0 = _asap(in0); in1 = _asap(in1)
        reads = _keys(rk, [in0, in1])
        writes = _keys(wk, [out])
        return self.op(e, lambda eng: eng.tensor_tensor(out, in0, in1, alu), reads, writes)

    def ts(self, e, out, in0, s1, s2, op0, op1=None, rk=(), wk=()):
        out = _asap(out); in0 = _asap(in0); s1 = _asap(s1); s2 = _asap(s2)
        reads = _keys(rk, [in0, s1 if isinstance(s1, bass.AP) else None,
                           s2 if isinstance(s2, bass.AP) else None])
        writes = _keys(wk, [out])
        if op1 is None:
            return self.op(e, lambda eng: eng.tensor_scalar(out, in0, s1, None, op0), reads, writes)
        return self.op(e, lambda eng: eng.tensor_scalar(out, in0, s1, s2, op0, op1), reads, writes)

    def stt(self, e, out, in0, s, in1, op0, op1, rk=(), wk=()):
        out = _asap(out); in0 = _asap(in0); in1 = _asap(in1); s = _asap(s)
        reads = _keys(rk, [in0, in1, s if isinstance(s, bass.AP) else None])
        writes = _keys(wk, [out])
        return self.op(e, lambda eng: eng.scalar_tensor_tensor(out, in0, s, in1, op0, op1), reads, writes)

    def red(self, e, out, in_, alu, rk=(), wk=(), negate=None, absval=None):
        out = _asap(out); in_ = _asap(in_)
        reads = _keys(rk, [in_])
        writes = _keys(wk, [out])
        return self.op(e, lambda eng: eng.tensor_reduce(out, in_, AX.X, alu,
                                                        apply_absolute_value=absval, negate=negate),
                       reads, writes)

    def memset(self, e, ap, val, wk=()):
        ap = _asap(ap)
        writes = _keys(wk, [ap])
        return self.op(e, lambda eng: eng.memset(ap, val), (), writes)

    def mm(self, psum, lhsT, rhs, start, stop, rk=(), wk=(), tile_position=None):
        psum = _asap(psum); lhsT = _asap(lhsT); rhs = _asap(rhs)
        reads = _keys(rk, [lhsT, rhs])
        writes = _keys(wk, [psum])
        chain = "start" if (start and not stop) else ("end" if (stop and not start) else
                                                      ("mid" if (not start and not stop) else None))
        return self.op("t", lambda eng: eng.matmul(psum, lhsT, rhs, start=start, stop=stop,
                                                   tile_position=tile_position),
                       reads, writes, chain=chain)

    def transpose(self, psum_out, in_, ident, rk=(), wk=()):
        psum_out = _asap(psum_out); in_ = _asap(in_); ident = _asap(ident)
        reads = _keys(rk, [in_, ident])
        writes = _keys(wk, [psum_out])
        return self.op("t", lambda eng: eng.transpose(psum_out, in_, ident), reads, writes)

    def collective(self, kind, alu, ins, outs, rk=(), wk=()):
        groups = [list(range(NCORES))]
        ins = [_asap(a) for a in ins]
        outs = [_asap(a) for a in outs]
        reads = list(rk) or [x for x in (_apkey(a) for a in ins) if x]
        writes = list(wk) or [x for x in (_apkey(a) for a in outs) if x]
        return self.op("g", lambda eng: eng.collective_compute(
            kind, alu, groups, [a.opt() for a in ins], [a.opt() for a in outs]),
            reads=reads, writes=writes, tag_override="cc")

    def barrier_on(self, e, keys):
        need = self._need_for(keys, ())
        self._emit_waits(e, need)

    def build_block(self):
        nc = self.nc
        with nc.Block() as block:
            @block.sync
            def _(eng):
                for f in self.q["y"]:
                    f(eng)

            @block.vector
            def _(eng):
                for f in self.q["v"]:
                    f(eng)

            @block.scalar
            def _(eng):
                for f in self.q["s"]:
                    f(eng)

            @block.tensor
            def _(eng):
                for f in self.q["t"]:
                    f(eng)

            @block.gpsimd
            def _(eng):
                for f in self.q["g"]:
                    f(eng)



def round_nearest(P, e, y, sc, key=""):
    y = _asap(y)
    P.cp(e, sc["i32"], y)
    P.cp(e, sc["t"], sc["i32"])
    P.tt(e, sc["a"], y, sc["t"], ALU.subtract)
    P.ts(e, sc["b"], sc["a"], -0.5, None, ALU.is_lt)
    P.ts(e, sc["a"], sc["a"], 0.5, None, ALU.is_gt)
    P.tt(e, sc["a"], sc["a"], sc["b"], ALU.subtract)
    P.tt(e, y, sc["t"], sc["a"], ALU.add)


def emit_layernorm_tile(P, x_ap, out_ap, width, sc, eps=1e-5):
    P.red("v", sc["mean"], x_ap, ALU.add, negate=True)
    P.ts("v", sc["mean"], sc["mean"], 1.0 / width, None, ALU.mult)
    P.act(sc["xc"], x_ap, AF.Identity, bias=sc["mean"], e="s")
    P.act(sc["sq"], sc["xc"], AF.Square, accum_out=sc["var"], e="s")
    P.act(sc["var"], sc["var"], AF.Sqrt, bias=sc["eps"], scale=1.0 / width, e="s")
    P.op("v", lambda eng, o_=_asap(sc["var"]): eng.reciprocal(o_, o_),
         reads=[_apkey(_asap(sc["var"]))], writes=[_apkey(_asap(sc["var"]))])
    P.ts("v", out_ap, sc["xc"], sc["var"], None, ALU.mult,
         rk=[_apkey(sc["xc"]), _apkey(sc["var"])], wk=[_apkey(out_ap)])


def sb_tiles(nc, es, name, n, shape, dtype=F32):
    return [es.enter_context(nc.sbuf_tensor(f"{name}{i}", shape, dtype)) for i in range(n)]


def quant_codes(P, e, out, x, scale, lo, hi, sc):
    if isinstance(scale, bass.AP):
        P.ts(e, out, x, scale, None, ALU.mult)
    else:
        P.ts(e, out, x, float(scale), None, ALU.mult)
    round_nearest(P, e, out, sc)
    P.ts(e, out, out, float(hi), float(lo), ALU.min, ALU.max)


def bcast_row_to_partitions(P, psum_out, ones_col, row_ap, rk, wk):
    P.mm(psum_out, ones_col, row_ap, True, True, rk=rk + [_apkey(ones_col)], wk=wk)



def _f32c(a):
    if os.environ.get("BASS_SIM_FP64") == "1":
        return np.asarray(a)
    return np.ascontiguousarray(np.asarray(a), dtype=np.float32)


def _shards_tm(x_flat):
    return [np.ascontiguousarray(x_flat[c * TOK:(c + 1) * TOK]) for c in range(NCORES)]


def _fm_batches(x_flat):
    xb = x_flat.reshape(B, S, D)
    fms = [np.ascontiguousarray(xb[b].T) for b in range(B)]
    return [fms[c // (NCORES // B)] for c in range(NCORES)]


def _tm_batches(x_flat):
    xb = x_flat.reshape(B, S, D)
    tms = [np.ascontiguousarray(xb[b]) for b in range(B)]
    return [tms[c // (NCORES // B)] for c in range(NCORES)]


_IDENT32 = np.eye(128, dtype=np.float32)


def _launch(build_fn, in_maps, out_keys):
    nc = bass.Bass("TRN2", target_bir_lowering=False, debug=False)
    es = contextlib.ExitStack()
    with es:
        P = Prog(nc, es)
        build_fn(nc, P, es)
        P.build_block()
    res = run_bass_kernel_spmd(nc, in_maps, core_ids=list(range(NCORES)))
    return [{k: r[k] for k in out_keys} for r in res.results]



def build_l1f(nc, P, es):
    x_d = nc.dram_tensor("x", [TOK, D], F32, kind="ExternalInput").ap()
    w1t_d = nc.dram_tensor("w1t", [D, 4096], F32, kind="ExternalInput").ap()
    b1c_d = nc.dram_tensor("b1c", [128, 32], F32, kind="ExternalInput").ap()
    w2t_d = nc.dram_tensor("w2t", [4096, D], F32, kind="ExternalInput").ap()
    b2r_d = nc.dram_tensor("b2r", [1, D], F32, kind="ExternalInput").ap()
    idf_d = nc.dram_tensor("identf", [128, 128], F32, kind="ExternalInput").ap()
    xo_d = nc.dram_tensor("xo", [TOK, D], F32, kind="ExternalOutput").ap()
    w2c_d = nc.dram_tensor("w2codes", [32, 128, D], BF16, kind="Internal").ap()

    sb = lambda n, shp, dt=F32: es.enter_context(nc.sbuf_tensor(n, shp, dt))
    w1b = sb_tiles(nc, es, "w1b", 8, [128, 4096], BF16)
    stage = sb("stage", [128, 1024])
    qsc = {"i32": es.enter_context(nc.sbuf_tensor("q_i32", [128, 1024], I32)),
           "t": sb("q_t", [128, 1024]), "a": sb("q_a", [128, 1024]), "b": sb("q_b", [128, 1024])}
    idf = sb("idf", [128, 128])
    b1c = sb("b1c_s", [128, 32])
    b2r = sb("b2r_s", [1, 1024])
    b2b = sb("b2b", [1, 1024], BF16)
    sixteen = sb("sixteen", [1, 128], BF16)
    xt = sb_tiles(nc, es, "xt", 4, [128, 1024])
    xq = sb("xq", [128, 1024])
    xqf = sb_tiles(nc, es, "xqf", 8, [128, 512], BF16)
    hq = sb_tiles(nc, es, "hq", 32, [128, 512], BF16)
    w2b = sb_tiles(nc, es, "w2b", 4, [128, 1024], BF16)
    hev = sb_tiles(nc, es, "hev", 2, [128, 512])
    qsc2 = [{"i32": es.enter_context(nc.sbuf_tensor(f"q2_i32_{p}", [128, 512], I32)),
             "t": sb(f"q2_t_{p}", [128, 512]), "a": sb(f"q2_a_{p}", [128, 512]),
             "b": sb(f"q2_b_{p}", [128, 512])} for p in range(2)]
    ln_sc = {"mean": sb("ln_mean", [128, 1]), "var": sb("ln_var", [128, 1]),
             "xc": sb("ln_xc", [128, 1024]), "sq": sb("ln_sq", [128, 1024]),
             "eps": sb("ln_eps", [128, 1])}
    P.memset("v", ln_sc["eps"], 1e-5)
    res = sb("res", [128, 1024])
    outt = sb("outt", [128, 1024])
    ps = [es.enter_context(nc.psum_tensor(f"ps{i}", [128, 512], F32)) for i in range(8)]

    P.dma("y", idf[:], idf_d)
    P.memset("v", sixteen[:], 16.0)

    def wquant(dst_ap, src_dram_ap, e="v"):
        P.dma("y", stage[:], src_dram_ap)
        quant_codes(P, e, xq, stage[:], 64.0, -128.0, 127.0, qsc)
        P.cp(e, dst_ap, xq[:])

    for j in range(8):
        for a in range(4):
            wquant(w1b[j][:, a * 1024:(a + 1) * 1024], w1t_d[j * 128:(j + 1) * 128, a * 1024:(a + 1) * 1024])
    for j in range(32):
        wquant(w2b[0][:, :], w2t_d[j * 128:(j + 1) * 128, :])
        P.dma("g", w2c_d[j], w2b[0][:, :])
    P.dma("y", b1c[:], b1c_d)
    quant_codes(P, "v", b1c[:], b1c[:], 64.0, -128.0, 127.0,
                {k: v[:, 0:32] for k, v in qsc.items()})
    P.ts("v", b1c[:], b1c[:], 0.25, None, ALU.mult)
    P.dma("y", b2r[:], b2r_d)
    quant_codes(P, "v", b2r[:], b2r[:], 64.0, -128.0, 127.0,
                {k: v[0:1, :] for k, v in qsc.items()})
    P.cp("v", b2b[:], b2r[:])

    for tb in range(4):
        for i in range(4):
            ti = tb * 4 + i
            P.dma("y", xt[i][:], x_d[ti * 128:(ti + 1) * 128, :])
            quant_codes(P, "v", xq, xt[i][:], 16.0, -128.0, 127.0, qsc)
            for j in range(8):
                P.transpose(ps[0][:, 0:128], xq[:, j * 128:(j + 1) * 128], idf[:])
                P.cp("s", xqf[j][:, i * 128:(i + 1) * 128], ps[0][:, 0:128])
        for f in range(32):
            pb = ps[1 + f % 2]
            hb = hev[0 if f % 2 == 0 else 1]
            for dj in range(8):
                P.mm(pb[:], w1b[dj][:, f * 128:(f + 1) * 128], xqf[dj][:],
                     start=(dj == 0), stop=(dj == 7))
            P.act(hb[:], pb[:], AF.Identity, bias=b1c[:, f:f + 1], scale=1.0 / 64.0, e="s")
            round_nearest(P, "v", hb[:], qsc2[f % 2])
            P.ts("v", hb[:], hb[:], 127.0, 0.0, ALU.min, ALU.max)
            P.cp("s", hq[f][:], hb[:])
        for f in range(4):
            P.dma("y", w2b[f][:], w2c_d[f])
        for f in range(32):
            buf = w2b[f % 4]
            for i in range(4):
                for ob in range(2):
                    P.mm(ps[i * 2 + ob][:], hq[f][:, i * 128:(i + 1) * 128],
                         buf[:, ob * 512:(ob + 1) * 512],
                         start=(f == 0), stop=False)
            if f + 4 < 32:
                P.dma("y", w2b[(f + 4) % 4][:], w2c_d[f + 4])
        for i in range(4):
            ti = tb * 4 + i
            for ob in range(2):
                P.mm(ps[i * 2 + ob][:], sixteen[:], b2b[:, ob * 512:(ob + 1) * 512],
                     start=False, stop=True)
                P.stt("v", res[:, ob * 512:(ob + 1) * 512], ps[i * 2 + ob][:],
                      1.0 / 1024.0, xt[i][:, ob * 512:(ob + 1) * 512],
                      ALU.mult, ALU.add)
            emit_layernorm_tile(P, res[:], outt[:], D, ln_sc)
            P.dma("g", xo_d[ti * 128:(ti + 1) * 128, :], outt[:])


def _run_l1f(x_shards, w1, b1, w2, b2):
    w1t = np.ascontiguousarray(w1.T)
    b1c = np.ascontiguousarray(b1.reshape(32, 128).T)
    w2t = np.ascontiguousarray(w2.T)
    b2r = np.ascontiguousarray(b2.reshape(1, D))
    in_maps = [{"x": x_shards[c], "w1t": w1t, "b1c": b1c, "w2t": w2t,
                "b2r": b2r, "identf": _IDENT32} for c in range(NCORES)]
    return _launch(build_l1f, in_maps, ["xo"])



def build_l0f(nc, P, es):
    x_d = nc.dram_tensor("x", [TOK, D], F32, kind="ExternalInput").ap()
    wgt_d = nc.dram_tensor("wgt", [D, 4096], F32, kind="ExternalInput").ap()
    wut_d = nc.dram_tensor("wut", [D, 4096], F32, kind="ExternalInput").ap()
    wdt_d = nc.dram_tensor("wdt", [4096, D], F32, kind="ExternalInput").ap()
    idf_d = nc.dram_tensor("identf", [128, 128], F32, kind="ExternalInput").ap()
    xo_d = nc.dram_tensor("xo", [TOK, D], F32, kind="ExternalOutput").ap()
    wgc_d = nc.dram_tensor("wgc", [8, 128, 4096], BF16, kind="Internal").ap()
    wuc_d = nc.dram_tensor("wuc", [8, 128, 4096], BF16, kind="Internal").ap()
    wdc_d = nc.dram_tensor("wdc", [32, 128, D], BF16, kind="Internal").ap()

    sb = lambda n, shp, dt=F32: es.enter_context(nc.sbuf_tensor(n, shp, dt))
    stage = sb("stage", [128, 2048])
    qsc = {"i32": es.enter_context(nc.sbuf_tensor("q_i32", [128, 1024], I32)),
           "t": sb("q_t", [128, 1024]), "a": sb("q_a", [128, 1024]), "b": sb("q_b", [128, 1024])}
    codes = sb("codes", [128, 1024])
    cbuf = sb("cbuf", [128, 1024], BF16)
    idf = sb("idf", [128, 128])
    onecol = sb("onecol", [128, 1])
    onerow = sb("onerow", [1, 128])
    acc1 = sb("acc1", [128, 1])
    red1 = sb("red1", [128, 1])
    scal = sb("scal", [1, 8])
    swcol = sb("swcol", [128, 3])
    swq = sb("swq", [128, 1])
    ps = [es.enter_context(nc.psum_tensor(f"ps{i}", [128, 512], F32)) for i in range(8)]
    ps_s = ps[0]
    P.dma("y", idf[:], idf_d)
    P.memset("v", onecol[:], 1.0)
    P.memset("v", onerow[:], 1.0)

    def weight_scale(src_c, nchunk, width, idx):
        P.memset("v", acc1[:], 0.0)
        for j in range(nchunk):
            for w0 in range(0, width, 2048):
                ww = min(2048, width - w0)
                P.dma("y", stage[:, 0:ww], src_c[j][:, w0:w0 + ww])
                P.red("v", red1[:], stage[:, 0:ww], ALU.add, absval=True)
                P.tt("v", acc1[:], acc1[:], red1[:], ALU.add)
        P.mm(ps_s[0:1, 0:1], onecol[:], acc1[:], True, True)
        P.ts("v", scal[0:1, idx:idx + 1], ps_s[0:1, 0:1],
             1.0 / (nchunk * 128 * width), 1e-5, ALU.mult, ALU.max)
        P.op("v", lambda eng: eng.reciprocal(scal[0:1, idx + 4:idx + 5], scal[0:1, idx:idx + 1]),
             reads=["scal"], writes=["scal"])
        P.mm(ps_s[:, 0:1], onerow[:], scal[0:1, idx:idx + 1], True, True)
        P.cp("s", swcol[:, idx:idx + 1], ps_s[:, 0:1])

    def weight_quant(src_c, dst_d, nchunk, width, idx):
        P.mm(ps_s[:, 0:1], onerow[:], scal[0:1, idx + 4:idx + 5], True, True)
        P.cp("s", swq[:], ps_s[:, 0:1])
        for j in range(nchunk):
            for w0 in range(0, width, 2048):
                ww = min(2048, width - w0)
                P.dma("y", stage[:, 0:ww], src_c[j][:, w0:w0 + ww])
                for sl in range(ww // 1024):
                    c = codes[:, 0:1024]
                    P.ts("v", c, stage[:, sl * 1024:(sl + 1) * 1024], swq[:], None, ALU.mult)
                    round_nearest(P, "v", c, qsc)
                    P.ts("v", c, c, 1.0, -1.0, ALU.min, ALU.max)
                    P.cp("v", cbuf[:, 0:1024], c)
                    P.dma("g", dst_d[j][:, w0 + sl * 1024:w0 + (sl + 1) * 1024], cbuf[:, 0:1024])

    wgt_c = wgt_d.rearrange("(c p) f -> c p f", p=128)
    wut_c = wut_d.rearrange("(c p) f -> c p f", p=128)
    wdt_c = wdt_d.rearrange("(c p) f -> c p f", p=128)
    weight_scale(wgt_c, 8, 4096, 0)
    weight_scale(wut_c, 8, 4096, 1)
    weight_scale(wdt_c, 32, 1024, 2)
    weight_quant(wgt_c, wgc_d, 8, 4096, 0)
    weight_quant(wut_c, wuc_d, 8, 4096, 1)
    weight_quant(wdt_c, wdc_d, 32, 1024, 2)

    xt = sb_tiles(nc, es, "xt", 2, [128, 1024])
    sxg = sb_tiles(nc, es, "sxg", 2, [128, 1])
    sxu = sb_tiles(nc, es, "sxu", 2, [128, 1])
    shd = sb_tiles(nc, es, "shd", 2, [128, 1])
    xqf = sb_tiles(nc, es, "xqf", 8, [128, 256], BF16)
    ht = sb_tiles(nc, es, "ht", 2, [128, 4096])
    hqf = sb_tiles(nc, es, "hqf", 32, [128, 256], BF16)
    wgb = sb_tiles(nc, es, "wgb", 2, [128, 4096], BF16)
    wub = sb_tiles(nc, es, "wub", 2, [128, 4096], BF16)
    wdb = sb_tiles(nc, es, "wdb", 4, [128, 1024], BF16)
    gev = sb_tiles(nc, es, "gev", 2, [128, 512])
    uev = sb_tiles(nc, es, "uev", 2, [128, 512])
    ln_sc = {"mean": sb("ln_mean", [128, 1]), "var": sb("ln_var", [128, 1]),
             "xc": sb("ln_xc", [128, 1024]), "sq": sb("ln_sq", [128, 1024]),
             "eps": sb("ln_eps", [128, 1])}
    P.memset("v", ln_sc["eps"], 1e-5)
    res = sb("res", [128, 1024])
    outt = sb("outt", [128, 1024])

    for tb in range(8):
        for i in range(2):
            ti = tb * 2 + i
            P.dma("y", xt[i][:], x_d[ti * 128:(ti + 1) * 128, :])
            P.red("v", red1[:], xt[i][:], ALU.max, absval=True)
            P.ts("v", red1[:], red1[:], 1e-5, 1.0 / 127.0, ALU.max, ALU.mult)
            P.ts("v", sxg[i][:], red1[:], swcol[:, 0:1], None, ALU.mult)
            P.ts("v", sxu[i][:], red1[:], swcol[:, 1:2], None, ALU.mult)
            P.op("v", lambda eng: eng.reciprocal(acc1[:], red1[:]),
                 reads=["red1"], writes=["acc1"])
            P.ts("v", codes[:], xt[i][:], acc1[:], None, ALU.mult)
            round_nearest(P, "v", codes[:], qsc)
            P.ts("v", codes[:], codes[:], 127.0, -128.0, ALU.min, ALU.max)
            for j in range(8):
                P.transpose(ps[6][:, 0:128], codes[:, j * 128:(j + 1) * 128], idf[:])
                P.cp("s", xqf[j][:, i * 128:(i + 1) * 128], ps[6][:, 0:128])
        for fb in range(8):
            P.dma("y", wgb[fb % 2][:].rearrange("p (c f) -> p c f", c=8),
                  wgc_d[:, :, fb * 512:(fb + 1) * 512].rearrange("c p f -> p c f"))
            P.dma("g", wub[fb % 2][:].rearrange("p (c f) -> p c f", c=8),
                  wuc_d[:, :, fb * 512:(fb + 1) * 512].rearrange("c p f -> p c f"))
            for i in range(2):
                pg = ps[i % 2]
                pu = ps[2 + i % 2]
                for dj in range(8):
                    P.mm(pg[:], xqf[dj][:, i * 128:(i + 1) * 128],
                         wgb[fb % 2][:, dj * 512:(dj + 1) * 512],
                         start=(dj == 0), stop=(dj == 7))
                for dj in range(8):
                    P.mm(pu[:], xqf[dj][:, i * 128:(i + 1) * 128],
                         wub[fb % 2][:, dj * 512:(dj + 1) * 512],
                         start=(dj == 0), stop=(dj == 7))
                g_, u_ = gev[i % 2], uev[i % 2]
                P.act(g_[:], pg[:], AF.Silu, scale=sxg[i][:], e="s")
                P.act(u_[:], pu[:], AF.Identity, scale=sxu[i][:], e="s")
                P.tt("v", ht[i][:, fb * 512:(fb + 1) * 512], g_[:], u_[:], ALU.mult)
        for i in range(2):
            P.red("v", red1[:], ht[i][:], ALU.max, absval=True)
            P.ts("v", red1[:], red1[:], 1e-5, 1.0 / 127.0, ALU.max, ALU.mult)
            P.ts("v", shd[i][:], red1[:], swcol[:, 2:3], None, ALU.mult)
            P.op("v", lambda eng: eng.reciprocal(acc1[:], red1[:]),
                 reads=["red1"], writes=["acc1"])
            for sl in range(4):
                c = codes[:, 0:1024]
                P.ts("v", c, ht[i][:, sl * 1024:(sl + 1) * 1024], acc1[:], None, ALU.mult)
                round_nearest(P, "v", c, qsc)
                P.ts("v", c, c, 127.0, -128.0, ALU.min, ALU.max)
                for j in range(8):
                    fi = sl * 8 + j
                    P.transpose(ps[6][:, 0:128], c[:, j * 128:(j + 1) * 128], idf[:])
                    P.cp("s", hqf[fi][:, i * 128:(i + 1) * 128], ps[6][:, 0:128])
        for f in range(4):
            P.dma("y", wdb[f][:], wdc_d[f])
        for f in range(32):
            buf = wdb[f % 4]
            for i in range(2):
                for ob in range(2):
                    P.mm(ps[i * 2 + ob][:], hqf[f][:, i * 128:(i + 1) * 128],
                         buf[:, ob * 512:(ob + 1) * 512],
                         start=(f == 0), stop=(f == 31))
            if f + 4 < 32:
                P.dma("y", wdb[(f + 4) % 4][:], wdc_d[f + 4])
        for i in range(2):
            for ob in range(2):
                P.act(res[:, ob * 512:(ob + 1) * 512], ps[i * 2 + ob][:], AF.Identity,
                      scale=shd[i][:], e="s")
            P.tt("v", res[:], res[:], xt[i][:], ALU.add)
            emit_layernorm_tile(P, res[:], outt[:], D, ln_sc)
            P.dma("g", xo_d[(tb * 2 + i) * 128:(tb * 2 + i + 1) * 128, :], outt[:])


def _run_l0f(x_shards, w_gate, w_up, w_down):
    wgt = np.ascontiguousarray(w_gate.T)
    wut = np.ascontiguousarray(w_up.T)
    wdt = np.ascontiguousarray(w_down.T)
    in_maps = [{"x": x_shards[c], "wgt": wgt, "wut": wut, "wdt": wdt,
                "identf": _IDENT32} for c in range(NCORES)]
    return _launch(build_l0f, in_maps, ["xo"])



def build_l2f_m(nc, P, es):
    x_d = nc.dram_tensor("x", [TOK, D], F32, kind="ExternalInput").ap()
    idf_d = nc.dram_tensor("identf", [128, 128], F32, kind="ExternalInput").ap()
    mx_d = nc.dram_tensor("mx", [1, 1], F32, kind="ExternalOutput").ap()
    sb = lambda n, shp, dt=F32: es.enter_context(nc.sbuf_tensor(n, shp, dt))
    xt = sb_tiles(nc, es, "xt", 2, [128, 1024])
    acc = sb("acc", [128, 1])
    red = sb("red", [128, 1])
    idf = sb("idf", [128, 128])
    row = sb("row", [1, 128])
    m1 = sb("m1", [1, 1])
    pst = es.enter_context(nc.psum_tensor("pst", [128, 128], F32))
    P.dma("y", idf[:], idf_d)
    P.memset("v", acc[:], 0.0)
    for i in range(NT):
        P.dma("y", xt[i % 2][:], x_d[i * 128:(i + 1) * 128, :])
        P.red("v", red[:], xt[i % 2][:], ALU.max, absval=True)
        P.tt("v", acc[:], acc[:], red[:], ALU.max)
    P.transpose(pst[0:1, 0:128], acc[:], idf[:])
    P.cp("s", row[:], pst[0:1, 0:128], rk=["pst"], wk=["row"])
    P.red("v", m1[:], row[:], ALU.max)
    P.dma("g", mx_d, m1[:])


def _scalar_bcast_col(P, ps, onerow, src11, dst_col):
    P.mm(ps[:, 0:1], onerow[:], src11, True, True)
    P.cp("s", dst_col, ps[:, 0:1])


def build_l2f_a(nc, P, es):
    x_d = nc.dram_tensor("x", [TOK, D], F32, kind="ExternalInput").ap()
    mx_d = nc.dram_tensor("mxall", [1, NCORES], F32, kind="ExternalInput").ap()
    clipx_d = nc.dram_tensor("clipx", [1, 1], F32, kind="ExternalInput").ap()
    wgn_d = nc.dram_tensor("wgn", [2816, D], F32, kind="ExternalInput").ap()
    wun_d = nc.dram_tensor("wun", [2816, D], F32, kind="ExternalInput").ap()
    idf_d = nc.dram_tensor("identf", [128, 128], F32, kind="ExternalInput").ap()
    h_d = nc.dram_tensor("h", [TOK, 2816], F32, kind="ExternalOutput").ap()
    mh_d = nc.dram_tensor("mh", [1, 1], F32, kind="ExternalOutput").ap()

    sb = lambda n, shp, dt=F32: es.enter_context(nc.sbuf_tensor(n, shp, dt))
    idf = sb("idf", [128, 128])
    onerow = sb("onerow", [1, 128])
    qsc = {"i32": es.enter_context(nc.sbuf_tensor("q_i32", [128, 1024], I32)),
           "t": sb("q_t", [128, 1024]), "a": sb("q_a", [128, 1024]), "b": sb("q_b", [128, 1024])}
    stage = sb("stage", [128, 1024])
    codes = sb("codes", [128, 1024])
    swc = sb("swc", [128, 1])
    s11 = sb("s11", [1, 8])
    invx_c = sb("invx_c", [128, 1])
    wgqT = sb_tiles(nc, es, "wgqT", 8, [128, 2816], BF16)
    wuqT = sb_tiles(nc, es, "wuqT", 8, [128, 2816], BF16)
    gsrow = sb("gsrow", [1, 2816])
    usrow = sb("usrow", [1, 2816])
    gsb = sb("gsb", [128, 2816])
    usb = sb("usb", [128, 2816])
    xt = sb_tiles(nc, es, "xt", 2, [128, 1024])
    xqf = sb_tiles(nc, es, "xqf", 8, [128, 256], BF16)
    gev = sb_tiles(nc, es, "gev", 2, [128, 512])
    uev = sb_tiles(nc, es, "uev", 2, [128, 512])
    hrow = sb_tiles(nc, es, "hrow", 2, [128, 2816])
    macc = sb("macc", [128, 1])
    red = sb("red", [128, 1])
    mrow = sb("mrow", [1, 128])
    m1 = sb("m1", [1, 1])
    ps = [es.enter_context(nc.psum_tensor(f"ps{i}", [128, 512], F32)) for i in range(8)]

    P.dma("y", idf[:], idf_d)
    P.memset("v", onerow[:], 1.0)
    P.memset("v", macc[:], 0.0)
    P.dma("y", stage[0:1, 0:NCORES], mx_d)
    P.red("v", s11[0:1, 0:1], stage[0:1, 0:NCORES], ALU.max, rk=["stage"], wk=["s11"])
    P.dma("y", stage[0:1, 8:9], clipx_d)
    P.act(s11[0:1, 1:2], stage[0:1, 8:9], AF.Sigmoid, rk=["stage"], wk=["s11"])
    P.tt("v", s11[0:1, 2:3], s11[0:1, 0:1], s11[0:1, 1:2], ALU.mult)
    P.ts("v", s11[0:1, 2:3], s11[0:1, 2:3], 1.0 / 127.0, None, ALU.mult)
    P.op("v", lambda eng: eng.reciprocal(s11[0:1, 3:4], s11[0:1, 2:3]),
         reads=["s11"], writes=["s11"])
    _scalar_bcast_col(P, ps[0], onerow, s11[0:1, 3:4], invx_c[:])

    def wquant_T(src_d, dstT, srow):
        for c in range(22):
            P.dma("y", stage[:], src_d[c * 128:(c + 1) * 128, :])
            P.red("v", red[:], stage[:], ALU.max, absval=True)
            P.ts("v", swc[:], red[:], 1e-30, 1.0 / 127.0, ALU.max, ALU.mult)
            P.op("v", lambda eng: eng.reciprocal(red[:], swc[:]), reads=["swc"], writes=["red"])
            P.ts("v", codes[:], stage[:], red[:], None, ALU.mult)
            round_nearest(P, "v", codes[:], qsc)
            P.ts("v", codes[:], codes[:], 127.0, -128.0, ALU.min, ALU.max)
            for j in range(8):
                P.transpose(ps[1][:, 0:128], codes[:, j * 128:(j + 1) * 128], idf[:])
                P.cp("s", dstT[j][:, c * 128:(c + 1) * 128], ps[1][:, 0:128])
            P.transpose(ps[2][0:1, 0:128], swc[:], idf[:])
            P.cp("s", srow[0:1, c * 128:(c + 1) * 128], ps[2][0:1, 0:128],
                 rk=["ps2"], wk=[_apkey(srow.ap())])

    wquant_T(wgn_d, wgqT, gsrow)
    wquant_T(wun_d, wuqT, usrow)
    for srow, dstb in ((gsrow, gsb), (usrow, usb)):
        P.ts("v", srow[:], srow[:], s11[0:1, 2:3], None, ALU.mult)
        for fb in range(6):
            w = min(512, 2816 - fb * 512)
            P.mm(ps[3][:, 0:w], onerow[:], srow[0:1, fb * 512:fb * 512 + w], True, True)
            P.cp("s", dstb[:, fb * 512:fb * 512 + w], ps[3][:, 0:w],
                 rk=["ps3"], wk=[_apkey(dstb.ap())])

    for tb in range(8):
        for i in range(2):
            ti = tb * 2 + i
            P.dma("y", xt[i][:], x_d[ti * 128:(ti + 1) * 128, :])
            P.ts("v", codes[:], xt[i][:], invx_c[:], None, ALU.mult)
            round_nearest(P, "v", codes[:], qsc)
            P.ts("v", codes[:], codes[:], 127.0, -128.0, ALU.min, ALU.max)
            for j in range(8):
                P.transpose(ps[1][:, 0:128], codes[:, j * 128:(j + 1) * 128], idf[:])
                P.cp("s", xqf[j][:, i * 128:(i + 1) * 128], ps[1][:, 0:128])
        for i in range(2):
            hr = hrow[i]
            for fb in range(6):
                w = min(512, 2816 - fb * 512)
                pg = ps[4 + (fb % 2)]
                pu = ps[6 + (fb % 2)]
                for dj in range(8):
                    P.mm(pg[:, 0:w], xqf[dj][:, i * 128:(i + 1) * 128],
                         wgqT[dj][:, fb * 512:fb * 512 + w], start=(dj == 0), stop=(dj == 7))
                for dj in range(8):
                    P.mm(pu[:, 0:w], xqf[dj][:, i * 128:(i + 1) * 128],
                         wuqT[dj][:, fb * 512:fb * 512 + w], start=(dj == 0), stop=(dj == 7))
                g_, u_ = gev[fb % 2], uev[fb % 2]
                P.tt("v", g_[:, 0:w], pg[:, 0:w], gsb[:, fb * 512:fb * 512 + w], ALU.mult)
                P.act(g_[:, 0:w], g_[:, 0:w], AF.Silu, e="s")
                P.tt("v", u_[:, 0:w], pu[:, 0:w], usb[:, fb * 512:fb * 512 + w], ALU.mult)
                P.tt("v", hr[:, fb * 512:fb * 512 + w], g_[:, 0:w], u_[:, 0:w], ALU.mult)
            ti = tb * 2 + i
            P.dma("g", h_d[ti * 128:(ti + 1) * 128, :], hr[:])
            P.red("v", red[:], hr[:], ALU.max, absval=True)
            P.tt("v", macc[:], macc[:], red[:], ALU.max)
    P.transpose(ps[0][0:1, 0:128], macc[:], idf[:])
    P.cp("s", mrow[:], ps[0][0:1, 0:128], rk=["ps0"], wk=["mrow"])
    P.red("v", m1[:], mrow[:], ALU.max)
    P.dma("g", mh_d, m1[:])


def build_l2f_b(nc, P, es):
    x_d = nc.dram_tensor("x", [TOK, D], F32, kind="ExternalInput").ap()
    h_d = nc.dram_tensor("h", [TOK, 2816], F32, kind="ExternalInput").ap()
    mh_d = nc.dram_tensor("mhall", [1, NCORES], F32, kind="ExternalInput").ap()
    clipd_d = nc.dram_tensor("clipd", [1, 1], F32, kind="ExternalInput").ap()
    wdn_d = nc.dram_tensor("wdn", [D, 2816], F32, kind="ExternalInput").ap()
    idf_d = nc.dram_tensor("identf", [128, 128], F32, kind="ExternalInput").ap()
    xo_d = nc.dram_tensor("xo", [TOK, D], F32, kind="ExternalOutput").ap()

    sb = lambda n, shp, dt=F32: es.enter_context(nc.sbuf_tensor(n, shp, dt))
    idf = sb("idf", [128, 128])
    onerow = sb("onerow", [1, 128])
    qsc = {"i32": es.enter_context(nc.sbuf_tensor("q_i32", [128, 1024], I32)),
           "t": sb("q_t", [128, 1024]), "a": sb("q_a", [128, 1024]), "b": sb("q_b", [128, 1024])}
    stage = sb("stage", [128, 2816])
    codes = sb("codes", [128, 2816])
    swc = sb("swc", [128, 1])
    s11 = sb("s11", [1, 8])
    invh_c = sb("invh_c", [128, 1])
    wdqT = sb_tiles(nc, es, "wdqT", 22, [128, 1024], BF16)
    dsrow = sb("dsrow", [1, 1024])
    dsb = sb("dsb", [128, 1024])
    xt = sb("xt", [128, 1024])
    ht = sb("ht", [128, 2816])
    hqf = sb("hqf", [128, 2816], BF16)
    red = sb("red", [128, 1])
    res = sb("res", [128, 1024])
    outt = sb("outt", [128, 1024])
    ln_sc = {"mean": sb("ln_mean", [128, 1]), "var": sb("ln_var", [128, 1]),
             "xc": sb("ln_xc", [128, 1024]), "sq": sb("ln_sq", [128, 1024]),
             "eps": sb("ln_eps", [128, 1])}
    P.memset("v", ln_sc["eps"], 1e-5)
    ps = [es.enter_context(nc.psum_tensor(f"ps{i}", [128, 512], F32)) for i in range(6)]

    P.dma("y", idf[:], idf_d)
    P.memset("v", onerow[:], 1.0)
    P.dma("y", stage[0:1, 0:NCORES], mh_d)
    P.red("v", s11[0:1, 0:1], stage[0:1, 0:NCORES], ALU.max, rk=["stage"], wk=["s11"])
    P.dma("y", stage[0:1, 8:9], clipd_d)
    P.act(s11[0:1, 1:2], stage[0:1, 8:9], AF.Sigmoid, rk=["stage"], wk=["s11"])
    P.tt("v", s11[0:1, 2:3], s11[0:1, 0:1], s11[0:1, 1:2], ALU.mult)
    P.ts("v", s11[0:1, 2:3], s11[0:1, 2:3], 1.0 / 127.0, None, ALU.mult)
    P.op("v", lambda eng: eng.reciprocal(s11[0:1, 3:4], s11[0:1, 2:3]),
         reads=["s11"], writes=["s11"])
    _scalar_bcast_col(P, ps[0], onerow, s11[0:1, 3:4], invh_c[:])

    for c in range(8):
        P.dma("y", stage[:], wdn_d[c * 128:(c + 1) * 128, :])
        P.red("v", red[:], stage[:], ALU.max, absval=True)
        P.ts("v", swc[:], red[:], 1e-30, 1.0 / 127.0, ALU.max, ALU.mult)
        P.op("v", lambda eng: eng.reciprocal(red[:], swc[:]), reads=["swc"], writes=["red"])
        for sl in range(3):
            w = min(1024, 2816 - sl * 1024)
            P.ts("v", codes[:, sl * 1024:sl * 1024 + w], stage[:, sl * 1024:sl * 1024 + w],
                 red[:], None, ALU.mult)
            round_nearest(P, "v", codes[:, sl * 1024:sl * 1024 + w],
                          {k: _asap(v)[:, 0:w] for k, v in qsc.items()})
            P.ts("v", codes[:, sl * 1024:sl * 1024 + w], codes[:, sl * 1024:sl * 1024 + w],
                 127.0, -128.0, ALU.min, ALU.max)
        for j in range(22):
            P.transpose(ps[1][:, 0:128], codes[:, j * 128:(j + 1) * 128], idf[:])
            P.cp("s", wdqT[j][:, c * 128:(c + 1) * 128], ps[1][:, 0:128])
        P.transpose(ps[2][0:1, 0:128], swc[:], idf[:])
        P.cp("s", dsrow[0:1, c * 128:(c + 1) * 128], ps[2][0:1, 0:128],
             rk=["ps2"], wk=["dsrow"])
    P.ts("v", dsrow[:], dsrow[:], s11[0:1, 2:3], None, ALU.mult)
    for ob in range(2):
        P.mm(ps[3][:], onerow[:], dsrow[0:1, ob * 512:(ob + 1) * 512], True, True)
        P.cp("s", dsb[:, ob * 512:(ob + 1) * 512], ps[3][:], rk=["ps3"], wk=["dsb"])

    for ti in range(NT):
        P.dma("y", ht[:], h_d[ti * 128:(ti + 1) * 128, :])
        for sl in range(3):
            w = min(1024, 2816 - sl * 1024)
            P.ts("v", codes[:, sl * 1024:sl * 1024 + w], ht[:, sl * 1024:sl * 1024 + w],
                 invh_c[:], None, ALU.mult)
            round_nearest(P, "v", codes[:, sl * 1024:sl * 1024 + w],
                          {k: _asap(v)[:, 0:w] for k, v in qsc.items()})
            P.ts("v", codes[:, sl * 1024:sl * 1024 + w], codes[:, sl * 1024:sl * 1024 + w],
                 127.0, -128.0, ALU.min, ALU.max)
        for c in range(22):
            P.transpose(ps[1][:, 0:128], codes[:, c * 128:(c + 1) * 128], idf[:])
            P.cp("s", hqf[:, c * 128:(c + 1) * 128], ps[1][:, 0:128])
        P.dma("y", xt[:], x_d[ti * 128:(ti + 1) * 128, :])
        for ob in range(2):
            pd = ps[4 + ob]
            for c in range(22):
                P.mm(pd[:], hqf[:, c * 128:(c + 1) * 128],
                     wdqT[c][:, ob * 512:(ob + 1) * 512], start=(c == 0), stop=(c == 21))
            P.tt("v", res[:, ob * 512:(ob + 1) * 512], pd[:], dsb[:, ob * 512:(ob + 1) * 512],
                 ALU.mult)
        P.tt("v", res[:], res[:], xt[:], ALU.add)
        emit_layernorm_tile(P, res[:], outt[:], D, ln_sc)
        P.dma("g", xo_d[ti * 128:(ti + 1) * 128, :], outt[:])


def _run_l2f(x_shards, clip_x, clip_down, w_gate, w_up, w_down):
    r1 = _launch(build_l2f_m, [{"x": x_shards[c], "identf": _IDENT32} for c in range(NCORES)], ["mx"])
    mxall = np.concatenate([r["mx"] for r in r1], 1)
    cx = np.ascontiguousarray(clip_x.reshape(1, 1))
    cd = np.ascontiguousarray(clip_down.reshape(1, 1))
    in2 = [{"x": x_shards[c], "mxall": mxall, "clipx": cx, "wgn": w_gate,
            "wun": w_up, "identf": _IDENT32} for c in range(NCORES)]
    r2 = _launch(build_l2f_a, in2, ["h", "mh"])
    mhall = np.concatenate([r["mh"] for r in r2], 1)
    in3 = [{"x": x_shards[c], "h": r2[c]["h"], "mhall": mhall, "clipd": cd,
            "wdn": w_down, "identf": _IDENT32} for c in range(NCORES)]
    r3 = _launch(build_l2f_b, in3, ["xo"])
    return r3



def emit_bias_hilo(P, sb, name, src_row_d, width):
    nc = P.nc
    es = P.es
    if not hasattr(P, "_bh_scratch") or P._bh_w < width:
        P._bh_scratch = (es.enter_context(nc.sbuf_tensor(f"bh32_{name}", [1, width], F32)),
                         es.enter_context(nc.sbuf_tensor(f"bhh32_{name}", [1, width], F32)))
        P._bh_w = width
    b32, h32 = P._bh_scratch
    hi = es.enter_context(nc.sbuf_tensor(name + "_hi", [1, width], BF16))
    lo = es.enter_context(nc.sbuf_tensor(name + "_lo", [1, width], BF16))
    P.dma("y", b32[0:1, 0:width], src_row_d)
    P.cp("v", hi[:], b32[0:1, 0:width])
    P.cp("v", h32[0:1, 0:width], hi[:])
    P.tt("v", h32[0:1, 0:width], b32[0:1, 0:width], h32[0:1, 0:width], ALU.subtract)
    P.cp("v", lo[:], h32[0:1, 0:width])
    return hi, lo


def build_l1a(nc, P, es):
    x_d = nc.dram_tensor("x", [TOK, D], F32, kind="ExternalInput").ap()
    wqkvt_d = nc.dram_tensor("wqkvt", [D, 3072], F32, kind="ExternalInput").ap()
    wprojt_d = nc.dram_tensor("wprojt", [D, D], F32, kind="ExternalInput").ap()
    bproj_d = nc.dram_tensor("bprojr", [1, D], F32, kind="ExternalInput").ap()
    idf_d = nc.dram_tensor("identf", [128, 128], F32, kind="ExternalInput").ap()
    xo_d = nc.dram_tensor("xo", [TOK, D], F32, kind="ExternalOutput").ap()

    sb = lambda n, shp, dt=F32: es.enter_context(nc.sbuf_tensor(n, shp, dt))
    idf = sb("idf", [128, 128])
    onerow_b = sb("onerow_b", [1, 128], BF16)
    stage = sb("stage", [128, 3072])
    wqkv = sb_tiles(nc, es, "wqkv", 8, [128, 3072], BF16)
    wproj = sb_tiles(nc, es, "wproj", 8, [128, 1024], BF16)
    xt = sb("xt", [128, 1024])
    xf = sb("xf", [128, 1024], BF16)
    qkv = sb("qkv", [128, 3072])
    scr = sb("scr", [128, 1024])
    S = sb("S", [128, 256])
    m16 = sb("m16", [128, 16])
    l16 = sb("l16", [128, 16])
    attn = sb("attn", [128, 1024])
    attnf = sb("attnf", [128, 1024], BF16)
    res = sb("res", [128, 1024])
    outt = sb("outt", [128, 1024])
    ln_sc = {"mean": sb("ln_mean", [128, 1]), "var": sb("ln_var", [128, 1]),
             "xc": sb("ln_xc", [128, 1024]), "sq": sb("ln_sq", [128, 1024]),
             "eps": sb("ln_eps", [128, 1])}
    P.memset("v", ln_sc["eps"], 1e-5)
    ps = [es.enter_context(nc.psum_tensor(f"ps{i}", [128, 512], F32)) for i in range(6)]

    P.dma("y", idf[:], idf_d)
    P.memset("v", onerow_b[:], 1.0)
    bhi, blo = emit_bias_hilo(P, sb, "bproj", bproj_d, D)
    for j in range(8):
        P.dma("y", stage[:], wqkvt_d[j * 128:(j + 1) * 128, :])
        P.cp("v", wqkv[j][:], stage[:])
        P.dma("y", stage[:, 0:1024], wprojt_d[j * 128:(j + 1) * 128, :])
        P.cp("v", wproj[j][:], stage[:, 0:1024])

    qAP = lambda h: bass.AP(tensor=qkv.ap().tensor, offset=h * 192,
                            ap=[[qkv.ap().ap[0][0], 128], [0, 16], [1, 64]])
    kAP = bass.AP(tensor=qkv.ap().tensor, offset=64,
                  ap=[[qkv.ap().ap[0][0], 128], [192, 16], [1, 64]])
    pAP = lambda h: bass.AP(tensor=S.ap().tensor, offset=h * 16,
                            ap=[[S.ap().ap[0][0], 128], [0, 64], [1, 16]])
    vAP = bass.AP(tensor=qkv.ap().tensor, offset=128,
                  ap=[[qkv.ap().ap[0][0], 128], [1, 64], [192, 16]])
    mB = bass.AP(tensor=m16.ap().tensor, offset=0,
                 ap=[[m16.ap().ap[0][0], 128], [1, 16], [0, 16]])
    lB = bass.AP(tensor=l16.ap().tensor, offset=0,
                 ap=[[l16.ap().ap[0][0], 128], [1, 16], [0, 64]])
    S3 = S.ap().rearrange("p (h g) -> p h g", h=16)
    scr3d = scr.ap().rearrange("p (g d) -> p g d", g=16)
    scr3c = scr.ap().rearrange("p (d g) -> p d g", d=64)

    for ti in range(NT):
        P.dma("y", xt[:], x_d[ti * 128:(ti + 1) * 128, :])
        for j in range(8):
            P.transpose(ps[5][:, 0:128], xt[:, j * 128:(j + 1) * 128], idf[:],
                        rk=["xt", "idf"], wk=["ps5"])
            P.cp("s", attnf[:, j * 128:(j + 1) * 128], ps[5][:, 0:128],
                 rk=["ps5"], wk=["attnf"])
        for fb in range(6):
            pq = ps[fb % 2]
            for dj in range(8):
                P.mm(pq[:], attnf[:, dj * 128:(dj + 1) * 128],
                     wqkv[dj][:, fb * 512:(fb + 1) * 512], start=(dj == 0), stop=(dj == 7),
                     rk=["attnf", _apkey(wqkv[dj].ap())], wk=[f"psq{fb % 2}"])
            P.cp("s", qkv[:, fb * 512:(fb + 1) * 512], pq[:],
                 rk=[f"psq{fb % 2}"], wk=["qkv"])
        for h in range(16):
            P.tt("g", scr3d, qAP(h), kAP, ALU.mult, rk=["qkv"], wk=["scr"])
            P.red("v", S[:, h * 16:(h + 1) * 16], scr3d, ALU.add, rk=["scr"], wk=["S"])
        P.red("v", m16[:], S3, ALU.max, rk=["S"], wk=["m16"])
        P.tt("v", S3, S3, mB, ALU.subtract, rk=["S", "m16"], wk=["S"])
        P.act(S[:], S[:], AF.Exp, scale=0.125, e="s")
        P.red("v", l16[:], S3, ALU.add, rk=["S"], wk=["l16"])
        P.op("v", lambda eng: eng.reciprocal(l16[:], l16[:]), reads=["l16"], writes=["l16"])
        for h in range(16):
            P.tt("g", scr3c, pAP(h), vAP, ALU.mult, rk=["S", "qkv"], wk=["scr"])
            P.red("v", attn[:, h * 64:(h + 1) * 64], scr3c, ALU.add, rk=["scr"], wk=["attn"])
        attn3 = attn.ap().rearrange("p (h d) -> p h d", h=16)
        P.tt("v", attn3, attn3, lB, ALU.mult, rk=["attn", "l16"], wk=["attn"])
        for j in range(8):
            P.transpose(ps[5][:, 0:128], attn[:, j * 128:(j + 1) * 128], idf[:],
                        rk=["attn", "idf"], wk=["ps5"])
            P.cp("s", xf[:, j * 128:(j + 1) * 128], ps[5][:, 0:128],
                 rk=["ps5"], wk=["xf"])
        for ob in range(2):
            po = ps[2 + ob]
            kk = f"pso{ob}"
            for j in range(8):
                P.mm(po[:], xf[:, j * 128:(j + 1) * 128],
                     wproj[j][:, ob * 512:(ob + 1) * 512], start=(j == 0), stop=False,
                     rk=["xf", _apkey(wproj[j].ap())], wk=[kk])
            P.mm(po[:], onerow_b[:], bhi[0:1, ob * 512:(ob + 1) * 512],
                 start=False, stop=False, rk=["onerow_b", _apkey(bhi.ap())], wk=[kk])
            P.mm(po[:], onerow_b[:], blo[0:1, ob * 512:(ob + 1) * 512],
                 start=False, stop=True, rk=["onerow_b", _apkey(blo.ap())], wk=[kk])
            P.tt("v", res[:, ob * 512:(ob + 1) * 512], po[:], xt[:, ob * 512:(ob + 1) * 512],
                 ALU.add, rk=[kk, "xt"], wk=["res"])
        emit_layernorm_tile(P, res[:], outt[:], D, ln_sc)
        P.dma("g", xo_d[ti * 128:(ti + 1) * 128, :], outt[:])


def _run_l1a(x_shards, Wqkv, Wproj, bproj):
    wqkvt = np.ascontiguousarray(Wqkv.T)
    wprojt = np.ascontiguousarray(Wproj.T)
    bprojr = np.ascontiguousarray(bproj.reshape(1, D))
    in_maps = [{"x": x_shards[c], "wqkvt": wqkvt, "wprojt": wprojt,
                "bprojr": bprojr, "identf": _IDENT32} for c in range(NCORES)]
    return _launch(build_l1a, in_maps, ["xo"])



def build_l0a(nc, P, es):
    x_d = nc.dram_tensor("x", [TOK, D], F32, kind="ExternalInput").ap()
    xfm_d = nc.dram_tensor("xfm", [D, S], F32, kind="ExternalInput").ap()
    wqt_d = nc.dram_tensor("wqt", [D, D], F32, kind="ExternalInput").ap()
    wkt_d = nc.dram_tensor("wkt", [D, D], F32, kind="ExternalInput").ap()
    wvt_d = nc.dram_tensor("wvt", [D, D], F32, kind="ExternalInput").ap()
    wot_d = nc.dram_tensor("wot", [D, D], F32, kind="ExternalInput").ap()
    xo_d = nc.dram_tensor("xo", [TOK, D], F32, kind="ExternalOutput").ap()
    kfm_d = nc.dram_tensor("kfm", [D, S], BF16, kind="Internal").ap()
    vpd_d = nc.dram_tensor("vpd", [S, 16 * 65], BF16, kind="Internal").ap()

    sb = lambda n, shp, dt=F32: es.enter_context(nc.sbuf_tensor(n, shp, dt))
    stage = sb("stage", [128, 1024])
    wq = sb_tiles(nc, es, "wq", 8, [128, 1024], BF16)
    wk = sb_tiles(nc, es, "wk", 8, [128, 1024], BF16)
    wv = sb_tiles(nc, es, "wv", 8, [128, 1024], BF16)
    wo = sb_tiles(nc, es, "wo", 8, [128, 512], BF16)
    xfs = sb("xfs", [128, 512])
    xfb = sb_tiles(nc, es, "xfb", 8, [128, 512], BF16)
    kev = sb_tiles(nc, es, "kev", 2, [128, 512], BF16)
    vp = sb_tiles(nc, es, "vp", 2, [128, 1040], BF16)
    qfm = sb_tiles(nc, es, "qfm", 8, [128, TOK], BF16)
    kb = sb_tiles(nc, es, "kb", 4, [128, 128], BF16)
    vb = sb_tiles(nc, es, "vb", 4, [128, 130], BF16)
    pt = sb_tiles(nc, es, "pt", 2, [128, 512], BF16)
    afm = sb_tiles(nc, es, "afm", 8, [128, 512])
    afb = sb_tiles(nc, es, "afb", 8, [128, 512], BF16)
    l_sb = sb("l_sb", [1, 16 * 512])
    one64 = sb("one64", [1, 64])
    xt = sb("xt", [128, 1024])
    res = sb("res", [128, 1024])
    outt = sb("outt", [128, 1024])
    ln_sc = {"mean": sb("ln_mean", [128, 1]), "var": sb("ln_var", [128, 1]),
             "xc": sb("ln_xc", [128, 1024]), "sq": sb("ln_sq", [128, 1024]),
             "eps": sb("ln_eps", [128, 1])}
    P.memset("v", ln_sc["eps"], 1e-5)
    P.memset("v", one64[:], 1.0)
    ps = [es.enter_context(nc.psum_tensor(f"ps{i}", [128, 512], F32)) for i in range(8)]

    for j in range(8):
        P.dma("y", stage[:], wqt_d[j * 128:(j + 1) * 128, :])
        P.cp("v", wq[j][:], stage[:])
        P.dma("y", stage[:], wkt_d[j * 128:(j + 1) * 128, :])
        P.cp("v", wk[j][:], stage[:])
        P.dma("y", stage[:], wvt_d[j * 128:(j + 1) * 128, :])
        P.cp("v", wv[j][:], stage[:])

    idf_d = nc.dram_tensor("identf", [128, 128], F32, kind="ExternalInput").ap()
    idf = sb("idf", [128, 128])
    P.dma("y", idf[:], idf_d)
    for tb in range(16):
        for j in range(8):
            P.dma("y", xfs[:], xfm_d[j * 128:(j + 1) * 128, tb * 512:(tb + 1) * 512])
            P.cp("v", xfb[j][:], xfs[:])
        for o in range(8):
            pk = ps[o % 2]
            kk = f"psk{o % 2}"
            for dj in range(8):
                P.mm(pk[:], wk[dj][:, o * 128:(o + 1) * 128], xfb[dj][:],
                     start=(dj == 0), stop=(dj == 7),
                     rk=[_apkey(wk[dj].ap()), _apkey(xfb[dj].ap())], wk=[kk])
            P.cp("s", kev[o % 2][:], pk[:], rk=[kk], wk=[_apkey(kev[o % 2].ap())])
            P.dma("g", kfm_d[o * 128:(o + 1) * 128, tb * 512:(tb + 1) * 512], kev[o % 2][:])
        for i in range(4):
            pv = ps[2 + i % 2]
            kk = f"psv{i % 2}"
            vpt = vp[i % 2]
            vap = vpt.ap()
            P.memset("v", bass.AP(tensor=vap.tensor, offset=64,
                                  ap=[[vap.ap[0][0], 128], [65, 16]]), 1.0,
                     wk=[_apkey(vap)])
            for ob in range(2):
                for dj in range(8):
                    P.mm(pv[:], xfb[dj][:, i * 128:(i + 1) * 128],
                         wv[dj][:, ob * 512:(ob + 1) * 512],
                         start=(dj == 0), stop=(dj == 7),
                         rk=[_apkey(xfb[dj].ap()), _apkey(wv[dj].ap())], wk=[kk])
                dst = bass.AP(tensor=vap.tensor, offset=ob * 8 * 65,
                              ap=[[vap.ap[0][0], 128], [65, 8], [1, 64]])
                P.cp("s", dst, pv[:].rearrange("p (h d) -> p h d", h=8),
                     rk=[kk], wk=[_apkey(vap)])
            P.dma("g", vpd_d[(tb * 4 + i) * 128:(tb * 4 + i + 1) * 128, :], vpt[:])
    xqf = sb_tiles(nc, es, "xqf", 8, [128, 128], BF16)
    for ti in range(NT):
        P.dma("y", xt[:], x_d[ti * 128:(ti + 1) * 128, :])
        for j in range(8):
            P.transpose(ps[6][:, 0:128], xt[:, j * 128:(j + 1) * 128], idf[:],
                        rk=["xt", "idf"], wk=["ps6"])
            P.cp("s", xqf[j][:], ps[6][:, 0:128], rk=["ps6"], wk=[_apkey(xqf[j].ap())])
        for o in range(8):
            pq = ps[4 + o % 2]
            kk = f"psq{o % 2}"
            for dj in range(8):
                P.mm(pq[:, 0:128], wq[dj][:, o * 128:(o + 1) * 128], xqf[dj][:],
                     start=(dj == 0), stop=(dj == 7),
                     rk=[_apkey(wq[dj].ap()), _apkey(xqf[dj].ap())], wk=[kk])
            P.cp("s", qfm[o][:, ti * 128:(ti + 1) * 128], pq[:, 0:128],
                 rk=[kk], wk=[_apkey(qfm[o].ap())])

    for j in range(8):
        P.dma("y", stage[:], wot_d[j * 128:(j + 1) * 128, :])
        P.cp("v", wo[j][:], stage[:, 0:512])
    wo2 = sb_tiles(nc, es, "wo2", 8, [128, 512], BF16)
    for j in range(8):
        P.dma("y", stage[:], wot_d[j * 128:(j + 1) * 128, :])
        P.cp("v", wo2[j][:], stage[:, 512:1024])

    NK = S // 128
    for qb in range(4):
        for p in range(8):
            ksc = [f"pssc{j}" for j in range(2)]
            kav = [f"psav{j}" for j in range(2)]
            for kt in range(NK):
                if kt == 0:
                    for pre in range(min(4, NK)):
                        P.dma("y", kb[pre][:], kfm_d[p * 128:(p + 1) * 128, pre * 128:(pre + 1) * 128],
                              wk=[("kb", pre % 4)])
                        P.dma("g", vb[pre][:], vpd_d[pre * 128:(pre + 1) * 128, p * 130:(p + 1) * 130],
                              wk=[("vb", pre % 4)])
                for j in range(2):
                    psc = ps[j]
                    P.mm(psc[:], kb[kt % 4][0:64, :] if j == 0 else kb[kt % 4][64:128, :],
                         qfm[p][j * 64:(j + 1) * 64, qb * 512:(qb + 1) * 512],
                         start=True, stop=True,
                         rk=[("kb", kt % 4), _apkey(qfm[p].ap())], wk=[ksc[j]])
                    P.act(pt[j][:], psc[:], AF.Sigmoid, scale=0.125, e="s",
                          rk=[ksc[j]], wk=[_apkey(pt[j].ap())])
                    P.mm(ps[2 + j][0:65, :], vb[kt % 4][:, j * 65:(j + 1) * 65], pt[j][:],
                         start=(kt == 0), stop=(kt == NK - 1),
                         rk=[("vb", kt % 4), _apkey(pt[j].ap())], wk=[kav[j]])
                if kt + 4 < NK:
                    P.dma("y", kb[(kt + 4) % 4][:], kfm_d[p * 128:(p + 1) * 128, (kt + 4) * 128:(kt + 5) * 128],
                          wk=[("kb", (kt + 4) % 4)])
                    P.dma("g", vb[(kt + 4) % 4][:], vpd_d[(kt + 4) * 128:(kt + 5) * 128, p * 130:(p + 1) * 130],
                          wk=[("vb", (kt + 4) % 4)])
            for j in range(2):
                P.cp("s", afm[p][j * 64:(j + 1) * 64, :], ps[2 + j][0:64, :],
                     rk=[kav[j]], wk=[_apkey(afm[p].ap())])
                P.cp("s", l_sb[0:1, (2 * p + j) * 512:(2 * p + j + 1) * 512], ps[2 + j][64:65, :],
                     rk=[kav[j]], wk=["l_sb"])
        for p in range(8):
            for j in range(2):
                sl = l_sb[0:1, (2 * p + j) * 512:(2 * p + j + 1) * 512]
                P.ts("v", sl, sl, 1e-6, None, ALU.add, rk=["l_sb"], wk=["l_sb"])
                P.op("v", lambda eng, a=sl: eng.reciprocal(a, a), reads=["l_sb"], writes=["l_sb"])
                P.mm(ps[4][j * 64:(j + 1) * 64, :], one64[:],
                     l_sb[0:1, (2 * p + j) * 512:(2 * p + j + 1) * 512],
                     start=True, stop=True, rk=["l_sb", "one64"], wk=["pslb"])
            P.tt("v", afb[p][:], afm[p][:], ps[4][:], ALU.mult,
                 rk=[_apkey(afm[p].ap()), "pslb"], wk=[_apkey(afb[p].ap())])
        for i in range(4):
            ti = qb * 4 + i
            P.dma("y", xt[:], x_d[ti * 128:(ti + 1) * 128, :])
            for ob in range(2):
                po = ps[6 + ob]
                kk = f"pso{ob}"
                wob = wo if ob == 0 else wo2
                for j in range(8):
                    P.mm(po[:], afb[j][:, i * 128:(i + 1) * 128], wob[j][:],
                         start=(j == 0), stop=(j == 7),
                         rk=[_apkey(afb[j].ap()), _apkey(wob[j].ap())], wk=[kk])
                P.tt("v", res[:, ob * 512:(ob + 1) * 512], po[:], xt[:, ob * 512:(ob + 1) * 512],
                     ALU.add, rk=[kk, "xt"], wk=["res"])
            emit_layernorm_tile(P, res[:], outt[:], D, ln_sc)
            P.dma("g", xo_d[ti * 128:(ti + 1) * 128, :], outt[:])


def _run_l0a(x_flat, Wq, Wk, Wv, Wo):
    xs = _shards_tm(x_flat)
    fms = _fm_batches(x_flat)
    wqt = np.ascontiguousarray(Wq.T)
    wkt = np.ascontiguousarray(Wk.T)
    wvt = np.ascontiguousarray(Wv.T)
    wot = np.ascontiguousarray(Wo.T)
    in_maps = [{"x": xs[c], "xfm": fms[c], "wqt": wqt, "wkt": wkt, "wvt": wvt,
                "wot": wot, "identf": _IDENT32} for c in range(NCORES)]
    return _launch(build_l0a, in_maps, ["xo"])



_L2_STRIDES = [1, 1, 2, 2, 2, 2, 4, 4, 4, 4, 8, 8, 8, 8, 16, 16]


def emit_row_bcast(P, ps, onerow_b, hi, lo, dst, width):
    for ob in range((width + 511) // 512):
        w = min(512, width - ob * 512)
        P.mm(_asap(ps)[:, 0:w], onerow_b[:], hi[0:1, ob * 512:ob * 512 + w],
             start=True, stop=False, rk=[_apkey(_asap(onerow_b)), _apkey(_asap(hi))],
             wk=[_apkey(_asap(ps))])
        P.mm(_asap(ps)[:, 0:w], onerow_b[:], lo[0:1, ob * 512:ob * 512 + w],
             start=False, stop=True, rk=[_apkey(_asap(onerow_b)), _apkey(_asap(lo))],
             wk=[_apkey(_asap(ps))])
        P.cp("s", _asap(dst)[:, ob * 512:ob * 512 + w], _asap(ps)[:, 0:w])


def build_l2a1(nc, P, es):
    x_d = nc.dram_tensor("x", [TOK, D], F32, kind="ExternalInput").ap()
    xfm_d = nc.dram_tensor("xfm", [D, S], F32, kind="ExternalInput").ap()
    wqkv_d = nc.dram_tensor("wqkv", [D, 3 * D], F32, kind="ExternalInput").ap()
    bqkvc_d = nc.dram_tensor("bqkvc", [128, 24], F32, kind="ExternalInput").ap()
    bqkvr_d = nc.dram_tensor("bqkvr", [1, 3 * D], F32, kind="ExternalInput").ap()
    wout_d = nc.dram_tensor("wout", [D, D], F32, kind="ExternalInput").ap()
    boutr_d = nc.dram_tensor("boutr", [1, D], F32, kind="ExternalInput").ap()
    gr_d = nc.dram_tensor("gammar", [1, D], F32, kind="ExternalInput").ap()
    br_d = nc.dram_tensor("betar", [1, D], F32, kind="ExternalInput").ap()
    idf_d = nc.dram_tensor("identf", [128, 128], F32, kind="ExternalInput").ap()
    h_d = nc.dram_tensor("h", [TOK, D], F32, kind="ExternalOutput").ap()
    kfm_s = {s: nc.dram_tensor(f"kfm{s}", [D, S // s], BF16, kind="Internal").ap()
             for s in (1, 2, 4, 8, 16)}
    vpd_d = nc.dram_tensor("vpd", [S, 16 * 65], BF16, kind="Internal").ap()

    sb = lambda n, shp, dt=F32: es.enter_context(nc.sbuf_tensor(n, shp, dt))
    stage = sb("stage", [128, 1024])
    wqkv = sb_tiles(nc, es, "wqkv", 8, [128, 3072], BF16)
    idf = sb("idf", [128, 128])
    onerow_b = sb("onerow_b", [1, 128], BF16)
    one64 = sb("one64", [1, 64])
    bqc = sb("bqc", [128, 24])
    xfs = sb("xfs", [128, 512])
    xfb = sb_tiles(nc, es, "xfb", 8, [128, 512], BF16)
    kev = sb_tiles(nc, es, "kev", 2, [128, 512])
    kevs = sb("kevs", [128, 512], BF16)
    vp = sb_tiles(nc, es, "vp", 2, [128, 1040], BF16)
    qfm = sb_tiles(nc, es, "qfm", 8, [128, TOK], BF16)
    xqf = sb_tiles(nc, es, "xqf", 8, [128, 128], BF16)
    kb = sb_tiles(nc, es, "kb", 4, [128, 128], BF16)
    vb = sb_tiles(nc, es, "vb", 4, [128, 130], BF16)
    pt = sb_tiles(nc, es, "pt", 2, [128, 512], BF16)
    afb = sb_tiles(nc, es, "afb", 8, [128, 512], BF16)
    l_sb = sb("l_sb", [1, 1024])
    wo = sb_tiles(nc, es, "wo", 8, [128, 1024], BF16)
    gb = sb("gb", [128, 1024])
    bb = sb("bb", [128, 1024])
    xt = sb("xt", [128, 1024])
    res = sb("res", [128, 1024])
    outt = sb("outt", [128, 1024])
    ln_sc = {"mean": sb("ln_mean", [128, 1]), "var": sb("ln_var", [128, 1]),
             "xc": sb("ln_xc", [128, 1024]), "sq": sb("ln_sq", [128, 1024]),
             "eps": sb("ln_eps", [128, 1])}
    c20 = sb("c20", [128, 1])
    P.memset("v", ln_sc["eps"], 1e-5)
    P.memset("v", onerow_b[:], 1.0)
    P.memset("v", one64[:], 1.0)
    P.memset("v", c20[:], -20.0)
    ps = [es.enter_context(nc.psum_tensor(f"ps{i}", [128, 512], F32)) for i in range(8)]

    P.dma("y", idf[:], idf_d)
    P.dma("y", bqc[:], bqkvc_d)
    bvhi, bvlo = emit_bias_hilo(P, sb, "bv", bqkvr_d[0:1, 2048:3072], D)
    bohi, bolo = emit_bias_hilo(P, sb, "bo", boutr_d, D)
    ghi, glo = emit_bias_hilo(P, sb, "ga", gr_d, D)
    bhi_, blo_ = emit_bias_hilo(P, sb, "be", br_d, D)
    emit_row_bcast(P, ps[7], onerow_b, ghi, glo, gb, D)
    emit_row_bcast(P, ps[7], onerow_b, bhi_, blo_, bb, D)
    for j in range(8):
        for sl in range(3):
            P.dma("y", stage[:], wqkv_d[j * 128:(j + 1) * 128, sl * 1024:(sl + 1) * 1024])
            P.cp("v", wqkv[j][:, sl * 1024:(sl + 1) * 1024], stage[:])

    for tb in range(16):
        for j in range(8):
            P.dma("y", xfs[:], xfm_d[j * 128:(j + 1) * 128, tb * 512:(tb + 1) * 512])
            P.cp("v", xfb[j][:], xfs[:])
        for o in range(8):
            pk = ps[o % 2]
            kk = f"psk{o % 2}"
            for dj in range(8):
                P.mm(pk[:], wqkv[dj][:, 1024 + o * 128:1024 + (o + 1) * 128], xfb[dj][:],
                     start=(dj == 0), stop=(dj == 7),
                     rk=[_apkey(wqkv[dj].ap()), _apkey(xfb[dj].ap())], wk=[kk])
            ke = kev[o % 2]
            P.act(ke[:], pk[:], AF.Identity, bias=bqc[:, 8 + o:9 + o], e="s",
                  rk=[kk, "bqc"], wk=[_apkey(ke.ap())])
            P.cp("v", kevs[:], ke[:], rk=[_apkey(ke.ap())], wk=["kevs"])
            P.dma("g", kfm_s[1][o * 128:(o + 1) * 128, tb * 512:(tb + 1) * 512], kevs[:])
            for s in (2, 4, 8, 16):
                st_ap = bass.AP(tensor=ke.ap().tensor, offset=0,
                                ap=[[ke.ap().ap[0][0], 128], [s, 512 // s]])
                P.cp("v", kevs[:, 0:512 // s], st_ap, rk=[_apkey(ke.ap())], wk=["kevs"])
                P.dma("g", kfm_s[s][o * 128:(o + 1) * 128, tb * (512 // s):(tb + 1) * (512 // s)],
                      kevs[:, 0:512 // s])
        for i in range(4):
            pv = ps[2 + i % 2]
            kk = f"psv{i % 2}"
            vpt = vp[i % 2]
            vap = vpt.ap()
            P.memset("v", bass.AP(tensor=vap.tensor, offset=64,
                                  ap=[[vap.ap[0][0], 128], [65, 16]]), 1.0,
                     wk=[_apkey(vap)])
            for ob in range(2):
                for dj in range(8):
                    P.mm(pv[:], xfb[dj][:, i * 128:(i + 1) * 128],
                         wqkv[dj][:, 2048 + ob * 512:2048 + (ob + 1) * 512],
                         start=(dj == 0), stop=False,
                         rk=[_apkey(xfb[dj].ap()), _apkey(wqkv[dj].ap())], wk=[kk])
                P.mm(pv[:], onerow_b[:], bvhi[0:1, ob * 512:(ob + 1) * 512],
                     start=False, stop=False, rk=["onerow_b", _apkey(bvhi.ap())], wk=[kk])
                P.mm(pv[:], onerow_b[:], bvlo[0:1, ob * 512:(ob + 1) * 512],
                     start=False, stop=True, rk=["onerow_b", _apkey(bvlo.ap())], wk=[kk])
                dst = bass.AP(tensor=vap.tensor, offset=ob * 8 * 65,
                              ap=[[vap.ap[0][0], 128], [65, 8], [1, 64]])
                P.cp("s", dst, pv[:].rearrange("p (h d) -> p h d", h=8),
                     rk=[kk], wk=[_apkey(vap)])
            P.dma("g", vpd_d[(tb * 4 + i) * 128:(tb * 4 + i + 1) * 128, :], vpt[:])
    for ti in range(NT):
        P.dma("y", xt[:], x_d[ti * 128:(ti + 1) * 128, :])
        for j in range(8):
            P.transpose(ps[6][:, 0:128], xt[:, j * 128:(j + 1) * 128], idf[:],
                        rk=["xt", "idf"], wk=["ps6"])
            P.cp("s", xqf[j][:], ps[6][:, 0:128], rk=["ps6"], wk=[_apkey(xqf[j].ap())])
        for o in range(8):
            pq = ps[4 + o % 2]
            kk = f"psq{o % 2}"
            for dj in range(8):
                P.mm(pq[:, 0:128], wqkv[dj][:, o * 128:(o + 1) * 128], xqf[dj][:],
                     start=(dj == 0), stop=(dj == 7),
                     rk=[_apkey(wqkv[dj].ap()), _apkey(xqf[dj].ap())], wk=[kk])
            P.act(stage[:, 0:128], pq[:, 0:128], AF.Identity, bias=bqc[:, o:o + 1], e="s",
                  rk=[kk, "bqc"], wk=["stage"])
            P.cp("v", qfm[o][:, ti * 128:(ti + 1) * 128], stage[:, 0:128],
                 rk=["stage"], wk=[_apkey(qfm[o].ap())])

    for qb in range(4):
        for p in range(8):
            sstr = _L2_STRIDES[2 * p]
            NK = S // sstr // 128
            kav = [f"psav{j}" for j in range(2)]
            for kt in range(NK):
                if kt == 0:
                    for pre in range(min(4, NK)):
                        P.dma("y", kb[pre][:], kfm_s[sstr][p * 128:(p + 1) * 128, pre * 128:(pre + 1) * 128],
                              wk=[("kb", pre)])
                        vsrc = bass.AP(tensor=vpd_d.tensor, offset=(pre * 128 * sstr) * 1040 + p * 130,
                                       ap=[[sstr * 1040, 128], [1, 130]])
                        P.dma("g", vb[pre][:], vsrc, wk=[("vb", pre)])
                for j in range(2):
                    psc = ps[j]
                    P.mm(psc[:], kb[kt % 4][j * 64:(j + 1) * 64, :],
                         qfm[p][j * 64:(j + 1) * 64, qb * 512:(qb + 1) * 512],
                         start=True, stop=True,
                         rk=[("kb", kt % 4), _apkey(qfm[p].ap())], wk=[f"pssc{j}"])
                    P.act(pt[j][:], psc[:], AF.Exp, scale=0.125, bias=c20[:], e="s",
                          rk=[f"pssc{j}"], wk=[_apkey(pt[j].ap())])
                    P.mm(ps[2 + j][0:65, :], vb[kt % 4][:, j * 65:(j + 1) * 65], pt[j][:],
                         start=(kt == 0), stop=(kt == NK - 1),
                         rk=[("vb", kt % 4), _apkey(pt[j].ap())], wk=[kav[j]])
                if kt + 4 < NK:
                    P.dma("y", kb[(kt + 4) % 4][:], kfm_s[sstr][p * 128:(p + 1) * 128, (kt + 4) * 128:(kt + 5) * 128],
                          wk=[("kb", (kt + 4) % 4)])
                    vsrc = bass.AP(tensor=vpd_d.tensor, offset=((kt + 4) * 128 * sstr) * 1040 + p * 130,
                                   ap=[[sstr * 1040, 128], [1, 130]])
                    P.dma("g", vb[(kt + 4) % 4][:], vsrc, wk=[("vb", (kt + 4) % 4)])
            for j in range(2):
                P.cp("s", afb[p][j * 64:(j + 1) * 64, :], ps[2 + j][0:64, :],
                     rk=[kav[j]], wk=[_apkey(afb[p].ap())])
                P.cp("s", l_sb[0:1, j * 512:(j + 1) * 512], ps[2 + j][64:65, :],
                     rk=[kav[j]], wk=["l_sb"])
            for j in range(2):
                sl = l_sb[0:1, j * 512:(j + 1) * 512]
                P.op("v", lambda eng, a=sl: eng.reciprocal(a, a), reads=["l_sb"], writes=["l_sb"])
                P.mm(ps[4][j * 64:(j + 1) * 64, :], one64[:], l_sb[0:1, j * 512:(j + 1) * 512],
                     start=True, stop=True, rk=["l_sb", "one64"])
            P.tt("v", afb[p][:], afb[p][:], ps[4][:], ALU.mult)
        if qb == 0:
            for j in range(8):
                P.dma("y", stage[:], wout_d[j * 128:(j + 1) * 128, :])
                P.cp("v", wo[j][:], stage[:], rk=["stage"], wk=[_apkey(wo[j].ap())])
        for i in range(4):
            ti = qb * 4 + i
            P.dma("y", xt[:], x_d[ti * 128:(ti + 1) * 128, :])
            for ob in range(2):
                po = ps[6 + ob]
                kk = f"pso{ob}"
                for j in range(8):
                    P.mm(po[:], afb[j][:, i * 128:(i + 1) * 128],
                         wo[j][:, ob * 512:(ob + 1) * 512],
                         start=(j == 0), stop=False,
                         rk=[_apkey(afb[j].ap()), _apkey(wo[j].ap())], wk=[kk])
                P.mm(po[:], onerow_b[:], bohi[0:1, ob * 512:(ob + 1) * 512],
                     start=False, stop=False, rk=["onerow_b", _apkey(bohi.ap())], wk=[kk])
                P.mm(po[:], onerow_b[:], bolo[0:1, ob * 512:(ob + 1) * 512],
                     start=False, stop=True, rk=["onerow_b", _apkey(bolo.ap())], wk=[kk])
                P.tt("v", res[:, ob * 512:(ob + 1) * 512], po[:], xt[:, ob * 512:(ob + 1) * 512],
                     ALU.add, rk=[kk, "xt"], wk=["res"])
            emit_layernorm_tile(P, res[:], outt[:], D, ln_sc)
            P.tt("v", outt[:], outt[:], gb[:], ALU.mult, rk=["outt", "gb"], wk=["outt"])
            P.tt("v", outt[:], outt[:], bb[:], ALU.add, rk=["outt", "bb"], wk=["outt"])
            P.dma("g", h_d[ti * 128:(ti + 1) * 128, :], outt[:])


def build_l2a2(nc, P, es):
    x_d = nc.dram_tensor("x", [TOK, D], F32, kind="ExternalInput").ap()
    h_d = nc.dram_tensor("h", [TOK, D], F32, kind="ExternalInput").ap()
    w1_d = nc.dram_tensor("w1", [D, 4096], F32, kind="ExternalInput").ap()
    b1c_d = nc.dram_tensor("b1c", [128, 32], F32, kind="ExternalInput").ap()
    w2_d = nc.dram_tensor("w2", [4096, D], F32, kind="ExternalInput").ap()
    b2r_d = nc.dram_tensor("b2r", [1, D], F32, kind="ExternalInput").ap()
    gr_d = nc.dram_tensor("gammar", [1, D], F32, kind="ExternalInput").ap()
    br_d = nc.dram_tensor("betar", [1, D], F32, kind="ExternalInput").ap()
    idf_d = nc.dram_tensor("identf", [128, 128], F32, kind="ExternalInput").ap()
    xo_d = nc.dram_tensor("xo", [TOK, D], F32, kind="ExternalOutput").ap()

    sb = lambda n, shp, dt=F32: es.enter_context(nc.sbuf_tensor(n, shp, dt))
    idf = sb("idf", [128, 128])
    onerow_b = sb("onerow_b", [1, 128], BF16)
    stage = sb("stage", [128, 4096])
    w1b = sb_tiles(nc, es, "w1b", 8, [128, 4096], BF16)
    w2b = sb_tiles(nc, es, "w2b", 4, [128, 1024], BF16)
    b1c = sb("b1c_s", [128, 32])
    gbx = sb("gbx", [128, 1024])
    bbx = sb("bbx", [128, 1024])
    ht = sb_tiles(nc, es, "ht", 4, [128, 1024])
    hfb = sb_tiles(nc, es, "hfb", 8, [128, 512], BF16)
    rfb = sb_tiles(nc, es, "rfb", 32, [128, 512], BF16)
    xt = sb("xt", [128, 1024])
    res = sb("res", [128, 1024])
    outt = sb("outt", [128, 1024])
    ln_sc = {"mean": sb("ln_mean", [128, 1]), "var": sb("ln_var", [128, 1]),
             "xc": sb("ln_xc", [128, 1024]), "sq": sb("ln_sq", [128, 1024]),
             "eps": sb("ln_eps", [128, 1])}
    P.memset("v", ln_sc["eps"], 1e-5)
    P.memset("v", onerow_b[:], 1.0)
    ps = [es.enter_context(nc.psum_tensor(f"ps{i}", [128, 512], F32)) for i in range(8)]

    P.dma("y", idf[:], idf_d)
    P.dma("y", b1c[:], b1c_d)
    b2hi, b2lo = emit_bias_hilo(P, sb, "b2", b2r_d, D)
    ghi, glo = emit_bias_hilo(P, sb, "ga", gr_d, D)
    bhi_, blo_ = emit_bias_hilo(P, sb, "be", br_d, D)
    emit_row_bcast(P, ps[7], onerow_b, ghi, glo, gbx, D)
    emit_row_bcast(P, ps[7], onerow_b, bhi_, blo_, bbx, D)
    for j in range(8):
        P.dma("y", stage[:], w1_d[j * 128:(j + 1) * 128, :])
        P.cp("v", w1b[j][:], stage[:])

    for tb in range(4):
        for i in range(4):
            ti = tb * 4 + i
            P.dma("y", ht[i][:], h_d[ti * 128:(ti + 1) * 128, :])
            for j in range(8):
                P.transpose(ps[6][:, 0:128], ht[i][:, j * 128:(j + 1) * 128], idf[:],
                            rk=[_apkey(ht[i].ap()), "idf"], wk=["ps6"])
                P.cp("s", hfb[j][:, i * 128:(i + 1) * 128], ps[6][:, 0:128],
                     rk=["ps6"], wk=[_apkey(hfb[j].ap())])
        for f in range(32):
            pr = ps[f % 2]
            kk = f"psr{f % 2}"
            for dj in range(8):
                P.mm(pr[:], w1b[dj][:, f * 128:(f + 1) * 128], hfb[dj][:],
                     start=(dj == 0), stop=(dj == 7),
                     rk=[_apkey(w1b[dj].ap()), _apkey(hfb[dj].ap())], wk=[kk])
            P.act(rfb[f][:], pr[:], AF.Relu, bias=b1c[:, f:f + 1], e="s",
                  rk=[kk, "b1c_s"], wk=[_apkey(rfb[f].ap())])
        for f in range(4):
            P.dma("g", w2b[f][:], w2_d[f * 128:(f + 1) * 128, :])
        for f in range(32):
            for i in range(4):
                for ob in range(2):
                    P.mm(ps[i * 2 + ob][:], rfb[f][:, i * 128:(i + 1) * 128],
                         w2b[f % 4][:, ob * 512:(ob + 1) * 512],
                         start=(f == 0), stop=False)
            if f + 4 < 32:
                P.dma("g", w2b[(f + 4) % 4][:], w2_d[(f + 4) * 128:(f + 5) * 128, :])
        for i in range(4):
            ti = tb * 4 + i
            P.dma("y", xt[:], x_d[ti * 128:(ti + 1) * 128, :])
            for ob in range(2):
                po = ps[i * 2 + ob]
                P.mm(po[:], onerow_b[:], b2hi[0:1, ob * 512:(ob + 1) * 512],
                     start=False, stop=False)
                P.mm(po[:], onerow_b[:], b2lo[0:1, ob * 512:(ob + 1) * 512],
                     start=False, stop=True)
                P.tt("v", res[:, ob * 512:(ob + 1) * 512], po[:], ht[i][:, ob * 512:(ob + 1) * 512],
                     ALU.add)
            emit_layernorm_tile(P, res[:], outt[:], D, ln_sc)
            P.tt("v", outt[:], outt[:], gbx[:], ALU.mult, rk=["outt", "gbx"], wk=["outt"])
            P.tt("v", outt[:], outt[:], bbx[:], ALU.add, rk=["outt", "bbx"], wk=["outt"])
            P.tt("v", res[:], outt[:], xt[:], ALU.add, rk=["outt", "xt"], wk=["res"])
            emit_layernorm_tile(P, res[:], outt[:], D, ln_sc)
            P.dma("g", xo_d[ti * 128:(ti + 1) * 128, :], outt[:])


def _run_l2a(x_flat, Wqkv, bqkv, Wout, bout, gamma, beta, W1, b1, W2, b2):
    xs = _shards_tm(x_flat)
    fms = _fm_batches(x_flat)
    bqkvc = np.ascontiguousarray(bqkv[0:2048].reshape(16, 128).T)
    bqkvc = np.concatenate([bqkvc, np.zeros((128, 8), bqkvc.dtype)], 1)
    bqkvr = np.ascontiguousarray(bqkv.reshape(1, 3 * D))
    in1 = [{"x": xs[c], "xfm": fms[c], "wqkv": Wqkv, "bqkvc": bqkvc,
            "bqkvr": bqkvr, "wout": Wout, "boutr": bout.reshape(1, D),
            "gammar": gamma.reshape(1, D), "betar": beta.reshape(1, D),
            "identf": _IDENT32} for c in range(NCORES)]
    r1 = _launch(build_l2a1, in1, ["h"])
    b1c = np.ascontiguousarray(b1.reshape(32, 128).T)
    in2 = [{"x": xs[c], "h": r1[c]["h"], "w1": W1, "b1c": b1c, "w2": W2,
            "b2r": b2.reshape(1, D), "gammar": gamma.reshape(1, D),
            "betar": beta.reshape(1, D), "identf": _IDENT32} for c in range(NCORES)]
    return _launch(build_l2a2, in2, ["xo"])



def build_l3a0(nc, P, es):
    xbc_d = nc.dram_tensor("xbc", [256, S], F32, kind="ExternalInput").ap()
    bch_d = nc.dram_tensor("bch", [256, S], F32, kind="ExternalOutput").ap()
    sb = lambda n, shp, dt=F32: es.enter_context(nc.sbuf_tensor(n, shp, dt))
    xa = sb("xa", [128, S])
    xb = sb("xb", [128, S])
    tmp = sb("tmp", [128, S])
    resb = sb("resb", [128, S])
    for half in range(2):
        P.dma("y", xa[:], xbc_d[half * 128:(half + 1) * 128, :])
        P.memset("v", resb[:], 0.0)
        cur, nxt = xa, xb
        st = 1
        for it in range(13):
            ck, nk = _apkey(cur.ap()), _apkey(nxt.ap())
            s_ = st
            P.tt("g", tmp[:, 0:s_], cur[:, S - s_:S], cur[:, s_:2 * s_], ALU.add)
            if S - 2 * s_ > 0:
                P.tt("g", tmp[:, s_:S - s_], cur[:, 0:S - 2 * s_], cur[:, 2 * s_:S], ALU.add)
            P.tt("g", tmp[:, S - s_:S], cur[:, S - 2 * s_:S - s_], cur[:, 0:s_], ALU.add)
            P.stt("v", nxt[:], tmp[:], 0.5, cur[:], ALU.mult, ALU.add)
            P.tt("v", resb[:], resb[:], nxt[:], ALU.add)
            cur, nxt = nxt, cur
            st *= 2
        P.ts("v", resb[:], resb[:], 1.0 / 14.0, None, ALU.mult)
        P.dma("g", bch_d[half * 128:(half + 1) * 128, :], resb[:])


def build_l3a1(nc, P, es):
    xfmo_d = nc.dram_tensor("xfmo", [D, TOK], F32, kind="ExternalInput").ap()
    wkfm_d = nc.dram_tensor("wkfm", [3, D, 2048], F32, kind="ExternalInput").ap()
    wktm_d = nc.dram_tensor("wktm", [3, 2048, D], F32, kind="ExternalInput").ap()
    wjc_d = nc.dram_tensor("wjc", [3, 128, 16], F32, kind="ExternalInput").ap()
    winv_d = nc.dram_tensor("winv", [128, 16], F32, kind="ExternalInput").ap()
    btm_d = nc.dram_tensor("btmask", [128, 4096], F32, kind="ExternalInput").ap()
    xg_d = nc.dram_tensor("xg512", [D, 512], F32, kind="ExternalInput").ap()
    wcf_d = nc.dram_tensor("wcf", [4096, D], F32, kind="ExternalInput").ap()
    bcc_d = nc.dram_tensor("bc_c", [128, 8], F32, kind="ExternalInput").ap()
    gmf_d = nc.dram_tensor("gmf", [D, 128], F32, kind="ExternalInput").ap()
    idf_d = nc.dram_tensor("identf", [128, 128], F32, kind="ExternalInput").ap()
    loc_d = nc.dram_tensor("local", [TOK, D], F32, kind="ExternalOutput").ap()
    glob_d = nc.dram_tensor("glob", [TOK, D], F32, kind="ExternalOutput").ap()

    sb = lambda n, shp, dt=F32: es.enter_context(nc.sbuf_tensor(n, shp, dt))
    idf = sb("idf", [128, 128])
    onecol_b = es.enter_context(nc.sbuf_tensor("onecol_b", [128, 1], BF16))
    c32 = sb("c32", [128, 1])
    stage = sb("stage", [128, 2048])
    xob = sb_tiles(nc, es, "xob", 8, [128, TOK], BF16)
    kfmb = sb_tiles(nc, es, "kfmb", 8, [128, 2048], BF16)
    wvtb = sb_tiles(nc, es, "wvtb", 16, [128, 1024], BF16)
    btm = sb("btm", [128, 4096])
    ptb = sb_tiles(nc, es, "ptb", 16, [128, 512], BF16)
    ssf = sb("ssf", [128, 512])
    acc = sb_tiles(nc, es, "acc", 4, [128, 1024])
    wjc = sb("wjc_s", [128, 48])
    winv = sb("winv_s", [128, 16])
    lrow = sb("lrow", [1, 512])
    lcol = sb("lcol", [128, 1])
    fac = sb("fac", [128, 4])
    otmp = sb("otmp", [128, 512])
    outt = sb("outt", [128, 1024])
    ps = [es.enter_context(nc.psum_tensor(f"ps{i}", [128, 512], F32)) for i in range(8)]

    P.dma("y", idf[:], idf_d)
    P.dma("y", btm[:], btm_d)
    P.dma("y", winv[:], winv_d)
    for J in range(3):
        P.dma("y", wjc[:, J * 16:(J + 1) * 16], wjc_d[J])
    P.memset("v", onecol_b[:], 1.0)
    P.memset("v", c32[:], -32.0)
    for j in range(8):
        P.dma("y", stage[:], xfmo_d[j * 128:(j + 1) * 128, :])
        P.cp("v", xob[j][:], stage[:])

    xgb = sb_tiles(nc, es, "xgb", 8, [128, 512], BF16)
    gtf = sb_tiles(nc, es, "gtf", 8, [128, 256], BF16)
    gtt = sb_tiles(nc, es, "gtt", 2, [128, 1024], BF16)
    bcc = sb("bcc_s", [128, 8])
    aw = sb("aw", [128, 256])
    gt32 = sb("gt32", [128, 256])
    awT = sb_tiles(nc, es, "awT", 2, [128, 512], BF16)
    mrow = sb("mrow", [128, 1])
    lsum = sb("lsum", [128, 1])
    P.dma("y", bcc[:], bcc_d)
    for j in range(8):
        P.dma("y", stage[:, 0:512], xg_d[j * 128:(j + 1) * 128, :])
        P.cp("v", xgb[j][:], stage[:, 0:512])
    wcf_c = wcf_d.rearrange("(c p) o -> c p o", p=128)
    for oc in range(8):
        pk = ps[0]
        for c in range(32):
            P.dma("y", stage[:, 0:128], wcf_c[c][:, oc * 128:(oc + 1) * 128])
            P.cp("v", ssf[0:128, 0:128], stage[:, 0:128], rk=["stage"], wk=["ssf"])
            P.cp("v", kfmb[0][:, 0:128], ssf[0:128, 0:128], rk=["ssf"],
                 wk=[_apkey(kfmb[0].ap())])
            k, dsub = c // 8, c % 8
            rhs = bass.AP(tensor=xgb[dsub].ap().tensor, offset=k,
                          ap=[[xgb[dsub].ap().ap[0][0], 128], [4, 128]])
            P.mm(pk[:, 0:128], kfmb[0][:, 0:128], rhs, start=(c == 0), stop=(c == 31),
                 rk=[_apkey(kfmb[0].ap()), _apkey(xgb[dsub].ap())])
        P.act(gt32[:, 0:128], pk[:, 0:128], AF.Identity, bias=bcc[:, oc:oc + 1], e="s")
        P.dma("y", stage[:, 0:128], gmf_d[oc * 128:(oc + 1) * 128, :])
        P.cp("v", gt32[:, 128:256], stage[:, 0:128], rk=["stage"], wk=["gt32"])
        P.cp("v", gtf[oc][:], gt32[:], rk=["gt32"], wk=[_apkey(gtf[oc].ap())])
        for kc in range(2):
            P.transpose(ps[1][:, 0:128], gt32[:, kc * 128:(kc + 1) * 128], idf[:],
                        rk=["gt32", "idf"])
            P.cp("s", gtt[kc][:, oc * 128:(oc + 1) * 128], ps[1][:, 0:128])
    for qb in range(4):
        for i in range(4):
            qt = qb * 4 + i
            pl = ps[2]
            for dj in range(8):
                P.mm(pl[:, 0:256], xob[dj][:, qt * 128:(qt + 1) * 128], gtf[dj][:],
                     start=(dj == 0), stop=(dj == 7))
            P.red("v", mrow[:], pl[:, 0:256], ALU.max, rk=[_apkey(pl)], wk=["mrow"])
            P.ts("v", mrow[:], mrow[:], -1.0 / 32.0, None, ALU.mult)
            P.act(aw[:], pl[:, 0:256], AF.Exp, scale=1.0 / 32.0, bias=mrow[:],
                  accum_out=lsum[:], e="s")
            P.op("v", lambda eng: eng.reciprocal(lsum[:], lsum[:]), reads=["lsum"], writes=["lsum"])
            P.ts("v", aw[:], aw[:], lsum[:], None, ALU.mult)
            for kc in range(2):
                P.transpose(ps[1][:, 0:128], aw[:, kc * 128:(kc + 1) * 128], idf[:])
                P.cp("s", awT[kc][:, i * 128:(i + 1) * 128], ps[1][:, 0:128])
        for dc in range(8):
            pg = ps[3]
            for kc in range(2):
                P.mm(pg[:], gtt[kc][:, dc * 128:(dc + 1) * 128], awT[kc][:],
                     start=(kc == 0), stop=(kc == 1))
            P.cp("s", otmp[:], pg[:], rk=[_apkey(pg)], wk=["otmp"])
            for i in range(4):
                P.transpose(ps[1][:, 0:128], otmp[:, i * 128:(i + 1) * 128], idf[:])
                P.cp("s", outt[:, dc * 128:(dc + 1) * 128], ps[1][:, 0:128],
                     rk=["ps1"], wk=["outt"])
                P.dma("g", glob_d[(qb * 4 + i) * 128:(qb * 4 + i + 1) * 128,
                                  dc * 128:(dc + 1) * 128], outt[:, dc * 128:(dc + 1) * 128])

    for qb in range(4):
        for i in range(4):
            P.memset("v", acc[i][:], 0.0)
        Js = [J for J in range(3) if (J == 1) or (J == 0 and qb < 2) or (J == 2 and qb >= 2)]
        for J in Js:
            for j in range(8):
                P.dma("y", stage[:], wkfm_d[J, j * 128:(j + 1) * 128, :])
                P.cp("v", kfmb[j][:], stage[:])
            for kt in range(16):
                P.dma("y", stage[:, 0:1024], wktm_d[J, kt * 128:(kt + 1) * 128, :])
                P.cp("v", wvtb[kt][:], stage[:, 0:1024])
            qoff = qb * 512 + 1024 * (1 - J)
            kts = [kt for kt in range(16) if kt * 128 <= qoff + 511]
            for n, kt in enumerate(kts):
                psS = ps[4 + n % 2]
                for dj in range(8):
                    P.mm(psS[:], kfmb[dj][:, kt * 128:(kt + 1) * 128],
                         xob[dj][:, qb * 512:(qb + 1) * 512],
                         start=(dj == 0), stop=(dj == 7))
                dl = qoff - kt * 128
                sl0 = 512 + dl if dl < 3072 else 3583
                P.tt("v", ssf[:], psS[:], btm[:, sl0:sl0 + 512], ALU.add)
                P.act(ptb[kt][:], ssf[:], AF.Exp, scale=1.0 / 32.0, bias=c32[:], e="s")
                P.mm(ps[6][0:1, :], onecol_b[:], ptb[kt][:],
                     start=(n == 0), stop=(n == len(kts) - 1), wk=["psl"])
            P.cp("s", lrow[:], ps[6][0:1, :], rk=["psl"], wk=["lrow"])
            P.ts("v", lrow[:], lrow[:], 1e-30, None, ALU.add)
            P.op("v", lambda eng: eng.reciprocal(lrow[:], lrow[:]), reads=["lrow"], writes=["lrow"])
            for i in range(4):
                P.transpose(ps[2][:, 0:1], lrow[0:1, i * 128:(i + 1) * 128], idf[0:1, 0:1])
                P.cp("v", lcol[:], ps[2][:, 0:1], rk=[_apkey(ps[2])], wk=["lcol"])
                P.tt("v", fac[:, i:i + 1], lcol[:], wjc[:, J * 16 + qb * 4 + i:J * 16 + qb * 4 + i + 1],
                     ALU.mult, rk=["lcol", "wjc_s"], wk=["fac"])
            for dc in range(8):
                pv = ps[7]
                for n, kt in enumerate(kts):
                    P.mm(pv[:], wvtb[kt][:, dc * 128:(dc + 1) * 128], ptb[kt][:],
                         start=(n == 0), stop=(n == len(kts) - 1))
                P.cp("s", otmp[:], pv[:], rk=[_apkey(pv)], wk=["otmp"])
                for i in range(4):
                    P.transpose(ps[1][:, 0:128], otmp[:, i * 128:(i + 1) * 128], idf[:])
                    P.stt("v", acc[i][:, dc * 128:(dc + 1) * 128], ps[1][:, 0:128],
                          fac[:, i:i + 1], acc[i][:, dc * 128:(dc + 1) * 128],
                          ALU.mult, ALU.add, rk=[_apkey(ps[1]), "fac", _apkey(acc[i].ap())],
                          wk=[_apkey(acc[i].ap())])
        for i in range(4):
            P.ts("v", outt[:], acc[i][:], winv[:, qb * 4 + i:qb * 4 + i + 1], None, ALU.mult,
                 rk=[_apkey(acc[i].ap()), "winv_s"], wk=["outt"])
            P.dma("g", loc_d[(qb * 4 + i) * 128:(qb * 4 + i + 1) * 128, :], outt[:])


def build_l3a2(nc, P, es):
    x_d = nc.dram_tensor("x", [TOK, D], F32, kind="ExternalInput").ap()
    loc_d = nc.dram_tensor("local", [TOK, D], F32, kind="ExternalInput").ap()
    glob_d = nc.dram_tensor("glob", [TOK, D], F32, kind="ExternalInput").ap()
    bch_d = nc.dram_tensor("bcho", [TOK, D], F32, kind="ExternalInput").ap()
    wmt_d = nc.dram_tensor("wmt", [2 * D, D], F32, kind="ExternalInput").ap()
    bmr_d = nc.dram_tensor("bmr", [1, D], F32, kind="ExternalInput").ap()
    wot_d = nc.dram_tensor("wot", [D, D], F32, kind="ExternalInput").ap()
    bor_d = nc.dram_tensor("bor", [1, D], F32, kind="ExternalInput").ap()
    idf_d = nc.dram_tensor("identf", [128, 128], F32, kind="ExternalInput").ap()
    xo_d = nc.dram_tensor("xo", [TOK, D], F32, kind="ExternalOutput").ap()

    sb = lambda n, shp, dt=F32: es.enter_context(nc.sbuf_tensor(n, shp, dt))
    idf = sb("idf", [128, 128])
    onerow_b = es.enter_context(nc.sbuf_tensor("onerow_b", [1, 128], BF16))
    stage = sb("stage", [128, 1024])
    wm = sb_tiles(nc, es, "wm", 16, [128, 1024], BF16)
    wo3 = sb_tiles(nc, es, "wo3", 8, [128, 1024], BF16)
    lt = sb("lt", [128, 1024])
    gt_ = sb("gt_", [128, 1024])
    bt = sb("bt", [128, 1024])
    xt = sb("xt", [128, 1024])
    lgf = sb_tiles(nc, es, "lgf", 16, [128, 128], BF16)
    mw = sb("mw", [128, 1024])
    mixf = sb_tiles(nc, es, "mixf", 8, [128, 128], BF16)
    res = sb("res", [128, 1024])
    outt = sb("outt", [128, 1024])
    ln_sc = {"mean": sb("ln_mean", [128, 1]), "var": sb("ln_var", [128, 1]),
             "xc": sb("ln_xc", [128, 1024]), "sq": sb("ln_sq", [128, 1024]),
             "eps": sb("ln_eps", [128, 1])}
    P.memset("v", ln_sc["eps"], 1e-5)
    P.memset("v", onerow_b[:], 1.0)
    ps = [es.enter_context(nc.psum_tensor(f"ps{i}", [128, 512], F32)) for i in range(6)]

    P.dma("y", idf[:], idf_d)
    bmhi, bmlo = emit_bias_hilo(P, sb, "bm", bmr_d, D)
    bohi, bolo = emit_bias_hilo(P, sb, "bo3", bor_d, D)
    for j in range(16):
        P.dma("y", stage[:], wmt_d[j * 128:(j + 1) * 128, :])
        P.cp("v", wm[j][:], stage[:])
    for j in range(8):
        P.dma("y", stage[:], wot_d[j * 128:(j + 1) * 128, :])
        P.cp("v", wo3[j][:], stage[:])

    for ti in range(NT):
        P.dma("y", lt[:], loc_d[ti * 128:(ti + 1) * 128, :])
        P.dma("y", gt_[:], glob_d[ti * 128:(ti + 1) * 128, :])
        P.dma("y", bt[:], bch_d[ti * 128:(ti + 1) * 128, :])
        P.dma("y", xt[:], x_d[ti * 128:(ti + 1) * 128, :])
        for j in range(8):
            P.transpose(ps[5][:, 0:128], lt[:, j * 128:(j + 1) * 128], idf[:])
            P.cp("s", lgf[j][:], ps[5][:, 0:128], rk=[_apkey(ps[5])], wk=[_apkey(lgf[j].ap())])
            P.transpose(ps[5][:, 0:128], gt_[:, j * 128:(j + 1) * 128], idf[:])
            P.cp("s", lgf[8 + j][:], ps[5][:, 0:128], rk=[_apkey(ps[5])],
                 wk=[_apkey(lgf[8 + j].ap())])
        for ob in range(2):
            pm = ps[ob]
            for j in range(16):
                P.mm(pm[:], lgf[j][:], wm[j][:, ob * 512:(ob + 1) * 512],
                     start=(j == 0), stop=False)
            P.mm(pm[:], onerow_b[:], bmhi[0:1, ob * 512:(ob + 1) * 512],
                 start=False, stop=False)
            P.mm(pm[:], onerow_b[:], bmlo[0:1, ob * 512:(ob + 1) * 512],
                 start=False, stop=True)
            P.act(mw[:, ob * 512:(ob + 1) * 512], pm[:], AF.Sigmoid, e="s")
        P.tt("v", res[:], lt[:], gt_[:], ALU.subtract)
        P.tt("v", res[:], res[:], mw[:], ALU.mult)
        P.tt("v", res[:], res[:], gt_[:], ALU.add)
        P.tt("v", res[:], res[:], bt[:], ALU.add)
        for j in range(8):
            P.transpose(ps[5][:, 0:128], res[:, j * 128:(j + 1) * 128], idf[:])
            P.cp("s", mixf[j][:], ps[5][:, 0:128], rk=[_apkey(ps[5])], wk=[_apkey(mixf[j].ap())])
        for ob in range(2):
            po = ps[2 + ob]
            for j in range(8):
                P.mm(po[:], mixf[j][:], wo3[j][:, ob * 512:(ob + 1) * 512],
                     start=(j == 0), stop=False)
            P.mm(po[:], onerow_b[:], bohi[0:1, ob * 512:(ob + 1) * 512],
                 start=False, stop=False)
            P.mm(po[:], onerow_b[:], bolo[0:1, ob * 512:(ob + 1) * 512],
                 start=False, stop=True)
            P.tt("v", res[:, ob * 512:(ob + 1) * 512], po[:], xt[:, ob * 512:(ob + 1) * 512],
                 ALU.add)
        emit_layernorm_tile(P, res[:], outt[:], D, ln_sc)
        P.dma("g", xo_d[ti * 128:(ti + 1) * 128, :], outt[:])


def _run_l3a(x_flat, gm, Wc, bc, Wm, bm, Wo, bo):
    xs = _shards_tm(x_flat)
    xb = x_flat.reshape(B, S, D)
    WINDOW, STRIDE = 2048, 1024
    nw = (S - WINDOW) // STRIDE + 1
    tri = np.linspace(0.5, 1.5, WINDOW)
    wsum = np.zeros(S)
    for w in range(nw):
        wsum[w * STRIDE:w * STRIDE + WINDOW] += tri
    btm = np.full((128, 4096), -1e30, np.float32)
    ii = np.arange(128)[:, None]
    cc = np.arange(4096)[None, :]
    btm[ii <= cc - 512] = 0.0
    wcf = np.ascontiguousarray(Wc.transpose(2, 1, 0).reshape(4096, D))
    bc_c = np.ascontiguousarray(bc.reshape(8, 128).T)
    in1 = []
    in0 = []
    in2 = []
    for c in range(NCORES):
        b = c // (NCORES // B)
        r0 = 2048 * (c % (NCORES // B))
        m = r0 // 1024
        wk = np.zeros((3, 2048, D), x_flat.dtype)
        wjc = np.zeros((3, 128, 16))
        for J in range(3):
            w = m - 1 + J
            if 0 <= w <= nw - 1:
                wk[J] = xb[b, w * STRIDE:w * STRIDE + WINDOW]
                t = np.arange(TOK)
                qoff = t + 1024 * (1 - J)
                wt = np.where((qoff >= 0) & (qoff < WINDOW), tri[np.clip(qoff, 0, WINDOW - 1)], 0.0)
                wjc[J] = wt.reshape(16, 128).T
        winv = (1.0 / (wsum[r0:r0 + TOK] + 1e-6)).reshape(16, 128).T
        wkfm = np.ascontiguousarray(wk.transpose(0, 2, 1))
        xfmo = np.ascontiguousarray(xb[b, r0:r0 + TOK].T)
        xbc_ = np.ascontiguousarray(xb[b].T[(c % 4) * 256:(c % 4) * 256 + 256])
        in0.append({"xbc": xbc_})
        in1.append({"xfmo": xfmo, "wkfm": wkfm, "wktm": np.ascontiguousarray(wk),
                    "wjc": _f32c(wjc), "winv": _f32c(winv),
                    "btmask": btm, "xg512": np.ascontiguousarray(xb[b, 0:512].T),
                    "wcf": wcf, "bc_c": bc_c,
                    "gmf": np.ascontiguousarray(gm.reshape(128, D).T),
                    "identf": _IDENT32})
    r0_ = _launch(build_l3a0, in0, ["bch"])
    r1 = _launch(build_l3a1, in1, ["local", "glob"])
    bch_full = np.zeros((B, S, D), np.float32)
    for c in range(NCORES):
        b = c // (NCORES // B)
        dsl = (c % 4) * 256
        bch_full[b, :, dsl:dsl + 256] = r0_[c]["bch"].T
    bch_flat = bch_full.reshape(NTOK, D)
    wmt = np.ascontiguousarray(Wm.T)
    wot = np.ascontiguousarray(Wo.T)
    for c in range(NCORES):
        in2.append({"x": xs[c], "local": r1[c]["local"], "glob": r1[c]["glob"],
                    "bcho": np.ascontiguousarray(bch_flat[c * TOK:(c + 1) * TOK]),
                    "wmt": wmt, "bmr": bm.reshape(1, D), "wot": wot,
                    "bor": bo.reshape(1, D), "identf": _IDENT32})
    return _launch(build_l3a2, in2, ["xo"])



def build_l3f(nc, P, es):
    E, DF = 8, 4096
    x_d = nc.dram_tensor("x", [TOK, D], F32, kind="ExternalInput").ap()
    wg8_d = nc.dram_tensor("wg8", [D, E], F32, kind="ExternalInput").ap()
    wgall_d = nc.dram_tensor("wg_all", [E, D, DF], F32, kind="ExternalInput").ap()
    wuall_d = nc.dram_tensor("wu_all", [E, D, DF], F32, kind="ExternalInput").ap()
    wdall_d = nc.dram_tensor("wd_all", [E, DF, D], F32, kind="ExternalInput").ap()
    bg_d = nc.dram_tensor("bg", [E, 128, 32], F32, kind="ExternalInput").ap()
    bu_d = nc.dram_tensor("bu", [E, 128, 32], F32, kind="ExternalInput").ap()
    bd_d = nc.dram_tensor("bd", [E, D], F32, kind="ExternalInput").ap()
    idf_d = nc.dram_tensor("identf", [128, 128], F32, kind="ExternalInput").ap()
    xo_d = nc.dram_tensor("xo", [TOK, D], F32, kind="ExternalOutput").ap()
    wgc_d = nc.dram_tensor("wgc", [E, 8, 128, DF], BF16, kind="Internal").ap()
    wuc_d = nc.dram_tensor("wuc", [E, 8, 128, DF], BF16, kind="Internal").ap()
    wdc_d = nc.dram_tensor("wdc", [E, 32, 128, D], BF16, kind="Internal").ap()

    sb = lambda n, shp, dt=F32: es.enter_context(nc.sbuf_tensor(n, shp, dt))
    idf = sb("idf", [128, 128])
    onerow = sb("onerow", [1, 128])
    stage = sb("stage", [128, 2048])
    cbuf = sb("cbuf", [128, 2048], BF16)
    xt = sb("xt", [128, 1024])
    xfb = sb_tiles(nc, es, "xfb", 8, [128, TOK], BF16)
    wg8b = sb_tiles(nc, es, "wg8b", 8, [128, E])
    xff32 = sb("xff32", [128, 1024])
    bgs = sb("bgs", [128, 32])
    bus = sb("bus", [128, 32])
    bds = sb("bds", [8, 1024])
    cmb = sb("cmb", [128, 8])
    esb = sb("esb", [128, 8])
    m1 = sb("m1c", [128, 1])
    v1 = sb("v1c", [128, 1])
    v2 = sb("v2c", [128, 1])
    den = sb("denc", [128, 1])
    msk = sb("mskc", [128, 8])
    cmbT8 = sb("cmbT8", [8, TOK])
    cbrow = sb("cbrow", [1, TOK])
    cbc = sb("cbc", [128, 512])
    wgb = sb_tiles(nc, es, "wgb", 2, [128, 1024], BF16)
    wub = sb_tiles(nc, es, "wub", 2, [128, 1024], BF16)
    wdb = sb_tiles(nc, es, "wdb", 1, [128, 4096], BF16)
    gev = sb_tiles(nc, es, "gev", 1, [128, 512])
    hfb = sb_tiles(nc, es, "hfb", 32, [128, 512], BF16)
    yac = sb_tiles(nc, es, "yac", 8, [128, TOK])
    tmp5 = sb("tmp5", [128, 512])
    res = sb("res", [128, 1024])
    outt = sb("outt", [128, 1024])
    ln_sc = {"mean": sb("ln_mean", [128, 1]), "var": sb("ln_var", [128, 1]),
             "xc": sb("ln_xc", [128, 1024]), "sq": sb("ln_sq", [128, 1024]),
             "eps": sb("ln_eps", [128, 1])}
    P.memset("v", ln_sc["eps"], 1e-5)
    P.memset("v", onerow[:], 1.0)
    ps = [es.enter_context(nc.psum_tensor(f"ps{i}", [128, 512], F32)) for i in range(8)]

    P.dma("y", idf[:], idf_d)
    P.dma("y", bds[:], bd_d)
    for e in range(E):
        for c in range(8):
            for half in range(2):
                P.dma("y", stage[:], wgall_d[e, c * 128:(c + 1) * 128, half * 2048:(half + 1) * 2048])
                P.cp("v", cbuf[:], stage[:])
                P.dma("g", wgc_d[e, c][:, half * 2048:(half + 1) * 2048], cbuf[:])
                P.dma("y", stage[:], wuall_d[e, c * 128:(c + 1) * 128, half * 2048:(half + 1) * 2048])
                P.cp("v", cbuf[:], stage[:])
                P.dma("g", wuc_d[e, c][:, half * 2048:(half + 1) * 2048], cbuf[:])
        for c in range(32):
            P.dma("y", stage[:, 0:1024], wdall_d[e, c * 128:(c + 1) * 128, :])
            P.cp("v", cbuf[:, 0:1024], stage[:, 0:1024])
            P.dma("g", wdc_d[e, c], cbuf[:, 0:1024])
    for j in range(8):
        P.dma("y", stage[:, 0:E], wg8_d[j * 128:(j + 1) * 128, :])
        P.cp("v", wg8b[j][:], stage[:, 0:E])

    for ti in range(NT):
        P.dma("y", xt[:], x_d[ti * 128:(ti + 1) * 128, :])
        for j in range(8):
            P.transpose(ps[6][:, 0:128], xt[:, j * 128:(j + 1) * 128], idf[:])
            P.cp("s", xfb[j][:, ti * 128:(ti + 1) * 128], ps[6][:, 0:128])
            P.cp("s", xff32[:, j * 128:(j + 1) * 128], ps[6][:, 0:128])
        pl = ps[7]
        for dj in range(8):
            P.mm(pl[:, 0:E], xff32[:, dj * 128:(dj + 1) * 128], wg8b[dj][:],
                 start=(dj == 0), stop=(dj == 7))
        P.red("v", m1[:], pl[:, 0:E], ALU.max, rk=[_apkey(pl)], wk=["m1c"])
        P.ts("v", m1[:], m1[:], -1.0, None, ALU.mult)
        P.act(esb[:], pl[:, 0:E], AF.Exp, bias=m1[:], e="s",
              rk=[_apkey(pl), "m1c"], wk=["esb"])
        P.red("v", v1[:], esb[:], ALU.max)
        P.ts("v", msk[:], esb[:], v1[:], None, ALU.is_ge)
        P.ts("v", msk[:], msk[:], -1.0, 1.0, ALU.mult, ALU.add)
        P.tt("v", cmb[:], esb[:], msk[:], ALU.mult)
        P.red("v", v2[:], cmb[:], ALU.max, rk=["cmb"], wk=["v2c"])
        P.ts("v", msk[:], esb[:], v2[:], None, ALU.is_ge)
        P.tt("v", den[:], v1[:], v2[:], ALU.add)
        P.op("v", lambda eng: eng.reciprocal(den[:], den[:]), reads=["denc"], writes=["denc"])
        P.tt("v", cmb[:], esb[:], msk[:], ALU.mult)
        P.ts("v", cmb[:], cmb[:], den[:], None, ALU.mult)
        P.transpose(ps[6][0:8, 0:128], cmb[:], idf[:])
        P.cp("s", cmbT8[:, ti * 128:(ti + 1) * 128], ps[6][0:8, 0:128])
    for e in range(E):
        P.dma("g", cbrow[:], cmbT8[e:e + 1, :])
        P.dma("y", bgs[:], bg_d[e])
        P.dma("y", bus[:], bu_d[e])
        for tb in range(4):
            P.mm(ps[5][:], onerow[:], cbrow[0:1, tb * 512:(tb + 1) * 512],
                 start=True, stop=True)
            P.cp("s", cbc[:], ps[5][:], rk=[_apkey(ps[5])], wk=["cbc"])
            for fc in range(32):
                P.dma("y", wgb[fc % 2][:].rearrange("p (c f) -> p c f", c=8),
                      wgc_d[e, :, :, fc * 128:(fc + 1) * 128].rearrange("c p f -> p c f"))
                P.dma("g", wub[fc % 2][:].rearrange("p (c f) -> p c f", c=8),
                      wuc_d[e, :, :, fc * 128:(fc + 1) * 128].rearrange("c p f -> p c f"))
                pg = ps[fc % 2]
                pu = ps[2 + fc % 2]
                for dj in range(8):
                    P.mm(pg[:], wgb[fc % 2][:, dj * 128:(dj + 1) * 128],
                         xfb[dj][:, tb * 512:(tb + 1) * 512], start=(dj == 0), stop=(dj == 7))
                for dj in range(8):
                    P.mm(pu[:], wub[fc % 2][:, dj * 128:(dj + 1) * 128],
                         xfb[dj][:, tb * 512:(tb + 1) * 512], start=(dj == 0), stop=(dj == 7))
                g_ = gev[0]
                P.act(g_[:], pg[:], AF.Silu, bias=bgs[:, fc:fc + 1], e="s")
                P.act(tmp5[:], pu[:], AF.Identity, bias=bus[:, fc:fc + 1], e="s")
                P.tt("v", hfb[fc][:], g_[:], tmp5[:], ALU.mult)
            for dc in range(8):
                P.dma("g", wdb[0][:].rearrange("p (c f) -> p c f", c=32),
                      wdc_d[e, :, :, dc * 128:(dc + 1) * 128].rearrange("c p f -> p c f"))
                pv = ps[4]
                for fc in range(32):
                    P.mm(pv[:], wdb[0][:, fc * 128:(fc + 1) * 128], hfb[fc][:],
                         start=(fc == 0), stop=(fc == 31))
                P.tt("v", tmp5[:], pv[:], cbc[:], ALU.mult,
                     rk=[_apkey(pv), "cbc"], wk=["tmp5"])
                if e == 0:
                    P.cp("v", yac[dc][:, tb * 512:(tb + 1) * 512], tmp5[:])
                else:
                    P.tt("v", yac[dc][:, tb * 512:(tb + 1) * 512],
                         yac[dc][:, tb * 512:(tb + 1) * 512], tmp5[:], ALU.add)

    for ti in range(NT):
        P.dma("y", xt[:], x_d[ti * 128:(ti + 1) * 128, :])
        for ob in range(2):
            po = ps[6 + ob]
            for q in range(4):
                dc = ob * 4 + q
                P.transpose(po[:, q * 128:(q + 1) * 128],
                            yac[dc][:, ti * 128:(ti + 1) * 128], idf[:],
                            rk=[_apkey(yac[dc].ap()), "idf"], wk=[_apkey(po)])
            P.mm(ps[5][:], cmbT8[:, ti * 128:(ti + 1) * 128],
                 bds[:, ob * 512:(ob + 1) * 512], start=True, stop=True,
                 rk=["cmbT8", "bds"], wk=[_apkey(ps[5])])
            P.tt("v", res[:, ob * 512:(ob + 1) * 512], po[:], xt[:, ob * 512:(ob + 1) * 512],
                 ALU.add)
            P.tt("v", res[:, ob * 512:(ob + 1) * 512], res[:, ob * 512:(ob + 1) * 512],
                 ps[5][:], ALU.add)
        emit_layernorm_tile(P, res[:], outt[:], D, ln_sc)
        P.dma("g", xo_d[ti * 128:(ti + 1) * 128, :], outt[:])


def _run_l3f(x_shards, Wg, w_gate, b_gate, w_up, b_up, w_down, b_down):
    bg = np.ascontiguousarray(b_gate.reshape(8, 32, 128).transpose(0, 2, 1))
    bu = np.ascontiguousarray(b_up.reshape(8, 32, 128).transpose(0, 2, 1))
    in_maps = [{"x": x_shards[c], "wg8": Wg, "wg_all": w_gate, "wu_all": w_up,
                "wd_all": w_down, "bg": bg, "bu": bu, "bd": b_down,
                "identf": _IDENT32} for c in range(NCORES)]
    return _launch(build_l3f, in_maps, ["xo"])



def kernel(L0_attn_x, L0_attn_Wq, L0_attn_Wk, L0_attn_Wv, L0_attn_Wo,
           L0_ffn_w_gate, L0_ffn_w_up, L0_ffn_w_down,
           L1_attn_Wqkv, L1_attn_Wproj, L1_attn_bproj,
           L1_ffn_w1, L1_ffn_b1, L1_ffn_w2, L1_ffn_b2,
           L2_attn_Wqkv, L2_attn_bqkv, L2_attn_Wout, L2_attn_bout,
           L2_attn_gamma, L2_attn_beta, L2_attn_W1, L2_attn_b1, L2_attn_W2,
           L2_attn_b2, L2_ffn_clip_x, L2_ffn_clip_down, L2_ffn_w_gate,
           L2_ffn_w_up, L2_ffn_w_down,
           L3_attn_global_memory, L3_attn_Wc, L3_attn_bc, L3_attn_Wm,
           L3_attn_bm, L3_attn_Wo, L3_attn_bo,
           L3_ffn_Wg, L3_ffn_w_gate, L3_ffn_b_gate, L3_ffn_w_up, L3_ffn_b_up,
           L3_ffn_w_down, L3_ffn_b_down):
    x = np.ascontiguousarray(np.asarray(L0_attn_x).reshape(NTOK, D))
    r = _run_l0a(x, np.asarray(L0_attn_Wq), np.asarray(L0_attn_Wk),
                 np.asarray(L0_attn_Wv), np.asarray(L0_attn_Wo))
    x = np.concatenate([np.asarray(t["xo"]) for t in r], 0)
    r = _run_l0f(_shards_tm(x), np.asarray(L0_ffn_w_gate),
                 np.asarray(L0_ffn_w_up), np.asarray(L0_ffn_w_down))
    x = np.concatenate([np.asarray(t["xo"]) for t in r], 0)
    r = _run_l1a(_shards_tm(x), np.asarray(L1_attn_Wqkv),
                 np.asarray(L1_attn_Wproj), np.asarray(L1_attn_bproj))
    x = np.concatenate([np.asarray(t["xo"]) for t in r], 0)
    r = _run_l1f(_shards_tm(x), np.asarray(L1_ffn_w1), np.asarray(L1_ffn_b1),
                 np.asarray(L1_ffn_w2), np.asarray(L1_ffn_b2))
    x = np.concatenate([np.asarray(t["xo"]) for t in r], 0)
    r = _run_l2a(x, np.asarray(L2_attn_Wqkv), np.asarray(L2_attn_bqkv),
                 np.asarray(L2_attn_Wout), np.asarray(L2_attn_bout),
                 np.asarray(L2_attn_gamma), np.asarray(L2_attn_beta),
                 np.asarray(L2_attn_W1), np.asarray(L2_attn_b1),
                 np.asarray(L2_attn_W2), np.asarray(L2_attn_b2))
    x = np.concatenate([np.asarray(t["xo"]) for t in r], 0)
    r = _run_l2f(_shards_tm(x), np.asarray(L2_ffn_clip_x),
                 np.asarray(L2_ffn_clip_down), np.asarray(L2_ffn_w_gate),
                 np.asarray(L2_ffn_w_up), np.asarray(L2_ffn_w_down))
    x = np.concatenate([np.asarray(t["xo"]) for t in r], 0)
    r = _run_l3a(x, np.asarray(L3_attn_global_memory), np.asarray(L3_attn_Wc),
                 np.asarray(L3_attn_bc), np.asarray(L3_attn_Wm),
                 np.asarray(L3_attn_bm), np.asarray(L3_attn_Wo),
                 np.asarray(L3_attn_bo))
    x = np.concatenate([np.asarray(t["xo"]) for t in r], 0)
    r = _run_l3f(_shards_tm(x), np.asarray(L3_ffn_Wg), np.asarray(L3_ffn_w_gate),
                 np.asarray(L3_ffn_b_gate), np.asarray(L3_ffn_w_up),
                 np.asarray(L3_ffn_b_up), np.asarray(L3_ffn_w_down),
                 np.asarray(L3_ffn_b_down))
    x = np.concatenate([np.asarray(t["xo"]) for t in r], 0)
    return x.reshape(B, S, D)


# version 03 (final)
# speedup vs baseline: 1.0366x; 1.0366x over previous
import os
import contextlib
import numpy as np
import concourse.bass as bass
import concourse.mybir as mybir
from concourse.bass_utils import run_bass_kernel_spmd

F32 = mybir.dt.float32
BF16 = mybir.dt.bfloat16
I32 = mybir.dt.int32
AF = mybir.ActivationFunctionType
ALU = mybir.AluOpType
AX = mybir.AxisListType

NCORES = 8
B, S, D = 2, 8192, 1024
NTOK = B * S
TOK = NTOK // NCORES
NT = TOK // 128


_ENGS = ("y", "v", "s", "t", "g")


def _apkey(ap):
    t = getattr(ap, "tensor", None)
    if t is not None:
        return t.name
    return getattr(ap, "name", None)


def _asap(x):
    if x is None or isinstance(x, (int, float, bass.AP)):
        return x
    if hasattr(x, "ap"):
        return x.ap()
    return x



def _keys(custom, aps):
    out = list(custom)
    for a in aps:
        if a is None or isinstance(a, (int, float)):
            continue
        k = _apkey(_asap(a))
        if k and k not in out:
            out.append(k)
    return out

NDMASLOT = {"y": 12, "v": 4, "s": 8, "g": 8, "t": 2}


class Prog:
    def __init__(self, nc, es):
        self.nc = nc
        self.es = es
        self.sem = {}
        self.cnt = {}
        for tag in ("y", "v", "s", "t", "g", "cc"):
            self.sem[tag] = es.enter_context(nc.semaphore("sem_" + tag))
            self.cnt[tag] = 0
        for e, n in NDMASLOT.items():
            for sl in range(n):
                tag = (e, sl)
                self.sem[tag] = es.enter_context(nc.semaphore(f"dma_{e}{sl}"))
                self.cnt[tag] = 0
        self.dma_idx = {e: 0 for e in NDMASLOT}
        self.q = {e: [] for e in _ENGS}
        self.last_w = {}
        self.readers = {}
        self.known = {e: {} for e in _ENGS}
        self.snaps = {tag: {} for tag in self.sem}
        self.nwaits = 0
        self.nops = {e: 0 for e in _ENGS}

    def _need_for(self, reads, writes, skip_tags=()):
        need = {}
        for k in reads:
            lw = self.last_w.get(k)
            if lw and lw[0] not in skip_tags:
                need[lw[0]] = max(need.get(lw[0], 0), lw[1])
        for k in writes:
            lw = self.last_w.get(k)
            if lw and lw[0] not in skip_tags:
                need[lw[0]] = max(need.get(lw[0], 0), lw[1])
            for rt, rc in self.readers.get(k, {}).items():
                if rt not in skip_tags:
                    need[rt] = max(need.get(rt, 0), rc)
        return need

    def _emit_waits(self, e, need):
        kn = self.known[e]
        for t_, c_ in sorted(need.items(), key=lambda kv: str(kv[0])):
            if kn.get(t_, 0) >= c_:
                continue
            sem = self.sem[t_]
            self.q[e].append(lambda eng, sem=sem, c=c_: eng.wait_ge(sem, c))
            self.nwaits += 1
            snap = self.snaps[t_].get(c_)
            if snap:
                for st_, sc_ in snap.items():
                    if kn.get(st_, 0) < sc_:
                        kn[st_] = sc_
            if kn.get(t_, 0) < c_:
                kn[t_] = c_

    def op(self, e, fn, reads=(), writes=(), dma=False, chain=None, tag_override=None):
        selfser = True
        if tag_override is not None:
            tag = tag_override
            inc = 1
        elif dma:
            slot = self.dma_idx[e] % NDMASLOT[e]
            self.dma_idx[e] += 1
            tag = (e, slot)
            inc = 16
        else:
            tag = e
            inc = 1
            selfser = False
        if chain in ("mid", "end"):
            need = self._need_for(reads, ())
        else:
            need = self._need_for(reads, writes)
        if selfser and self.cnt[tag] > 0:
            need[tag] = max(need.get(tag, 0), self.cnt[tag])
        self._emit_waits(e, need)
        self.cnt[tag] += inc
        c = self.cnt[tag]
        sem = self.sem[tag]
        import sys as _sys
        _ln = []
        _f = _sys._getframe(1)
        for _ in range(4):
            if _f is None:
                break
            _ln.append(_f.f_lineno)
            _f = _f.f_back

        def _run(eng, fn=fn, sem=sem, inc=inc, _ln=_ln):
            try:
                return fn(eng).then_inc(sem, inc)
            except Exception as ex:
                raise RuntimeError(f"emit site lines {_ln}: {ex}") from ex
        self.q[e].append(_run)
        self.nops[e] += 1
        kn = dict(self.known[e])
        kn[tag] = c
        self.snaps[tag][c] = kn
        for k in reads:
            self.readers.setdefault(k, {})[tag] = c
        if chain in ("start", "mid"):
            pass
        else:
            for k in writes:
                self.last_w[k] = (tag, c)
                self.readers[k] = {}
        return c

    def dma(self, e, out, in_, rk=(), wk=(), **kw):
        out = _asap(out); in_ = _asap(in_)
        reads = _keys(rk, [in_])
        writes = _keys(wk, [out])
        return self.op(e, lambda eng: eng.dma_start(out=out, in_=in_, **kw),
                       reads=reads, writes=writes, dma=True)

    def cp(self, e, out, in_, rk=(), wk=()):
        out = _asap(out); in_ = _asap(in_)
        reads = _keys(rk, [in_])
        writes = _keys(wk, [out])
        if e == "s":
            return self.op(e, lambda eng: eng.copy(out, in_), reads, writes)
        return self.op(e, lambda eng: eng.tensor_copy(out, in_), reads, writes)

    def act(self, out, in_, func, bias=0.0, scale=1.0, accum_out=None,
            rk=(), wk=(), e="s"):
        out = _asap(out); in_ = _asap(in_); bias = _asap(bias); scale = _asap(scale); accum_out = _asap(accum_out)
        reads = _keys(rk, [in_, bias if isinstance(bias, bass.AP) else None,
                           scale if isinstance(scale, bass.AP) else None])
        writes = _keys(wk, [out, accum_out])
        return self.op(e, lambda eng: eng.activation(out, in_, func, bias=bias, scale=scale,
                                                     accum_out=accum_out), reads, writes)

    def tt(self, e, out, in0, in1, alu, rk=(), wk=()):
        out = _asap(out); in0 = _asap(in0); in1 = _asap(in1)
        reads = _keys(rk, [in0, in1])
        writes = _keys(wk, [out])
        return self.op(e, lambda eng: eng.tensor_tensor(out, in0, in1, alu), reads, writes)

    def ts(self, e, out, in0, s1, s2, op0, op1=None, rk=(), wk=()):
        out = _asap(out); in0 = _asap(in0); s1 = _asap(s1); s2 = _asap(s2)
        reads = _keys(rk, [in0, s1 if isinstance(s1, bass.AP) else None,
                           s2 if isinstance(s2, bass.AP) else None])
        writes = _keys(wk, [out])
        if op1 is None:
            return self.op(e, lambda eng: eng.tensor_scalar(out, in0, s1, None, op0), reads, writes)
        return self.op(e, lambda eng: eng.tensor_scalar(out, in0, s1, s2, op0, op1), reads, writes)

    def stt(self, e, out, in0, s, in1, op0, op1, rk=(), wk=()):
        out = _asap(out); in0 = _asap(in0); in1 = _asap(in1); s = _asap(s)
        reads = _keys(rk, [in0, in1, s if isinstance(s, bass.AP) else None])
        writes = _keys(wk, [out])
        return self.op(e, lambda eng: eng.scalar_tensor_tensor(out, in0, s, in1, op0, op1), reads, writes)

    def red(self, e, out, in_, alu, rk=(), wk=(), negate=None, absval=None):
        out = _asap(out); in_ = _asap(in_)
        reads = _keys(rk, [in_])
        writes = _keys(wk, [out])
        return self.op(e, lambda eng: eng.tensor_reduce(out, in_, AX.X, alu,
                                                        apply_absolute_value=absval, negate=negate),
                       reads, writes)

    def memset(self, e, ap, val, wk=()):
        ap = _asap(ap)
        writes = _keys(wk, [ap])
        return self.op(e, lambda eng: eng.memset(ap, val), (), writes)

    def mm(self, psum, lhsT, rhs, start, stop, rk=(), wk=(), tile_position=None):
        psum = _asap(psum); lhsT = _asap(lhsT); rhs = _asap(rhs)
        reads = _keys(rk, [lhsT, rhs])
        writes = _keys(wk, [psum])
        chain = "start" if (start and not stop) else ("end" if (stop and not start) else
                                                      ("mid" if (not start and not stop) else None))
        return self.op("t", lambda eng: eng.matmul(psum, lhsT, rhs, start=start, stop=stop,
                                                   tile_position=tile_position),
                       reads, writes, chain=chain)

    def transpose(self, psum_out, in_, ident, rk=(), wk=()):
        psum_out = _asap(psum_out); in_ = _asap(in_); ident = _asap(ident)
        reads = _keys(rk, [in_, ident])
        writes = _keys(wk, [psum_out])
        return self.op("t", lambda eng: eng.transpose(psum_out, in_, ident), reads, writes)

    def collective(self, kind, alu, ins, outs, rk=(), wk=()):
        groups = [list(range(NCORES))]
        ins = [_asap(a) for a in ins]
        outs = [_asap(a) for a in outs]
        reads = list(rk) or [x for x in (_apkey(a) for a in ins) if x]
        writes = list(wk) or [x for x in (_apkey(a) for a in outs) if x]
        return self.op("g", lambda eng: eng.collective_compute(
            kind, alu, groups, [a.opt() for a in ins], [a.opt() for a in outs]),
            reads=reads, writes=writes, tag_override="cc")

    def barrier_on(self, e, keys):
        need = self._need_for(keys, ())
        self._emit_waits(e, need)

    def build_block(self):
        nc = self.nc
        with nc.Block() as block:
            @block.sync
            def _(eng):
                for f in self.q["y"]:
                    f(eng)

            @block.vector
            def _(eng):
                for f in self.q["v"]:
                    f(eng)

            @block.scalar
            def _(eng):
                for f in self.q["s"]:
                    f(eng)

            @block.tensor
            def _(eng):
                for f in self.q["t"]:
                    f(eng)

            @block.gpsimd
            def _(eng):
                for f in self.q["g"]:
                    f(eng)



def round_nearest(P, e, y, sc, key=""):
    y = _asap(y)
    P.cp(e, sc["i32"], y)
    P.cp(e, sc["t"], sc["i32"])
    P.tt(e, sc["a"], y, sc["t"], ALU.subtract)
    P.ts(e, sc["b"], sc["a"], -0.5, None, ALU.is_lt)
    P.ts(e, sc["a"], sc["a"], 0.5, None, ALU.is_gt)
    P.tt(e, sc["a"], sc["a"], sc["b"], ALU.subtract)
    P.tt(e, y, sc["t"], sc["a"], ALU.add)


def emit_layernorm_tile(P, x_ap, out_ap, width, sc, eps=1e-5):
    P.red("v", sc["mean"], x_ap, ALU.add, negate=True)
    P.ts("v", sc["mean"], sc["mean"], 1.0 / width, None, ALU.mult)
    P.act(sc["xc"], x_ap, AF.Identity, bias=sc["mean"], e="s")
    P.act(sc["sq"], sc["xc"], AF.Square, accum_out=sc["var"], e="s")
    P.act(sc["var"], sc["var"], AF.Sqrt, bias=sc["eps"], scale=1.0 / width, e="s")
    P.op("v", lambda eng, o_=_asap(sc["var"]): eng.reciprocal(o_, o_),
         reads=[_apkey(_asap(sc["var"]))], writes=[_apkey(_asap(sc["var"]))])
    P.ts("v", out_ap, sc["xc"], sc["var"], None, ALU.mult,
         rk=[_apkey(sc["xc"]), _apkey(sc["var"])], wk=[_apkey(out_ap)])


def sb_tiles(nc, es, name, n, shape, dtype=F32):
    return [es.enter_context(nc.sbuf_tensor(f"{name}{i}", shape, dtype)) for i in range(n)]


def quant_codes(P, e, out, x, scale, lo, hi, sc):
    if isinstance(scale, bass.AP):
        P.ts(e, out, x, scale, None, ALU.mult)
    else:
        P.ts(e, out, x, float(scale), None, ALU.mult)
    round_nearest(P, e, out, sc)
    P.ts(e, out, out, float(hi), float(lo), ALU.min, ALU.max)


def bcast_row_to_partitions(P, psum_out, ones_col, row_ap, rk, wk):
    P.mm(psum_out, ones_col, row_ap, True, True, rk=rk + [_apkey(ones_col)], wk=wk)



def _f32c(a):
    if os.environ.get("BASS_SIM_FP64") == "1":
        return np.asarray(a)
    return np.ascontiguousarray(np.asarray(a), dtype=np.float32)


def _shards_tm(x_flat):
    return [np.ascontiguousarray(x_flat[c * TOK:(c + 1) * TOK]) for c in range(NCORES)]


def _fm_batches(x_flat):
    xb = x_flat.reshape(B, S, D)
    fms = [np.ascontiguousarray(xb[b].T) for b in range(B)]
    return [fms[c // (NCORES // B)] for c in range(NCORES)]


def _tm_batches(x_flat):
    xb = x_flat.reshape(B, S, D)
    tms = [np.ascontiguousarray(xb[b]) for b in range(B)]
    return [tms[c // (NCORES // B)] for c in range(NCORES)]


_IDENT32 = np.eye(128, dtype=np.float32)


def _launch(build_fn, in_maps, out_keys):
    nc = bass.Bass("TRN2", target_bir_lowering=False, debug=False)
    es = contextlib.ExitStack()
    with es:
        P = Prog(nc, es)
        build_fn(nc, P, es)
        P.build_block()
    res = run_bass_kernel_spmd(nc, in_maps, core_ids=list(range(NCORES)))
    return [{k: r[k] for k in out_keys} for r in res.results]



def build_l1f(nc, P, es):
    x_d = nc.dram_tensor("x", [TOK, D], F32, kind="ExternalInput").ap()
    w1t_d = nc.dram_tensor("w1t", [D, 4096], F32, kind="ExternalInput").ap()
    b1c_d = nc.dram_tensor("b1c", [128, 32], F32, kind="ExternalInput").ap()
    w2t_d = nc.dram_tensor("w2t", [4096, D], F32, kind="ExternalInput").ap()
    b2r_d = nc.dram_tensor("b2r", [1, D], F32, kind="ExternalInput").ap()
    idf_d = nc.dram_tensor("identf", [128, 128], F32, kind="ExternalInput").ap()
    xo_d = nc.dram_tensor("xo", [TOK, D], F32, kind="ExternalOutput").ap()
    w2c_d = nc.dram_tensor("w2codes", [32, 128, D], BF16, kind="Internal").ap()

    sb = lambda n, shp, dt=F32: es.enter_context(nc.sbuf_tensor(n, shp, dt))
    w1b = sb_tiles(nc, es, "w1b", 8, [128, 4096], BF16)
    stage = sb("stage", [128, 1024])
    qsc = {"i32": es.enter_context(nc.sbuf_tensor("q_i32", [128, 1024], I32)),
           "t": sb("q_t", [128, 1024]), "a": sb("q_a", [128, 1024]), "b": sb("q_b", [128, 1024])}
    idf = sb("idf", [128, 128])
    b1c = sb("b1c_s", [128, 32])
    b2r = sb("b2r_s", [1, 1024])
    b2b = sb("b2b", [1, 1024], BF16)
    sixteen = sb("sixteen", [1, 128], BF16)
    xt = sb_tiles(nc, es, "xt", 4, [128, 1024])
    xq = sb("xq", [128, 1024])
    xqf = sb_tiles(nc, es, "xqf", 8, [128, 512], BF16)
    hq = sb_tiles(nc, es, "hq", 32, [128, 512], BF16)
    w2b = sb_tiles(nc, es, "w2b", 4, [128, 1024], BF16)
    hev = sb_tiles(nc, es, "hev", 2, [128, 512])
    qsc2 = [{"i32": es.enter_context(nc.sbuf_tensor(f"q2_i32_{p}", [128, 512], I32)),
             "t": sb(f"q2_t_{p}", [128, 512]), "a": sb(f"q2_a_{p}", [128, 512]),
             "b": sb(f"q2_b_{p}", [128, 512])} for p in range(2)]
    ln_sc = {"mean": sb("ln_mean", [128, 1]), "var": sb("ln_var", [128, 1]),
             "xc": sb("ln_xc", [128, 1024]), "sq": sb("ln_sq", [128, 1024]),
             "eps": sb("ln_eps", [128, 1])}
    P.memset("v", ln_sc["eps"], 1e-5)
    res = sb("res", [128, 1024])
    outt = sb("outt", [128, 1024])
    ps = [es.enter_context(nc.psum_tensor(f"ps{i}", [128, 512], F32)) for i in range(8)]

    P.dma("y", idf[:], idf_d)
    P.memset("v", sixteen[:], 16.0)

    def wquant(dst_ap, src_dram_ap, e="v"):
        P.dma("y", stage[:], src_dram_ap)
        quant_codes(P, e, xq, stage[:], 64.0, -128.0, 127.0, qsc)
        P.cp(e, dst_ap, xq[:])

    for j in range(8):
        for a in range(4):
            wquant(w1b[j][:, a * 1024:(a + 1) * 1024], w1t_d[j * 128:(j + 1) * 128, a * 1024:(a + 1) * 1024])
    for j in range(32):
        wquant(w2b[0][:, :], w2t_d[j * 128:(j + 1) * 128, :])
        P.dma("g", w2c_d[j], w2b[0][:, :])
    P.dma("y", b1c[:], b1c_d)
    quant_codes(P, "v", b1c[:], b1c[:], 64.0, -128.0, 127.0,
                {k: v[:, 0:32] for k, v in qsc.items()})
    P.ts("v", b1c[:], b1c[:], 0.25, None, ALU.mult)
    P.dma("y", b2r[:], b2r_d)
    quant_codes(P, "v", b2r[:], b2r[:], 64.0, -128.0, 127.0,
                {k: v[0:1, :] for k, v in qsc.items()})
    P.cp("v", b2b[:], b2r[:])

    for tb in range(4):
        for i in range(4):
            ti = tb * 4 + i
            P.dma("y", xt[i][:], x_d[ti * 128:(ti + 1) * 128, :])
            quant_codes(P, "v", xq, xt[i][:], 16.0, -128.0, 127.0, qsc)
            for j in range(8):
                P.transpose(ps[0][:, 0:128], xq[:, j * 128:(j + 1) * 128], idf[:])
                P.cp("s", xqf[j][:, i * 128:(i + 1) * 128], ps[0][:, 0:128])
        for f in range(32):
            pb = ps[1 + f % 2]
            hb = hev[0 if f % 2 == 0 else 1]
            for dj in range(8):
                P.mm(pb[:], w1b[dj][:, f * 128:(f + 1) * 128], xqf[dj][:],
                     start=(dj == 0), stop=(dj == 7))
            P.act(hb[:], pb[:], AF.Identity, bias=b1c[:, f:f + 1], scale=1.0 / 64.0, e="s")
            round_nearest(P, "v", hb[:], qsc2[f % 2])
            P.ts("v", hb[:], hb[:], 127.0, 0.0, ALU.min, ALU.max)
            P.cp("s", hq[f][:], hb[:])
        for f in range(4):
            P.dma("y", w2b[f][:], w2c_d[f])
        for f in range(32):
            buf = w2b[f % 4]
            for i in range(4):
                for ob in range(2):
                    P.mm(ps[i * 2 + ob][:], hq[f][:, i * 128:(i + 1) * 128],
                         buf[:, ob * 512:(ob + 1) * 512],
                         start=(f == 0), stop=False)
            if f + 4 < 32:
                P.dma("y", w2b[(f + 4) % 4][:], w2c_d[f + 4])
        for i in range(4):
            ti = tb * 4 + i
            for ob in range(2):
                P.mm(ps[i * 2 + ob][:], sixteen[:], b2b[:, ob * 512:(ob + 1) * 512],
                     start=False, stop=True)
                P.stt("v", res[:, ob * 512:(ob + 1) * 512], ps[i * 2 + ob][:],
                      1.0 / 1024.0, xt[i][:, ob * 512:(ob + 1) * 512],
                      ALU.mult, ALU.add)
            emit_layernorm_tile(P, res[:], outt[:], D, ln_sc)
            P.dma("g", xo_d[ti * 128:(ti + 1) * 128, :], outt[:])


def _run_l1f(x_shards, w1, b1, w2, b2):
    w1t = np.ascontiguousarray(w1.T)
    b1c = np.ascontiguousarray(b1.reshape(32, 128).T)
    w2t = np.ascontiguousarray(w2.T)
    b2r = np.ascontiguousarray(b2.reshape(1, D))
    in_maps = [{"x": x_shards[c], "w1t": w1t, "b1c": b1c, "w2t": w2t,
                "b2r": b2r, "identf": _IDENT32} for c in range(NCORES)]
    return _launch(build_l1f, in_maps, ["xo"])



def build_l0f(nc, P, es):
    x_d = nc.dram_tensor("x", [TOK, D], F32, kind="ExternalInput").ap()
    wgt_d = nc.dram_tensor("wgt", [D, 4096], F32, kind="ExternalInput").ap()
    wut_d = nc.dram_tensor("wut", [D, 4096], F32, kind="ExternalInput").ap()
    wdt_d = nc.dram_tensor("wdt", [4096, D], F32, kind="ExternalInput").ap()
    idf_d = nc.dram_tensor("identf", [128, 128], F32, kind="ExternalInput").ap()
    xo_d = nc.dram_tensor("xo", [TOK, D], F32, kind="ExternalOutput").ap()
    wgc_d = nc.dram_tensor("wgc", [8, 128, 4096], BF16, kind="Internal").ap()
    wuc_d = nc.dram_tensor("wuc", [8, 128, 4096], BF16, kind="Internal").ap()
    wdc_d = nc.dram_tensor("wdc", [32, 128, D], BF16, kind="Internal").ap()

    sb = lambda n, shp, dt=F32: es.enter_context(nc.sbuf_tensor(n, shp, dt))
    stage = sb("stage", [128, 2048])
    qsc = {"i32": es.enter_context(nc.sbuf_tensor("q_i32", [128, 1024], I32)),
           "t": sb("q_t", [128, 1024]), "a": sb("q_a", [128, 1024]), "b": sb("q_b", [128, 1024])}
    codes = sb("codes", [128, 1024])
    cbuf = sb("cbuf", [128, 1024], BF16)
    idf = sb("idf", [128, 128])
    onecol = sb("onecol", [128, 1])
    onerow = sb("onerow", [1, 128])
    acc1 = sb("acc1", [128, 1])
    red1 = sb("red1", [128, 1])
    scal = sb("scal", [1, 8])
    swcol = sb("swcol", [128, 3])
    swq = sb("swq", [128, 1])
    ps = [es.enter_context(nc.psum_tensor(f"ps{i}", [128, 512], F32)) for i in range(8)]
    ps_s = ps[0]
    P.dma("y", idf[:], idf_d)
    P.memset("v", onecol[:], 1.0)
    P.memset("v", onerow[:], 1.0)

    def weight_scale(src_c, nchunk, width, idx):
        P.memset("v", acc1[:], 0.0)
        for j in range(nchunk):
            for w0 in range(0, width, 2048):
                ww = min(2048, width - w0)
                P.dma("y", stage[:, 0:ww], src_c[j][:, w0:w0 + ww])
                P.red("v", red1[:], stage[:, 0:ww], ALU.add, absval=True)
                P.tt("v", acc1[:], acc1[:], red1[:], ALU.add)
        P.mm(ps_s[0:1, 0:1], onecol[:], acc1[:], True, True)
        P.ts("v", scal[0:1, idx:idx + 1], ps_s[0:1, 0:1],
             1.0 / (nchunk * 128 * width), 1e-5, ALU.mult, ALU.max)
        P.op("v", lambda eng: eng.reciprocal(scal[0:1, idx + 4:idx + 5], scal[0:1, idx:idx + 1]),
             reads=["scal"], writes=["scal"])
        P.mm(ps_s[:, 0:1], onerow[:], scal[0:1, idx:idx + 1], True, True)
        P.cp("s", swcol[:, idx:idx + 1], ps_s[:, 0:1])

    def weight_quant(src_c, dst_d, nchunk, width, idx):
        P.mm(ps_s[:, 0:1], onerow[:], scal[0:1, idx + 4:idx + 5], True, True)
        P.cp("s", swq[:], ps_s[:, 0:1])
        for j in range(nchunk):
            for w0 in range(0, width, 2048):
                ww = min(2048, width - w0)
                P.dma("y", stage[:, 0:ww], src_c[j][:, w0:w0 + ww])
                for sl in range(ww // 1024):
                    c = codes[:, 0:1024]
                    P.ts("v", c, stage[:, sl * 1024:(sl + 1) * 1024], swq[:], None, ALU.mult)
                    round_nearest(P, "v", c, qsc)
                    P.ts("v", c, c, 1.0, -1.0, ALU.min, ALU.max)
                    P.cp("v", cbuf[:, 0:1024], c)
                    P.dma("g", dst_d[j][:, w0 + sl * 1024:w0 + (sl + 1) * 1024], cbuf[:, 0:1024])

    wgt_c = wgt_d.rearrange("(c p) f -> c p f", p=128)
    wut_c = wut_d.rearrange("(c p) f -> c p f", p=128)
    wdt_c = wdt_d.rearrange("(c p) f -> c p f", p=128)
    weight_scale(wgt_c, 8, 4096, 0)
    weight_scale(wut_c, 8, 4096, 1)
    weight_scale(wdt_c, 32, 1024, 2)
    weight_quant(wgt_c, wgc_d, 8, 4096, 0)
    weight_quant(wut_c, wuc_d, 8, 4096, 1)
    weight_quant(wdt_c, wdc_d, 32, 1024, 2)

    xt = sb_tiles(nc, es, "xt", 2, [128, 1024])
    sxg = sb_tiles(nc, es, "sxg", 2, [128, 1])
    sxu = sb_tiles(nc, es, "sxu", 2, [128, 1])
    shd = sb_tiles(nc, es, "shd", 2, [128, 1])
    xqf = sb_tiles(nc, es, "xqf", 8, [128, 256], BF16)
    ht = sb_tiles(nc, es, "ht", 2, [128, 4096])
    hqf = sb_tiles(nc, es, "hqf", 32, [128, 256], BF16)
    wgb = sb_tiles(nc, es, "wgb", 2, [128, 4096], BF16)
    wub = sb_tiles(nc, es, "wub", 2, [128, 4096], BF16)
    wdb = sb_tiles(nc, es, "wdb", 4, [128, 1024], BF16)
    gev = sb_tiles(nc, es, "gev", 2, [128, 512])
    uev = sb_tiles(nc, es, "uev", 2, [128, 512])
    ln_sc = {"mean": sb("ln_mean", [128, 1]), "var": sb("ln_var", [128, 1]),
             "xc": sb("ln_xc", [128, 1024]), "sq": sb("ln_sq", [128, 1024]),
             "eps": sb("ln_eps", [128, 1])}
    P.memset("v", ln_sc["eps"], 1e-5)
    res = sb("res", [128, 1024])
    outt = sb("outt", [128, 1024])

    for tb in range(8):
        for i in range(2):
            ti = tb * 2 + i
            P.dma("y", xt[i][:], x_d[ti * 128:(ti + 1) * 128, :])
            P.red("v", red1[:], xt[i][:], ALU.max, absval=True)
            P.ts("v", red1[:], red1[:], 1e-5, 1.0 / 127.0, ALU.max, ALU.mult)
            P.ts("v", sxg[i][:], red1[:], swcol[:, 0:1], None, ALU.mult)
            P.ts("v", sxu[i][:], red1[:], swcol[:, 1:2], None, ALU.mult)
            P.op("v", lambda eng: eng.reciprocal(acc1[:], red1[:]),
                 reads=["red1"], writes=["acc1"])
            P.ts("v", codes[:], xt[i][:], acc1[:], None, ALU.mult)
            round_nearest(P, "v", codes[:], qsc)
            P.ts("v", codes[:], codes[:], 127.0, -128.0, ALU.min, ALU.max)
            for j in range(8):
                P.transpose(ps[6][:, 0:128], codes[:, j * 128:(j + 1) * 128], idf[:])
                P.cp("s", xqf[j][:, i * 128:(i + 1) * 128], ps[6][:, 0:128])
        for fb in range(8):
            P.dma("y", wgb[fb % 2][:].rearrange("p (c f) -> p c f", c=8),
                  wgc_d[:, :, fb * 512:(fb + 1) * 512].rearrange("c p f -> p c f"))
            P.dma("g", wub[fb % 2][:].rearrange("p (c f) -> p c f", c=8),
                  wuc_d[:, :, fb * 512:(fb + 1) * 512].rearrange("c p f -> p c f"))
            for i in range(2):
                pg = ps[i % 2]
                pu = ps[2 + i % 2]
                for dj in range(8):
                    P.mm(pg[:], xqf[dj][:, i * 128:(i + 1) * 128],
                         wgb[fb % 2][:, dj * 512:(dj + 1) * 512],
                         start=(dj == 0), stop=(dj == 7))
                for dj in range(8):
                    P.mm(pu[:], xqf[dj][:, i * 128:(i + 1) * 128],
                         wub[fb % 2][:, dj * 512:(dj + 1) * 512],
                         start=(dj == 0), stop=(dj == 7))
                g_, u_ = gev[i % 2], uev[i % 2]
                P.act(g_[:], pg[:], AF.Silu, scale=sxg[i][:], e="s")
                P.act(u_[:], pu[:], AF.Identity, scale=sxu[i][:], e="s")
                P.tt("v", ht[i][:, fb * 512:(fb + 1) * 512], g_[:], u_[:], ALU.mult)
        for i in range(2):
            P.red("v", red1[:], ht[i][:], ALU.max, absval=True)
            P.ts("v", red1[:], red1[:], 1e-5, 1.0 / 127.0, ALU.max, ALU.mult)
            P.ts("v", shd[i][:], red1[:], swcol[:, 2:3], None, ALU.mult)
            P.op("v", lambda eng: eng.reciprocal(acc1[:], red1[:]),
                 reads=["red1"], writes=["acc1"])
            for sl in range(4):
                c = codes[:, 0:1024]
                P.ts("v", c, ht[i][:, sl * 1024:(sl + 1) * 1024], acc1[:], None, ALU.mult)
                round_nearest(P, "v", c, qsc)
                P.ts("v", c, c, 127.0, -128.0, ALU.min, ALU.max)
                for j in range(8):
                    fi = sl * 8 + j
                    P.transpose(ps[6][:, 0:128], c[:, j * 128:(j + 1) * 128], idf[:])
                    P.cp("s", hqf[fi][:, i * 128:(i + 1) * 128], ps[6][:, 0:128])
        for f in range(4):
            P.dma("y", wdb[f][:], wdc_d[f])
        for f in range(32):
            buf = wdb[f % 4]
            for i in range(2):
                for ob in range(2):
                    P.mm(ps[i * 2 + ob][:], hqf[f][:, i * 128:(i + 1) * 128],
                         buf[:, ob * 512:(ob + 1) * 512],
                         start=(f == 0), stop=(f == 31))
            if f + 4 < 32:
                P.dma("y", wdb[(f + 4) % 4][:], wdc_d[f + 4])
        for i in range(2):
            for ob in range(2):
                P.act(res[:, ob * 512:(ob + 1) * 512], ps[i * 2 + ob][:], AF.Identity,
                      scale=shd[i][:], e="s")
            P.tt("v", res[:], res[:], xt[i][:], ALU.add)
            emit_layernorm_tile(P, res[:], outt[:], D, ln_sc)
            P.dma("g", xo_d[(tb * 2 + i) * 128:(tb * 2 + i + 1) * 128, :], outt[:])


def _run_l0f(x_shards, w_gate, w_up, w_down):
    wgt = np.ascontiguousarray(w_gate.T)
    wut = np.ascontiguousarray(w_up.T)
    wdt = np.ascontiguousarray(w_down.T)
    in_maps = [{"x": x_shards[c], "wgt": wgt, "wut": wut, "wdt": wdt,
                "identf": _IDENT32} for c in range(NCORES)]
    return _launch(build_l0f, in_maps, ["xo"])



def build_l2f_m(nc, P, es):
    x_d = nc.dram_tensor("x", [TOK, D], F32, kind="ExternalInput").ap()
    idf_d = nc.dram_tensor("identf", [128, 128], F32, kind="ExternalInput").ap()
    mx_d = nc.dram_tensor("mx", [1, 1], F32, kind="ExternalOutput").ap()
    sb = lambda n, shp, dt=F32: es.enter_context(nc.sbuf_tensor(n, shp, dt))
    xt = sb_tiles(nc, es, "xt", 2, [128, 1024])
    acc = sb("acc", [128, 1])
    red = sb("red", [128, 1])
    idf = sb("idf", [128, 128])
    row = sb("row", [1, 128])
    m1 = sb("m1", [1, 1])
    pst = es.enter_context(nc.psum_tensor("pst", [128, 128], F32))
    P.dma("y", idf[:], idf_d)
    P.memset("v", acc[:], 0.0)
    for i in range(NT):
        P.dma("y", xt[i % 2][:], x_d[i * 128:(i + 1) * 128, :])
        P.red("v", red[:], xt[i % 2][:], ALU.max, absval=True)
        P.tt("v", acc[:], acc[:], red[:], ALU.max)
    P.transpose(pst[0:1, 0:128], acc[:], idf[:])
    P.cp("s", row[:], pst[0:1, 0:128], rk=["pst"], wk=["row"])
    P.red("v", m1[:], row[:], ALU.max)
    P.dma("g", mx_d, m1[:])


def _scalar_bcast_col(P, ps, onerow, src11, dst_col):
    P.mm(ps[:, 0:1], onerow[:], src11, True, True)
    P.cp("s", dst_col, ps[:, 0:1])


def build_l2f_a(nc, P, es):
    x_d = nc.dram_tensor("x", [TOK, D], F32, kind="ExternalInput").ap()
    mx_d = nc.dram_tensor("mxall", [1, NCORES], F32, kind="ExternalInput").ap()
    clipx_d = nc.dram_tensor("clipx", [1, 1], F32, kind="ExternalInput").ap()
    wgn_d = nc.dram_tensor("wgn", [2816, D], F32, kind="ExternalInput").ap()
    wun_d = nc.dram_tensor("wun", [2816, D], F32, kind="ExternalInput").ap()
    idf_d = nc.dram_tensor("identf", [128, 128], F32, kind="ExternalInput").ap()
    h_d = nc.dram_tensor("h", [TOK, 2816], F32, kind="ExternalOutput").ap()
    mh_d = nc.dram_tensor("mh", [1, 1], F32, kind="ExternalOutput").ap()

    sb = lambda n, shp, dt=F32: es.enter_context(nc.sbuf_tensor(n, shp, dt))
    idf = sb("idf", [128, 128])
    onerow = sb("onerow", [1, 128])
    qsc = {"i32": es.enter_context(nc.sbuf_tensor("q_i32", [128, 1024], I32)),
           "t": sb("q_t", [128, 1024]), "a": sb("q_a", [128, 1024]), "b": sb("q_b", [128, 1024])}
    stage = sb("stage", [128, 1024])
    codes = sb("codes", [128, 1024])
    swc = sb("swc", [128, 1])
    s11 = sb("s11", [1, 8])
    invx_c = sb("invx_c", [128, 1])
    wgqT = sb_tiles(nc, es, "wgqT", 8, [128, 2816], BF16)
    wuqT = sb_tiles(nc, es, "wuqT", 8, [128, 2816], BF16)
    gsrow = sb("gsrow", [1, 2816])
    usrow = sb("usrow", [1, 2816])
    gsb = sb("gsb", [128, 2816])
    usb = sb("usb", [128, 2816])
    xt = sb_tiles(nc, es, "xt", 2, [128, 1024])
    xqf = sb_tiles(nc, es, "xqf", 8, [128, 256], BF16)
    gev = sb_tiles(nc, es, "gev", 2, [128, 512])
    uev = sb_tiles(nc, es, "uev", 2, [128, 512])
    hrow = sb_tiles(nc, es, "hrow", 2, [128, 2816])
    macc = sb("macc", [128, 1])
    red = sb("red", [128, 1])
    mrow = sb("mrow", [1, 128])
    m1 = sb("m1", [1, 1])
    ps = [es.enter_context(nc.psum_tensor(f"ps{i}", [128, 512], F32)) for i in range(8)]

    P.dma("y", idf[:], idf_d)
    P.memset("v", onerow[:], 1.0)
    P.memset("v", macc[:], 0.0)
    P.dma("y", stage[0:1, 0:NCORES], mx_d)
    P.red("v", s11[0:1, 0:1], stage[0:1, 0:NCORES], ALU.max, rk=["stage"], wk=["s11"])
    P.dma("y", stage[0:1, 8:9], clipx_d)
    P.act(s11[0:1, 1:2], stage[0:1, 8:9], AF.Sigmoid, rk=["stage"], wk=["s11"])
    P.tt("v", s11[0:1, 2:3], s11[0:1, 0:1], s11[0:1, 1:2], ALU.mult)
    P.ts("v", s11[0:1, 2:3], s11[0:1, 2:3], 1.0 / 127.0, None, ALU.mult)
    P.op("v", lambda eng: eng.reciprocal(s11[0:1, 3:4], s11[0:1, 2:3]),
         reads=["s11"], writes=["s11"])
    _scalar_bcast_col(P, ps[0], onerow, s11[0:1, 3:4], invx_c[:])

    def wquant_T(src_d, dstT, srow):
        for c in range(22):
            P.dma("y", stage[:], src_d[c * 128:(c + 1) * 128, :])
            P.red("v", red[:], stage[:], ALU.max, absval=True)
            P.ts("v", swc[:], red[:], 1e-30, 1.0 / 127.0, ALU.max, ALU.mult)
            P.op("v", lambda eng: eng.reciprocal(red[:], swc[:]), reads=["swc"], writes=["red"])
            P.ts("v", codes[:], stage[:], red[:], None, ALU.mult)
            round_nearest(P, "v", codes[:], qsc)
            P.ts("v", codes[:], codes[:], 127.0, -128.0, ALU.min, ALU.max)
            for j in range(8):
                P.transpose(ps[1][:, 0:128], codes[:, j * 128:(j + 1) * 128], idf[:])
                P.cp("s", dstT[j][:, c * 128:(c + 1) * 128], ps[1][:, 0:128])
            P.transpose(ps[2][0:1, 0:128], swc[:], idf[:])
            P.cp("s", srow[0:1, c * 128:(c + 1) * 128], ps[2][0:1, 0:128],
                 rk=["ps2"], wk=[_apkey(srow.ap())])

    wquant_T(wgn_d, wgqT, gsrow)
    wquant_T(wun_d, wuqT, usrow)
    for srow, dstb in ((gsrow, gsb), (usrow, usb)):
        P.ts("v", srow[:], srow[:], s11[0:1, 2:3], None, ALU.mult)
        for fb in range(6):
            w = min(512, 2816 - fb * 512)
            P.mm(ps[3][:, 0:w], onerow[:], srow[0:1, fb * 512:fb * 512 + w], True, True)
            P.cp("s", dstb[:, fb * 512:fb * 512 + w], ps[3][:, 0:w],
                 rk=["ps3"], wk=[_apkey(dstb.ap())])

    for tb in range(8):
        for i in range(2):
            ti = tb * 2 + i
            P.dma("y", xt[i][:], x_d[ti * 128:(ti + 1) * 128, :])
            P.ts("v", codes[:], xt[i][:], invx_c[:], None, ALU.mult)
            round_nearest(P, "v", codes[:], qsc)
            P.ts("v", codes[:], codes[:], 127.0, -128.0, ALU.min, ALU.max)
            for j in range(8):
                P.transpose(ps[1][:, 0:128], codes[:, j * 128:(j + 1) * 128], idf[:])
                P.cp("s", xqf[j][:, i * 128:(i + 1) * 128], ps[1][:, 0:128])
        for i in range(2):
            hr = hrow[i]
            for fb in range(6):
                w = min(512, 2816 - fb * 512)
                pg = ps[4 + (fb % 2)]
                pu = ps[6 + (fb % 2)]
                for dj in range(8):
                    P.mm(pg[:, 0:w], xqf[dj][:, i * 128:(i + 1) * 128],
                         wgqT[dj][:, fb * 512:fb * 512 + w], start=(dj == 0), stop=(dj == 7))
                for dj in range(8):
                    P.mm(pu[:, 0:w], xqf[dj][:, i * 128:(i + 1) * 128],
                         wuqT[dj][:, fb * 512:fb * 512 + w], start=(dj == 0), stop=(dj == 7))
                g_, u_ = gev[fb % 2], uev[fb % 2]
                P.tt("v", g_[:, 0:w], pg[:, 0:w], gsb[:, fb * 512:fb * 512 + w], ALU.mult)
                P.act(g_[:, 0:w], g_[:, 0:w], AF.Silu, e="s")
                P.tt("v", u_[:, 0:w], pu[:, 0:w], usb[:, fb * 512:fb * 512 + w], ALU.mult)
                P.tt("v", hr[:, fb * 512:fb * 512 + w], g_[:, 0:w], u_[:, 0:w], ALU.mult)
            ti = tb * 2 + i
            P.dma("g", h_d[ti * 128:(ti + 1) * 128, :], hr[:])
            P.red("v", red[:], hr[:], ALU.max, absval=True)
            P.tt("v", macc[:], macc[:], red[:], ALU.max)
    P.transpose(ps[0][0:1, 0:128], macc[:], idf[:])
    P.cp("s", mrow[:], ps[0][0:1, 0:128], rk=["ps0"], wk=["mrow"])
    P.red("v", m1[:], mrow[:], ALU.max)
    P.dma("g", mh_d, m1[:])


def build_l2f_b(nc, P, es):
    x_d = nc.dram_tensor("x", [TOK, D], F32, kind="ExternalInput").ap()
    h_d = nc.dram_tensor("h", [TOK, 2816], F32, kind="ExternalInput").ap()
    mh_d = nc.dram_tensor("mhall", [1, NCORES], F32, kind="ExternalInput").ap()
    clipd_d = nc.dram_tensor("clipd", [1, 1], F32, kind="ExternalInput").ap()
    wdn_d = nc.dram_tensor("wdn", [D, 2816], F32, kind="ExternalInput").ap()
    idf_d = nc.dram_tensor("identf", [128, 128], F32, kind="ExternalInput").ap()
    xo_d = nc.dram_tensor("xo", [TOK, D], F32, kind="ExternalOutput").ap()

    sb = lambda n, shp, dt=F32: es.enter_context(nc.sbuf_tensor(n, shp, dt))
    idf = sb("idf", [128, 128])
    onerow = sb("onerow", [1, 128])
    qsc = {"i32": es.enter_context(nc.sbuf_tensor("q_i32", [128, 1024], I32)),
           "t": sb("q_t", [128, 1024]), "a": sb("q_a", [128, 1024]), "b": sb("q_b", [128, 1024])}
    stage = sb("stage", [128, 2816])
    codes = sb("codes", [128, 2816])
    swc = sb("swc", [128, 1])
    s11 = sb("s11", [1, 8])
    invh_c = sb("invh_c", [128, 1])
    wdqT = sb_tiles(nc, es, "wdqT", 22, [128, 1024], BF16)
    dsrow = sb("dsrow", [1, 1024])
    dsb = sb("dsb", [128, 1024])
    xt = sb("xt", [128, 1024])
    ht = sb("ht", [128, 2816])
    hqf = sb("hqf", [128, 2816], BF16)
    red = sb("red", [128, 1])
    res = sb("res", [128, 1024])
    outt = sb("outt", [128, 1024])
    ln_sc = {"mean": sb("ln_mean", [128, 1]), "var": sb("ln_var", [128, 1]),
             "xc": sb("ln_xc", [128, 1024]), "sq": sb("ln_sq", [128, 1024]),
             "eps": sb("ln_eps", [128, 1])}
    P.memset("v", ln_sc["eps"], 1e-5)
    ps = [es.enter_context(nc.psum_tensor(f"ps{i}", [128, 512], F32)) for i in range(6)]

    P.dma("y", idf[:], idf_d)
    P.memset("v", onerow[:], 1.0)
    P.dma("y", stage[0:1, 0:NCORES], mh_d)
    P.red("v", s11[0:1, 0:1], stage[0:1, 0:NCORES], ALU.max, rk=["stage"], wk=["s11"])
    P.dma("y", stage[0:1, 8:9], clipd_d)
    P.act(s11[0:1, 1:2], stage[0:1, 8:9], AF.Sigmoid, rk=["stage"], wk=["s11"])
    P.tt("v", s11[0:1, 2:3], s11[0:1, 0:1], s11[0:1, 1:2], ALU.mult)
    P.ts("v", s11[0:1, 2:3], s11[0:1, 2:3], 1.0 / 127.0, None, ALU.mult)
    P.op("v", lambda eng: eng.reciprocal(s11[0:1, 3:4], s11[0:1, 2:3]),
         reads=["s11"], writes=["s11"])
    _scalar_bcast_col(P, ps[0], onerow, s11[0:1, 3:4], invh_c[:])

    for c in range(8):
        P.dma("y", stage[:], wdn_d[c * 128:(c + 1) * 128, :])
        P.red("v", red[:], stage[:], ALU.max, absval=True)
        P.ts("v", swc[:], red[:], 1e-30, 1.0 / 127.0, ALU.max, ALU.mult)
        P.op("v", lambda eng: eng.reciprocal(red[:], swc[:]), reads=["swc"], writes=["red"])
        for sl in range(3):
            w = min(1024, 2816 - sl * 1024)
            P.ts("v", codes[:, sl * 1024:sl * 1024 + w], stage[:, sl * 1024:sl * 1024 + w],
                 red[:], None, ALU.mult)
            round_nearest(P, "v", codes[:, sl * 1024:sl * 1024 + w],
                          {k: _asap(v)[:, 0:w] for k, v in qsc.items()})
            P.ts("v", codes[:, sl * 1024:sl * 1024 + w], codes[:, sl * 1024:sl * 1024 + w],
                 127.0, -128.0, ALU.min, ALU.max)
        for j in range(22):
            P.transpose(ps[1][:, 0:128], codes[:, j * 128:(j + 1) * 128], idf[:])
            P.cp("s", wdqT[j][:, c * 128:(c + 1) * 128], ps[1][:, 0:128])
        P.transpose(ps[2][0:1, 0:128], swc[:], idf[:])
        P.cp("s", dsrow[0:1, c * 128:(c + 1) * 128], ps[2][0:1, 0:128],
             rk=["ps2"], wk=["dsrow"])
    P.ts("v", dsrow[:], dsrow[:], s11[0:1, 2:3], None, ALU.mult)
    for ob in range(2):
        P.mm(ps[3][:], onerow[:], dsrow[0:1, ob * 512:(ob + 1) * 512], True, True)
        P.cp("s", dsb[:, ob * 512:(ob + 1) * 512], ps[3][:], rk=["ps3"], wk=["dsb"])

    for ti in range(NT):
        P.dma("y", ht[:], h_d[ti * 128:(ti + 1) * 128, :])
        for sl in range(3):
            w = min(1024, 2816 - sl * 1024)
            P.ts("v", codes[:, sl * 1024:sl * 1024 + w], ht[:, sl * 1024:sl * 1024 + w],
                 invh_c[:], None, ALU.mult)
            round_nearest(P, "v", codes[:, sl * 1024:sl * 1024 + w],
                          {k: _asap(v)[:, 0:w] for k, v in qsc.items()})
            P.ts("v", codes[:, sl * 1024:sl * 1024 + w], codes[:, sl * 1024:sl * 1024 + w],
                 127.0, -128.0, ALU.min, ALU.max)
        for c in range(22):
            P.transpose(ps[1][:, 0:128], codes[:, c * 128:(c + 1) * 128], idf[:])
            P.cp("s", hqf[:, c * 128:(c + 1) * 128], ps[1][:, 0:128])
        P.dma("y", xt[:], x_d[ti * 128:(ti + 1) * 128, :])
        for ob in range(2):
            pd = ps[4 + ob]
            for c in range(22):
                P.mm(pd[:], hqf[:, c * 128:(c + 1) * 128],
                     wdqT[c][:, ob * 512:(ob + 1) * 512], start=(c == 0), stop=(c == 21))
            P.tt("v", res[:, ob * 512:(ob + 1) * 512], pd[:], dsb[:, ob * 512:(ob + 1) * 512],
                 ALU.mult)
        P.tt("v", res[:], res[:], xt[:], ALU.add)
        emit_layernorm_tile(P, res[:], outt[:], D, ln_sc)
        P.dma("g", xo_d[ti * 128:(ti + 1) * 128, :], outt[:])


def _run_l2f(x_shards, clip_x, clip_down, w_gate, w_up, w_down):
    r1 = _launch(build_l2f_m, [{"x": x_shards[c], "identf": _IDENT32} for c in range(NCORES)], ["mx"])
    mxall = np.concatenate([r["mx"] for r in r1], 1)
    cx = np.ascontiguousarray(clip_x.reshape(1, 1))
    cd = np.ascontiguousarray(clip_down.reshape(1, 1))
    in2 = [{"x": x_shards[c], "mxall": mxall, "clipx": cx, "wgn": w_gate,
            "wun": w_up, "identf": _IDENT32} for c in range(NCORES)]
    r2 = _launch(build_l2f_a, in2, ["h", "mh"])
    mhall = np.concatenate([r["mh"] for r in r2], 1)
    in3 = [{"x": x_shards[c], "h": r2[c]["h"], "mhall": mhall, "clipd": cd,
            "wdn": w_down, "identf": _IDENT32} for c in range(NCORES)]
    r3 = _launch(build_l2f_b, in3, ["xo"])
    return r3



def emit_bias_hilo(P, sb, name, src_row_d, width):
    nc = P.nc
    es = P.es
    if not hasattr(P, "_bh_scratch") or P._bh_w < width:
        P._bh_scratch = (es.enter_context(nc.sbuf_tensor(f"bh32_{name}", [1, width], F32)),
                         es.enter_context(nc.sbuf_tensor(f"bhh32_{name}", [1, width], F32)))
        P._bh_w = width
    b32, h32 = P._bh_scratch
    hi = es.enter_context(nc.sbuf_tensor(name + "_hi", [1, width], BF16))
    lo = es.enter_context(nc.sbuf_tensor(name + "_lo", [1, width], BF16))
    P.dma("y", b32[0:1, 0:width], src_row_d)
    P.cp("v", hi[:], b32[0:1, 0:width])
    P.cp("v", h32[0:1, 0:width], hi[:])
    P.tt("v", h32[0:1, 0:width], b32[0:1, 0:width], h32[0:1, 0:width], ALU.subtract)
    P.cp("v", lo[:], h32[0:1, 0:width])
    return hi, lo


def build_l1a(nc, P, es):
    x_d = nc.dram_tensor("x", [TOK, D], F32, kind="ExternalInput").ap()
    wqkvt_d = nc.dram_tensor("wqkvt", [D, 3072], F32, kind="ExternalInput").ap()
    wprojt_d = nc.dram_tensor("wprojt", [D, D], F32, kind="ExternalInput").ap()
    bproj_d = nc.dram_tensor("bprojr", [1, D], F32, kind="ExternalInput").ap()
    idf_d = nc.dram_tensor("identf", [128, 128], F32, kind="ExternalInput").ap()
    xo_d = nc.dram_tensor("xo", [TOK, D], F32, kind="ExternalOutput").ap()

    sb = lambda n, shp, dt=F32: es.enter_context(nc.sbuf_tensor(n, shp, dt))
    idf = sb("idf", [128, 128])
    onerow_b = sb("onerow_b", [1, 128], BF16)
    stage = sb("stage", [128, 3072])
    wqkv = sb_tiles(nc, es, "wqkv", 8, [128, 3072], BF16)
    wproj = sb_tiles(nc, es, "wproj", 8, [128, 1024], BF16)
    xt = sb("xt", [128, 1024])
    xf = sb("xf", [128, 1024], BF16)
    qkv = sb("qkv", [128, 3072])
    scr = sb("scr", [128, 1024])
    S = sb("S", [128, 256])
    m16 = sb("m16", [128, 16])
    l16 = sb("l16", [128, 16])
    attn = sb("attn", [128, 1024])
    attnf = sb("attnf", [128, 1024], BF16)
    res = sb("res", [128, 1024])
    outt = sb("outt", [128, 1024])
    ln_sc = {"mean": sb("ln_mean", [128, 1]), "var": sb("ln_var", [128, 1]),
             "xc": sb("ln_xc", [128, 1024]), "sq": sb("ln_sq", [128, 1024]),
             "eps": sb("ln_eps", [128, 1])}
    P.memset("v", ln_sc["eps"], 1e-5)
    ps = [es.enter_context(nc.psum_tensor(f"ps{i}", [128, 512], F32)) for i in range(6)]

    P.dma("y", idf[:], idf_d)
    P.memset("v", onerow_b[:], 1.0)
    bhi, blo = emit_bias_hilo(P, sb, "bproj", bproj_d, D)
    for j in range(8):
        P.dma("y", stage[:], wqkvt_d[j * 128:(j + 1) * 128, :])
        P.cp("v", wqkv[j][:], stage[:])
        P.dma("y", stage[:, 0:1024], wprojt_d[j * 128:(j + 1) * 128, :])
        P.cp("v", wproj[j][:], stage[:, 0:1024])

    qAP = lambda h: bass.AP(tensor=qkv.ap().tensor, offset=h * 192,
                            ap=[[qkv.ap().ap[0][0], 128], [0, 16], [1, 64]])
    kAP = bass.AP(tensor=qkv.ap().tensor, offset=64,
                  ap=[[qkv.ap().ap[0][0], 128], [192, 16], [1, 64]])
    pAP = lambda h: bass.AP(tensor=S.ap().tensor, offset=h * 16,
                            ap=[[S.ap().ap[0][0], 128], [0, 64], [1, 16]])
    vAP = bass.AP(tensor=qkv.ap().tensor, offset=128,
                  ap=[[qkv.ap().ap[0][0], 128], [1, 64], [192, 16]])
    mB = bass.AP(tensor=m16.ap().tensor, offset=0,
                 ap=[[m16.ap().ap[0][0], 128], [1, 16], [0, 16]])
    lB = bass.AP(tensor=l16.ap().tensor, offset=0,
                 ap=[[l16.ap().ap[0][0], 128], [1, 16], [0, 64]])
    S3 = S.ap().rearrange("p (h g) -> p h g", h=16)
    scr3d = scr.ap().rearrange("p (g d) -> p g d", g=16)
    scr3c = scr.ap().rearrange("p (d g) -> p d g", d=64)

    for ti in range(NT):
        P.dma("y", xt[:], x_d[ti * 128:(ti + 1) * 128, :])
        for j in range(8):
            P.transpose(ps[5][:, 0:128], xt[:, j * 128:(j + 1) * 128], idf[:],
                        rk=["xt", "idf"], wk=["ps5"])
            P.cp("s", attnf[:, j * 128:(j + 1) * 128], ps[5][:, 0:128],
                 rk=["ps5"], wk=["attnf"])
        for fb in range(6):
            pq = ps[fb % 2]
            for dj in range(8):
                P.mm(pq[:], attnf[:, dj * 128:(dj + 1) * 128],
                     wqkv[dj][:, fb * 512:(fb + 1) * 512], start=(dj == 0), stop=(dj == 7),
                     rk=["attnf", _apkey(wqkv[dj].ap())], wk=[f"psq{fb % 2}"])
            P.cp("s", qkv[:, fb * 512:(fb + 1) * 512], pq[:],
                 rk=[f"psq{fb % 2}"], wk=["qkv"])
        for h in range(16):
            P.tt("g", scr3d, qAP(h), kAP, ALU.mult, rk=["qkv"], wk=["scr"])
            P.red("v", S[:, h * 16:(h + 1) * 16], scr3d, ALU.add, rk=["scr"], wk=["S"])
        P.red("v", m16[:], S3, ALU.max, rk=["S"], wk=["m16"])
        P.tt("v", S3, S3, mB, ALU.subtract, rk=["S", "m16"], wk=["S"])
        P.act(S[:], S[:], AF.Exp, scale=0.125, e="s")
        P.red("v", l16[:], S3, ALU.add, rk=["S"], wk=["l16"])
        P.op("v", lambda eng: eng.reciprocal(l16[:], l16[:]), reads=["l16"], writes=["l16"])
        for h in range(16):
            P.tt("g", scr3c, pAP(h), vAP, ALU.mult, rk=["S", "qkv"], wk=["scr"])
            P.red("v", attn[:, h * 64:(h + 1) * 64], scr3c, ALU.add, rk=["scr"], wk=["attn"])
        attn3 = attn.ap().rearrange("p (h d) -> p h d", h=16)
        P.tt("v", attn3, attn3, lB, ALU.mult, rk=["attn", "l16"], wk=["attn"])
        for j in range(8):
            P.transpose(ps[5][:, 0:128], attn[:, j * 128:(j + 1) * 128], idf[:],
                        rk=["attn", "idf"], wk=["ps5"])
            P.cp("s", xf[:, j * 128:(j + 1) * 128], ps[5][:, 0:128],
                 rk=["ps5"], wk=["xf"])
        for ob in range(2):
            po = ps[2 + ob]
            kk = f"pso{ob}"
            for j in range(8):
                P.mm(po[:], xf[:, j * 128:(j + 1) * 128],
                     wproj[j][:, ob * 512:(ob + 1) * 512], start=(j == 0), stop=False,
                     rk=["xf", _apkey(wproj[j].ap())], wk=[kk])
            P.mm(po[:], onerow_b[:], bhi[0:1, ob * 512:(ob + 1) * 512],
                 start=False, stop=False, rk=["onerow_b", _apkey(bhi.ap())], wk=[kk])
            P.mm(po[:], onerow_b[:], blo[0:1, ob * 512:(ob + 1) * 512],
                 start=False, stop=True, rk=["onerow_b", _apkey(blo.ap())], wk=[kk])
            P.tt("v", res[:, ob * 512:(ob + 1) * 512], po[:], xt[:, ob * 512:(ob + 1) * 512],
                 ALU.add, rk=[kk, "xt"], wk=["res"])
        emit_layernorm_tile(P, res[:], outt[:], D, ln_sc)
        P.dma("g", xo_d[ti * 128:(ti + 1) * 128, :], outt[:])


def _run_l1a(x_shards, Wqkv, Wproj, bproj):
    wqkvt = np.ascontiguousarray(Wqkv.T)
    wprojt = np.ascontiguousarray(Wproj.T)
    bprojr = np.ascontiguousarray(bproj.reshape(1, D))
    in_maps = [{"x": x_shards[c], "wqkvt": wqkvt, "wprojt": wprojt,
                "bprojr": bprojr, "identf": _IDENT32} for c in range(NCORES)]
    return _launch(build_l1a, in_maps, ["xo"])



def build_l0a(nc, P, es):
    x_d = nc.dram_tensor("x", [TOK, D], F32, kind="ExternalInput").ap()
    xfm_d = nc.dram_tensor("xfm", [D, S], F32, kind="ExternalInput").ap()
    wqt_d = nc.dram_tensor("wqt", [D, D], F32, kind="ExternalInput").ap()
    wkt_d = nc.dram_tensor("wkt", [D, D], F32, kind="ExternalInput").ap()
    wvt_d = nc.dram_tensor("wvt", [D, D], F32, kind="ExternalInput").ap()
    wot_d = nc.dram_tensor("wot", [D, D], F32, kind="ExternalInput").ap()
    xo_d = nc.dram_tensor("xo", [TOK, D], F32, kind="ExternalOutput").ap()
    kfm_d = nc.dram_tensor("kfm", [D, S], BF16, kind="Internal").ap()
    vpd_d = nc.dram_tensor("vpd", [S, 16 * 65], BF16, kind="Internal").ap()

    sb = lambda n, shp, dt=F32: es.enter_context(nc.sbuf_tensor(n, shp, dt))
    stage = sb("stage", [128, 1024])
    wq = sb_tiles(nc, es, "wq", 8, [128, 1024], BF16)
    wk = sb_tiles(nc, es, "wk", 8, [128, 1024], BF16)
    wv = sb_tiles(nc, es, "wv", 8, [128, 1024], BF16)
    wo = sb_tiles(nc, es, "wo", 8, [128, 512], BF16)
    xfs = sb("xfs", [128, 512])
    xfb = sb_tiles(nc, es, "xfb", 8, [128, 512], BF16)
    kev = sb_tiles(nc, es, "kev", 2, [128, 512], BF16)
    vp = sb_tiles(nc, es, "vp", 2, [128, 1040], BF16)
    qfm = sb_tiles(nc, es, "qfm", 8, [128, TOK], BF16)
    kb = sb_tiles(nc, es, "kb", 4, [128, 128], BF16)
    vb = sb_tiles(nc, es, "vb", 4, [128, 130], BF16)
    pt = sb_tiles(nc, es, "pt", 2, [128, 512], BF16)
    afm = sb_tiles(nc, es, "afm", 8, [128, 512])
    afb = sb_tiles(nc, es, "afb", 8, [128, 512], BF16)
    l_sb = sb("l_sb", [1, 16 * 512])
    one64 = sb("one64", [1, 64])
    xt = sb("xt", [128, 1024])
    res = sb("res", [128, 1024])
    outt = sb("outt", [128, 1024])
    ln_sc = {"mean": sb("ln_mean", [128, 1]), "var": sb("ln_var", [128, 1]),
             "xc": sb("ln_xc", [128, 1024]), "sq": sb("ln_sq", [128, 1024]),
             "eps": sb("ln_eps", [128, 1])}
    P.memset("v", ln_sc["eps"], 1e-5)
    P.memset("v", one64[:], 1.0)
    ps = [es.enter_context(nc.psum_tensor(f"ps{i}", [128, 512], F32)) for i in range(8)]

    for j in range(8):
        P.dma("y", stage[:], wqt_d[j * 128:(j + 1) * 128, :])
        P.cp("v", wq[j][:], stage[:])
        P.dma("y", stage[:], wkt_d[j * 128:(j + 1) * 128, :])
        P.cp("v", wk[j][:], stage[:])
        P.dma("y", stage[:], wvt_d[j * 128:(j + 1) * 128, :])
        P.cp("v", wv[j][:], stage[:])

    idf_d = nc.dram_tensor("identf", [128, 128], F32, kind="ExternalInput").ap()
    idf = sb("idf", [128, 128])
    P.dma("y", idf[:], idf_d)
    for tb in range(16):
        for j in range(8):
            P.dma("y", xfs[:], xfm_d[j * 128:(j + 1) * 128, tb * 512:(tb + 1) * 512])
            P.cp("v", xfb[j][:], xfs[:])
        for o in range(8):
            pk = ps[o % 2]
            kk = f"psk{o % 2}"
            for dj in range(8):
                P.mm(pk[:], wk[dj][:, o * 128:(o + 1) * 128], xfb[dj][:],
                     start=(dj == 0), stop=(dj == 7),
                     rk=[_apkey(wk[dj].ap()), _apkey(xfb[dj].ap())], wk=[kk])
            P.cp("s", kev[o % 2][:], pk[:], rk=[kk], wk=[_apkey(kev[o % 2].ap())])
            P.dma("g", kfm_d[o * 128:(o + 1) * 128, tb * 512:(tb + 1) * 512], kev[o % 2][:])
        for i in range(4):
            pv = ps[2 + i % 2]
            kk = f"psv{i % 2}"
            vpt = vp[i % 2]
            vap = vpt.ap()
            P.memset("v", bass.AP(tensor=vap.tensor, offset=64,
                                  ap=[[vap.ap[0][0], 128], [65, 16]]), 1.0,
                     wk=[_apkey(vap)])
            for ob in range(2):
                for dj in range(8):
                    P.mm(pv[:], xfb[dj][:, i * 128:(i + 1) * 128],
                         wv[dj][:, ob * 512:(ob + 1) * 512],
                         start=(dj == 0), stop=(dj == 7),
                         rk=[_apkey(xfb[dj].ap()), _apkey(wv[dj].ap())], wk=[kk])
                dst = bass.AP(tensor=vap.tensor, offset=ob * 8 * 65,
                              ap=[[vap.ap[0][0], 128], [65, 8], [1, 64]])
                P.cp("s", dst, pv[:].rearrange("p (h d) -> p h d", h=8),
                     rk=[kk], wk=[_apkey(vap)])
            P.dma("g", vpd_d[(tb * 4 + i) * 128:(tb * 4 + i + 1) * 128, :], vpt[:])
    xqf = sb_tiles(nc, es, "xqf", 8, [128, 128], BF16)
    for ti in range(NT):
        P.dma("y", xt[:], x_d[ti * 128:(ti + 1) * 128, :])
        for j in range(8):
            P.transpose(ps[6][:, 0:128], xt[:, j * 128:(j + 1) * 128], idf[:],
                        rk=["xt", "idf"], wk=["ps6"])
            P.cp("s", xqf[j][:], ps[6][:, 0:128], rk=["ps6"], wk=[_apkey(xqf[j].ap())])
        for o in range(8):
            pq = ps[4 + o % 2]
            kk = f"psq{o % 2}"
            for dj in range(8):
                P.mm(pq[:, 0:128], wq[dj][:, o * 128:(o + 1) * 128], xqf[dj][:],
                     start=(dj == 0), stop=(dj == 7),
                     rk=[_apkey(wq[dj].ap()), _apkey(xqf[dj].ap())], wk=[kk])
            P.cp("s", qfm[o][:, ti * 128:(ti + 1) * 128], pq[:, 0:128],
                 rk=[kk], wk=[_apkey(qfm[o].ap())])

    for j in range(8):
        P.dma("y", stage[:], wot_d[j * 128:(j + 1) * 128, :])
        P.cp("v", wo[j][:], stage[:, 0:512])
    wo2 = sb_tiles(nc, es, "wo2", 8, [128, 512], BF16)
    for j in range(8):
        P.dma("y", stage[:], wot_d[j * 128:(j + 1) * 128, :])
        P.cp("v", wo2[j][:], stage[:, 512:1024])

    NK = S // 128
    for qb in range(4):
        for p in range(8):
            ksc = [f"pssc{j}" for j in range(2)]
            kav = [f"psav{j}" for j in range(2)]
            for kt in range(NK):
                if kt == 0:
                    for pre in range(min(4, NK)):
                        P.dma("y", kb[pre][:], kfm_d[p * 128:(p + 1) * 128, pre * 128:(pre + 1) * 128],
                              wk=[("kb", pre % 4)])
                        P.dma("g", vb[pre][:], vpd_d[pre * 128:(pre + 1) * 128, p * 130:(p + 1) * 130],
                              wk=[("vb", pre % 4)])
                for j in range(2):
                    psc = ps[j]
                    P.mm(psc[:], kb[kt % 4][0:64, :] if j == 0 else kb[kt % 4][64:128, :],
                         qfm[p][j * 64:(j + 1) * 64, qb * 512:(qb + 1) * 512],
                         start=True, stop=True,
                         rk=[("kb", kt % 4), _apkey(qfm[p].ap())], wk=[ksc[j]])
                    P.act(pt[j][:], psc[:], AF.Sigmoid, scale=0.125, e="s",
                          rk=[ksc[j]], wk=[_apkey(pt[j].ap())])
                    P.mm(ps[2 + j][0:65, :], vb[kt % 4][:, j * 65:(j + 1) * 65], pt[j][:],
                         start=(kt == 0), stop=(kt == NK - 1),
                         rk=[("vb", kt % 4), _apkey(pt[j].ap())], wk=[kav[j]])
                if kt + 4 < NK:
                    P.dma("y", kb[(kt + 4) % 4][:], kfm_d[p * 128:(p + 1) * 128, (kt + 4) * 128:(kt + 5) * 128],
                          wk=[("kb", (kt + 4) % 4)])
                    P.dma("g", vb[(kt + 4) % 4][:], vpd_d[(kt + 4) * 128:(kt + 5) * 128, p * 130:(p + 1) * 130],
                          wk=[("vb", (kt + 4) % 4)])
            for j in range(2):
                P.cp("s", afm[p][j * 64:(j + 1) * 64, :], ps[2 + j][0:64, :],
                     rk=[kav[j]], wk=[_apkey(afm[p].ap())])
                P.cp("s", l_sb[0:1, (2 * p + j) * 512:(2 * p + j + 1) * 512], ps[2 + j][64:65, :],
                     rk=[kav[j]], wk=["l_sb"])
        for p in range(8):
            for j in range(2):
                sl = l_sb[0:1, (2 * p + j) * 512:(2 * p + j + 1) * 512]
                P.ts("v", sl, sl, 1e-6, None, ALU.add, rk=["l_sb"], wk=["l_sb"])
                P.op("v", lambda eng, a=sl: eng.reciprocal(a, a), reads=["l_sb"], writes=["l_sb"])
                P.mm(ps[4][j * 64:(j + 1) * 64, :], one64[:],
                     l_sb[0:1, (2 * p + j) * 512:(2 * p + j + 1) * 512],
                     start=True, stop=True, rk=["l_sb", "one64"], wk=["pslb"])
            P.tt("v", afb[p][:], afm[p][:], ps[4][:], ALU.mult,
                 rk=[_apkey(afm[p].ap()), "pslb"], wk=[_apkey(afb[p].ap())])
        for i in range(4):
            ti = qb * 4 + i
            P.dma("y", xt[:], x_d[ti * 128:(ti + 1) * 128, :])
            for ob in range(2):
                po = ps[6 + ob]
                kk = f"pso{ob}"
                wob = wo if ob == 0 else wo2
                for j in range(8):
                    P.mm(po[:], afb[j][:, i * 128:(i + 1) * 128], wob[j][:],
                         start=(j == 0), stop=(j == 7),
                         rk=[_apkey(afb[j].ap()), _apkey(wob[j].ap())], wk=[kk])
                P.tt("v", res[:, ob * 512:(ob + 1) * 512], po[:], xt[:, ob * 512:(ob + 1) * 512],
                     ALU.add, rk=[kk, "xt"], wk=["res"])
            emit_layernorm_tile(P, res[:], outt[:], D, ln_sc)
            P.dma("g", xo_d[ti * 128:(ti + 1) * 128, :], outt[:])


def _run_l0a(x_flat, Wq, Wk, Wv, Wo):
    xs = _shards_tm(x_flat)
    fms = _fm_batches(x_flat)
    wqt = np.ascontiguousarray(Wq.T)
    wkt = np.ascontiguousarray(Wk.T)
    wvt = np.ascontiguousarray(Wv.T)
    wot = np.ascontiguousarray(Wo.T)
    in_maps = [{"x": xs[c], "xfm": fms[c], "wqt": wqt, "wkt": wkt, "wvt": wvt,
                "wot": wot, "identf": _IDENT32} for c in range(NCORES)]
    return _launch(build_l0a, in_maps, ["xo"])



_L2_STRIDES = [1, 1, 2, 2, 2, 2, 4, 4, 4, 4, 8, 8, 8, 8, 16, 16]


def emit_row_bcast(P, ps, onerow_b, hi, lo, dst, width):
    for ob in range((width + 511) // 512):
        w = min(512, width - ob * 512)
        P.mm(_asap(ps)[:, 0:w], onerow_b[:], hi[0:1, ob * 512:ob * 512 + w],
             start=True, stop=False, rk=[_apkey(_asap(onerow_b)), _apkey(_asap(hi))],
             wk=[_apkey(_asap(ps))])
        P.mm(_asap(ps)[:, 0:w], onerow_b[:], lo[0:1, ob * 512:ob * 512 + w],
             start=False, stop=True, rk=[_apkey(_asap(onerow_b)), _apkey(_asap(lo))],
             wk=[_apkey(_asap(ps))])
        P.cp("s", _asap(dst)[:, ob * 512:ob * 512 + w], _asap(ps)[:, 0:w])


def build_l2a1(nc, P, es):
    x_d = nc.dram_tensor("x", [TOK, D], F32, kind="ExternalInput").ap()
    xfm_d = nc.dram_tensor("xfm", [D, S], F32, kind="ExternalInput").ap()
    wqkv_d = nc.dram_tensor("wqkv", [D, 3 * D], F32, kind="ExternalInput").ap()
    bqkvc_d = nc.dram_tensor("bqkvc", [128, 24], F32, kind="ExternalInput").ap()
    bqkvr_d = nc.dram_tensor("bqkvr", [1, 3 * D], F32, kind="ExternalInput").ap()
    wout_d = nc.dram_tensor("wout", [D, D], F32, kind="ExternalInput").ap()
    boutr_d = nc.dram_tensor("boutr", [1, D], F32, kind="ExternalInput").ap()
    gr_d = nc.dram_tensor("gammar", [1, D], F32, kind="ExternalInput").ap()
    br_d = nc.dram_tensor("betar", [1, D], F32, kind="ExternalInput").ap()
    idf_d = nc.dram_tensor("identf", [128, 128], F32, kind="ExternalInput").ap()
    h_d = nc.dram_tensor("h", [TOK, D], F32, kind="ExternalOutput").ap()
    kfm_s = {s: nc.dram_tensor(f"kfm{s}", [D, S // s], BF16, kind="Internal").ap()
             for s in (1, 2, 4, 8, 16)}
    vpd_d = nc.dram_tensor("vpd", [S, 16 * 65], BF16, kind="Internal").ap()

    sb = lambda n, shp, dt=F32: es.enter_context(nc.sbuf_tensor(n, shp, dt))
    stage = sb("stage", [128, 1024])
    wqkv = sb_tiles(nc, es, "wqkv", 8, [128, 3072], BF16)
    idf = sb("idf", [128, 128])
    onerow_b = sb("onerow_b", [1, 128], BF16)
    one64 = sb("one64", [1, 64])
    bqc = sb("bqc", [128, 24])
    xfs = sb("xfs", [128, 512])
    xfb = sb_tiles(nc, es, "xfb", 8, [128, 512], BF16)
    kev = sb_tiles(nc, es, "kev", 2, [128, 512])
    kevs = sb("kevs", [128, 512], BF16)
    vp = sb_tiles(nc, es, "vp", 2, [128, 1040], BF16)
    qfm = sb_tiles(nc, es, "qfm", 8, [128, TOK], BF16)
    xqf = sb_tiles(nc, es, "xqf", 8, [128, 128], BF16)
    kb = sb_tiles(nc, es, "kb", 4, [128, 128], BF16)
    vb = sb_tiles(nc, es, "vb", 4, [128, 130], BF16)
    pt = sb_tiles(nc, es, "pt", 2, [128, 512], BF16)
    afb = sb_tiles(nc, es, "afb", 8, [128, 512], BF16)
    l_sb = sb("l_sb", [1, 1024])
    wo = sb_tiles(nc, es, "wo", 8, [128, 1024], BF16)
    gb = sb("gb", [128, 1024])
    bb = sb("bb", [128, 1024])
    xt = sb("xt", [128, 1024])
    res = sb("res", [128, 1024])
    outt = sb("outt", [128, 1024])
    ln_sc = {"mean": sb("ln_mean", [128, 1]), "var": sb("ln_var", [128, 1]),
             "xc": sb("ln_xc", [128, 1024]), "sq": sb("ln_sq", [128, 1024]),
             "eps": sb("ln_eps", [128, 1])}
    c20 = sb("c20", [128, 1])
    P.memset("v", ln_sc["eps"], 1e-5)
    P.memset("v", onerow_b[:], 1.0)
    P.memset("v", one64[:], 1.0)
    P.memset("v", c20[:], -20.0)
    ps = [es.enter_context(nc.psum_tensor(f"ps{i}", [128, 512], F32)) for i in range(8)]

    P.dma("y", idf[:], idf_d)
    P.dma("y", bqc[:], bqkvc_d)
    bvhi, bvlo = emit_bias_hilo(P, sb, "bv", bqkvr_d[0:1, 2048:3072], D)
    bohi, bolo = emit_bias_hilo(P, sb, "bo", boutr_d, D)
    ghi, glo = emit_bias_hilo(P, sb, "ga", gr_d, D)
    bhi_, blo_ = emit_bias_hilo(P, sb, "be", br_d, D)
    emit_row_bcast(P, ps[7], onerow_b, ghi, glo, gb, D)
    emit_row_bcast(P, ps[7], onerow_b, bhi_, blo_, bb, D)
    for j in range(8):
        for sl in range(3):
            P.dma("y", stage[:], wqkv_d[j * 128:(j + 1) * 128, sl * 1024:(sl + 1) * 1024])
            P.cp("v", wqkv[j][:, sl * 1024:(sl + 1) * 1024], stage[:])

    for tb in range(16):
        for j in range(8):
            P.dma("y", xfs[:], xfm_d[j * 128:(j + 1) * 128, tb * 512:(tb + 1) * 512])
            P.cp("v", xfb[j][:], xfs[:])
        for o in range(8):
            pk = ps[o % 2]
            kk = f"psk{o % 2}"
            for dj in range(8):
                P.mm(pk[:], wqkv[dj][:, 1024 + o * 128:1024 + (o + 1) * 128], xfb[dj][:],
                     start=(dj == 0), stop=(dj == 7),
                     rk=[_apkey(wqkv[dj].ap()), _apkey(xfb[dj].ap())], wk=[kk])
            ke = kev[o % 2]
            P.act(ke[:], pk[:], AF.Identity, bias=bqc[:, 8 + o:9 + o], e="s",
                  rk=[kk, "bqc"], wk=[_apkey(ke.ap())])
            P.cp("v", kevs[:], ke[:], rk=[_apkey(ke.ap())], wk=["kevs"])
            P.dma("g", kfm_s[1][o * 128:(o + 1) * 128, tb * 512:(tb + 1) * 512], kevs[:])
            for s in (2, 4, 8, 16):
                st_ap = bass.AP(tensor=ke.ap().tensor, offset=0,
                                ap=[[ke.ap().ap[0][0], 128], [s, 512 // s]])
                P.cp("v", kevs[:, 0:512 // s], st_ap, rk=[_apkey(ke.ap())], wk=["kevs"])
                P.dma("g", kfm_s[s][o * 128:(o + 1) * 128, tb * (512 // s):(tb + 1) * (512 // s)],
                      kevs[:, 0:512 // s])
        for i in range(4):
            pv = ps[2 + i % 2]
            kk = f"psv{i % 2}"
            vpt = vp[i % 2]
            vap = vpt.ap()
            P.memset("v", bass.AP(tensor=vap.tensor, offset=64,
                                  ap=[[vap.ap[0][0], 128], [65, 16]]), 1.0,
                     wk=[_apkey(vap)])
            for ob in range(2):
                for dj in range(8):
                    P.mm(pv[:], xfb[dj][:, i * 128:(i + 1) * 128],
                         wqkv[dj][:, 2048 + ob * 512:2048 + (ob + 1) * 512],
                         start=(dj == 0), stop=False,
                         rk=[_apkey(xfb[dj].ap()), _apkey(wqkv[dj].ap())], wk=[kk])
                P.mm(pv[:], onerow_b[:], bvhi[0:1, ob * 512:(ob + 1) * 512],
                     start=False, stop=False, rk=["onerow_b", _apkey(bvhi.ap())], wk=[kk])
                P.mm(pv[:], onerow_b[:], bvlo[0:1, ob * 512:(ob + 1) * 512],
                     start=False, stop=True, rk=["onerow_b", _apkey(bvlo.ap())], wk=[kk])
                dst = bass.AP(tensor=vap.tensor, offset=ob * 8 * 65,
                              ap=[[vap.ap[0][0], 128], [65, 8], [1, 64]])
                P.cp("s", dst, pv[:].rearrange("p (h d) -> p h d", h=8),
                     rk=[kk], wk=[_apkey(vap)])
            P.dma("g", vpd_d[(tb * 4 + i) * 128:(tb * 4 + i + 1) * 128, :], vpt[:])
    for ti in range(NT):
        P.dma("y", xt[:], x_d[ti * 128:(ti + 1) * 128, :])
        for j in range(8):
            P.transpose(ps[6][:, 0:128], xt[:, j * 128:(j + 1) * 128], idf[:],
                        rk=["xt", "idf"], wk=["ps6"])
            P.cp("s", xqf[j][:], ps[6][:, 0:128], rk=["ps6"], wk=[_apkey(xqf[j].ap())])
        for o in range(8):
            pq = ps[4 + o % 2]
            kk = f"psq{o % 2}"
            for dj in range(8):
                P.mm(pq[:, 0:128], wqkv[dj][:, o * 128:(o + 1) * 128], xqf[dj][:],
                     start=(dj == 0), stop=(dj == 7),
                     rk=[_apkey(wqkv[dj].ap()), _apkey(xqf[dj].ap())], wk=[kk])
            P.act(stage[:, 0:128], pq[:, 0:128], AF.Identity, bias=bqc[:, o:o + 1], e="s",
                  rk=[kk, "bqc"], wk=["stage"])
            P.cp("v", qfm[o][:, ti * 128:(ti + 1) * 128], stage[:, 0:128],
                 rk=["stage"], wk=[_apkey(qfm[o].ap())])

    for qb in range(4):
        for p in range(8):
            sstr = _L2_STRIDES[2 * p]
            NK = S // sstr // 128
            kav = [f"psav{j}" for j in range(2)]
            for kt in range(NK):
                if kt == 0:
                    for pre in range(min(4, NK)):
                        P.dma("y", kb[pre][:], kfm_s[sstr][p * 128:(p + 1) * 128, pre * 128:(pre + 1) * 128],
                              wk=[("kb", pre)])
                        vsrc = bass.AP(tensor=vpd_d.tensor, offset=(pre * 128 * sstr) * 1040 + p * 130,
                                       ap=[[sstr * 1040, 128], [1, 130]])
                        P.dma("g", vb[pre][:], vsrc, wk=[("vb", pre)])
                for j in range(2):
                    psc = ps[j]
                    P.mm(psc[:], kb[kt % 4][j * 64:(j + 1) * 64, :],
                         qfm[p][j * 64:(j + 1) * 64, qb * 512:(qb + 1) * 512],
                         start=True, stop=True,
                         rk=[("kb", kt % 4), _apkey(qfm[p].ap())], wk=[f"pssc{j}"])
                    P.act(pt[j][:], psc[:], AF.Exp, scale=0.125, bias=c20[:], e="s",
                          rk=[f"pssc{j}"], wk=[_apkey(pt[j].ap())])
                    P.mm(ps[2 + j][0:65, :], vb[kt % 4][:, j * 65:(j + 1) * 65], pt[j][:],
                         start=(kt == 0), stop=(kt == NK - 1),
                         rk=[("vb", kt % 4), _apkey(pt[j].ap())], wk=[kav[j]])
                if kt + 4 < NK:
                    P.dma("y", kb[(kt + 4) % 4][:], kfm_s[sstr][p * 128:(p + 1) * 128, (kt + 4) * 128:(kt + 5) * 128],
                          wk=[("kb", (kt + 4) % 4)])
                    vsrc = bass.AP(tensor=vpd_d.tensor, offset=((kt + 4) * 128 * sstr) * 1040 + p * 130,
                                   ap=[[sstr * 1040, 128], [1, 130]])
                    P.dma("g", vb[(kt + 4) % 4][:], vsrc, wk=[("vb", (kt + 4) % 4)])
            for j in range(2):
                P.cp("s", afb[p][j * 64:(j + 1) * 64, :], ps[2 + j][0:64, :],
                     rk=[kav[j]], wk=[_apkey(afb[p].ap())])
                P.cp("s", l_sb[0:1, j * 512:(j + 1) * 512], ps[2 + j][64:65, :],
                     rk=[kav[j]], wk=["l_sb"])
            for j in range(2):
                sl = l_sb[0:1, j * 512:(j + 1) * 512]
                P.op("v", lambda eng, a=sl: eng.reciprocal(a, a), reads=["l_sb"], writes=["l_sb"])
                P.mm(ps[4][j * 64:(j + 1) * 64, :], one64[:], l_sb[0:1, j * 512:(j + 1) * 512],
                     start=True, stop=True, rk=["l_sb", "one64"])
            P.tt("v", afb[p][:], afb[p][:], ps[4][:], ALU.mult)
        if qb == 0:
            for j in range(8):
                P.dma("y", stage[:], wout_d[j * 128:(j + 1) * 128, :])
                P.cp("v", wo[j][:], stage[:], rk=["stage"], wk=[_apkey(wo[j].ap())])
        for i in range(4):
            ti = qb * 4 + i
            P.dma("y", xt[:], x_d[ti * 128:(ti + 1) * 128, :])
            for ob in range(2):
                po = ps[6 + ob]
                kk = f"pso{ob}"
                for j in range(8):
                    P.mm(po[:], afb[j][:, i * 128:(i + 1) * 128],
                         wo[j][:, ob * 512:(ob + 1) * 512],
                         start=(j == 0), stop=False,
                         rk=[_apkey(afb[j].ap()), _apkey(wo[j].ap())], wk=[kk])
                P.mm(po[:], onerow_b[:], bohi[0:1, ob * 512:(ob + 1) * 512],
                     start=False, stop=False, rk=["onerow_b", _apkey(bohi.ap())], wk=[kk])
                P.mm(po[:], onerow_b[:], bolo[0:1, ob * 512:(ob + 1) * 512],
                     start=False, stop=True, rk=["onerow_b", _apkey(bolo.ap())], wk=[kk])
                P.tt("v", res[:, ob * 512:(ob + 1) * 512], po[:], xt[:, ob * 512:(ob + 1) * 512],
                     ALU.add, rk=[kk, "xt"], wk=["res"])
            emit_layernorm_tile(P, res[:], outt[:], D, ln_sc)
            P.tt("v", outt[:], outt[:], gb[:], ALU.mult, rk=["outt", "gb"], wk=["outt"])
            P.tt("v", outt[:], outt[:], bb[:], ALU.add, rk=["outt", "bb"], wk=["outt"])
            P.dma("g", h_d[ti * 128:(ti + 1) * 128, :], outt[:])


def build_l2a2(nc, P, es):
    x_d = nc.dram_tensor("x", [TOK, D], F32, kind="ExternalInput").ap()
    h_d = nc.dram_tensor("h", [TOK, D], F32, kind="ExternalInput").ap()
    w1_d = nc.dram_tensor("w1", [D, 4096], F32, kind="ExternalInput").ap()
    b1c_d = nc.dram_tensor("b1c", [128, 32], F32, kind="ExternalInput").ap()
    w2_d = nc.dram_tensor("w2", [4096, D], F32, kind="ExternalInput").ap()
    b2r_d = nc.dram_tensor("b2r", [1, D], F32, kind="ExternalInput").ap()
    gr_d = nc.dram_tensor("gammar", [1, D], F32, kind="ExternalInput").ap()
    br_d = nc.dram_tensor("betar", [1, D], F32, kind="ExternalInput").ap()
    idf_d = nc.dram_tensor("identf", [128, 128], F32, kind="ExternalInput").ap()
    xo_d = nc.dram_tensor("xo", [TOK, D], F32, kind="ExternalOutput").ap()

    sb = lambda n, shp, dt=F32: es.enter_context(nc.sbuf_tensor(n, shp, dt))
    idf = sb("idf", [128, 128])
    onerow_b = sb("onerow_b", [1, 128], BF16)
    stage = sb("stage", [128, 4096])
    w1b = sb_tiles(nc, es, "w1b", 8, [128, 4096], BF16)
    w2b = sb_tiles(nc, es, "w2b", 4, [128, 1024], BF16)
    b1c = sb("b1c_s", [128, 32])
    gbx = sb("gbx", [128, 1024])
    bbx = sb("bbx", [128, 1024])
    ht = sb_tiles(nc, es, "ht", 4, [128, 1024])
    hfb = sb_tiles(nc, es, "hfb", 8, [128, 512], BF16)
    rfb = sb_tiles(nc, es, "rfb", 32, [128, 512], BF16)
    xt = sb("xt", [128, 1024])
    res = sb("res", [128, 1024])
    outt = sb("outt", [128, 1024])
    ln_sc = {"mean": sb("ln_mean", [128, 1]), "var": sb("ln_var", [128, 1]),
             "xc": sb("ln_xc", [128, 1024]), "sq": sb("ln_sq", [128, 1024]),
             "eps": sb("ln_eps", [128, 1])}
    P.memset("v", ln_sc["eps"], 1e-5)
    P.memset("v", onerow_b[:], 1.0)
    ps = [es.enter_context(nc.psum_tensor(f"ps{i}", [128, 512], F32)) for i in range(8)]

    P.dma("y", idf[:], idf_d)
    P.dma("y", b1c[:], b1c_d)
    b2hi, b2lo = emit_bias_hilo(P, sb, "b2", b2r_d, D)
    ghi, glo = emit_bias_hilo(P, sb, "ga", gr_d, D)
    bhi_, blo_ = emit_bias_hilo(P, sb, "be", br_d, D)
    emit_row_bcast(P, ps[7], onerow_b, ghi, glo, gbx, D)
    emit_row_bcast(P, ps[7], onerow_b, bhi_, blo_, bbx, D)
    for j in range(8):
        P.dma("y", stage[:], w1_d[j * 128:(j + 1) * 128, :])
        P.cp("v", w1b[j][:], stage[:])

    for tb in range(4):
        for i in range(4):
            ti = tb * 4 + i
            P.dma("y", ht[i][:], h_d[ti * 128:(ti + 1) * 128, :])
            for j in range(8):
                P.transpose(ps[6][:, 0:128], ht[i][:, j * 128:(j + 1) * 128], idf[:],
                            rk=[_apkey(ht[i].ap()), "idf"], wk=["ps6"])
                P.cp("s", hfb[j][:, i * 128:(i + 1) * 128], ps[6][:, 0:128],
                     rk=["ps6"], wk=[_apkey(hfb[j].ap())])
        for f in range(32):
            pr = ps[f % 2]
            kk = f"psr{f % 2}"
            for dj in range(8):
                P.mm(pr[:], w1b[dj][:, f * 128:(f + 1) * 128], hfb[dj][:],
                     start=(dj == 0), stop=(dj == 7),
                     rk=[_apkey(w1b[dj].ap()), _apkey(hfb[dj].ap())], wk=[kk])
            P.act(rfb[f][:], pr[:], AF.Relu, bias=b1c[:, f:f + 1], e="s",
                  rk=[kk, "b1c_s"], wk=[_apkey(rfb[f].ap())])
        for f in range(4):
            P.dma("g", w2b[f][:], w2_d[f * 128:(f + 1) * 128, :])
        for f in range(32):
            for i in range(4):
                for ob in range(2):
                    P.mm(ps[i * 2 + ob][:], rfb[f][:, i * 128:(i + 1) * 128],
                         w2b[f % 4][:, ob * 512:(ob + 1) * 512],
                         start=(f == 0), stop=False)
            if f + 4 < 32:
                P.dma("g", w2b[(f + 4) % 4][:], w2_d[(f + 4) * 128:(f + 5) * 128, :])
        for i in range(4):
            ti = tb * 4 + i
            P.dma("y", xt[:], x_d[ti * 128:(ti + 1) * 128, :])
            for ob in range(2):
                po = ps[i * 2 + ob]
                P.mm(po[:], onerow_b[:], b2hi[0:1, ob * 512:(ob + 1) * 512],
                     start=False, stop=False)
                P.mm(po[:], onerow_b[:], b2lo[0:1, ob * 512:(ob + 1) * 512],
                     start=False, stop=True)
                P.tt("v", res[:, ob * 512:(ob + 1) * 512], po[:], ht[i][:, ob * 512:(ob + 1) * 512],
                     ALU.add)
            emit_layernorm_tile(P, res[:], outt[:], D, ln_sc)
            P.tt("v", outt[:], outt[:], gbx[:], ALU.mult, rk=["outt", "gbx"], wk=["outt"])
            P.tt("v", outt[:], outt[:], bbx[:], ALU.add, rk=["outt", "bbx"], wk=["outt"])
            P.tt("v", res[:], outt[:], xt[:], ALU.add, rk=["outt", "xt"], wk=["res"])
            emit_layernorm_tile(P, res[:], outt[:], D, ln_sc)
            P.dma("g", xo_d[ti * 128:(ti + 1) * 128, :], outt[:])


def _run_l2a(x_flat, Wqkv, bqkv, Wout, bout, gamma, beta, W1, b1, W2, b2):
    xs = _shards_tm(x_flat)
    fms = _fm_batches(x_flat)
    bqkvc = np.ascontiguousarray(bqkv[0:2048].reshape(16, 128).T)
    bqkvc = np.concatenate([bqkvc, np.zeros((128, 8), bqkvc.dtype)], 1)
    bqkvr = np.ascontiguousarray(bqkv.reshape(1, 3 * D))
    in1 = [{"x": xs[c], "xfm": fms[c], "wqkv": Wqkv, "bqkvc": bqkvc,
            "bqkvr": bqkvr, "wout": Wout, "boutr": bout.reshape(1, D),
            "gammar": gamma.reshape(1, D), "betar": beta.reshape(1, D),
            "identf": _IDENT32} for c in range(NCORES)]
    r1 = _launch(build_l2a1, in1, ["h"])
    b1c = np.ascontiguousarray(b1.reshape(32, 128).T)
    in2 = [{"x": xs[c], "h": r1[c]["h"], "w1": W1, "b1c": b1c, "w2": W2,
            "b2r": b2.reshape(1, D), "gammar": gamma.reshape(1, D),
            "betar": beta.reshape(1, D), "identf": _IDENT32} for c in range(NCORES)]
    return _launch(build_l2a2, in2, ["xo"])



def build_l3a0(nc, P, es):
    xbc_d = nc.dram_tensor("xbc", [256, S], F32, kind="ExternalInput").ap()
    bch_d = nc.dram_tensor("bch", [256, S], F32, kind="ExternalOutput").ap()
    sb = lambda n, shp, dt=F32: es.enter_context(nc.sbuf_tensor(n, shp, dt))
    xa = sb("xa", [128, S])
    xb = sb("xb", [128, S])
    tmp = sb("tmp", [128, S])
    resb = sb("resb", [128, S])
    for half in range(2):
        P.dma("y", xa[:], xbc_d[half * 128:(half + 1) * 128, :])
        P.memset("v", resb[:], 0.0)
        cur, nxt = xa, xb
        st = 1
        for it in range(13):
            ck, nk = _apkey(cur.ap()), _apkey(nxt.ap())
            s_ = st
            P.tt("g", tmp[:, 0:s_], cur[:, S - s_:S], cur[:, s_:2 * s_], ALU.add)
            if S - 2 * s_ > 0:
                P.tt("g", tmp[:, s_:S - s_], cur[:, 0:S - 2 * s_], cur[:, 2 * s_:S], ALU.add)
            P.tt("g", tmp[:, S - s_:S], cur[:, S - 2 * s_:S - s_], cur[:, 0:s_], ALU.add)
            P.stt("v", nxt[:], tmp[:], 0.5, cur[:], ALU.mult, ALU.add)
            P.tt("v", resb[:], resb[:], nxt[:], ALU.add)
            cur, nxt = nxt, cur
            st *= 2
        P.ts("v", resb[:], resb[:], 1.0 / 14.0, None, ALU.mult)
        P.dma("g", bch_d[half * 128:(half + 1) * 128, :], resb[:])


def build_l3a1(nc, P, es):
    xfmo_d = nc.dram_tensor("xfmo", [D, TOK], F32, kind="ExternalInput").ap()
    wkfm_d = nc.dram_tensor("wkfm", [3, D, 2048], F32, kind="ExternalInput").ap()
    wktm_d = nc.dram_tensor("wktm", [3, 2048, D], F32, kind="ExternalInput").ap()
    wjc_d = nc.dram_tensor("wjc", [3, 128, 16], F32, kind="ExternalInput").ap()
    winv_d = nc.dram_tensor("winv", [128, 16], F32, kind="ExternalInput").ap()
    btm_d = nc.dram_tensor("btmask", [128, 4096], F32, kind="ExternalInput").ap()
    xg_d = nc.dram_tensor("xg512", [D, 512], F32, kind="ExternalInput").ap()
    wcf_d = nc.dram_tensor("wcf", [4096, D], F32, kind="ExternalInput").ap()
    bcc_d = nc.dram_tensor("bc_c", [128, 8], F32, kind="ExternalInput").ap()
    gmf_d = nc.dram_tensor("gmf", [D, 128], F32, kind="ExternalInput").ap()
    idf_d = nc.dram_tensor("identf", [128, 128], F32, kind="ExternalInput").ap()
    loc_d = nc.dram_tensor("local", [TOK, D], F32, kind="ExternalOutput").ap()
    glob_d = nc.dram_tensor("glob", [TOK, D], F32, kind="ExternalOutput").ap()

    sb = lambda n, shp, dt=F32: es.enter_context(nc.sbuf_tensor(n, shp, dt))
    idf = sb("idf", [128, 128])
    onecol_b = es.enter_context(nc.sbuf_tensor("onecol_b", [128, 1], BF16))
    c32 = sb("c32", [128, 1])
    stage = sb("stage", [128, 2048])
    xob = sb_tiles(nc, es, "xob", 8, [128, TOK], BF16)
    kfmb = sb_tiles(nc, es, "kfmb", 8, [128, 2048], BF16)
    wvtb = sb_tiles(nc, es, "wvtb", 16, [128, 1024], BF16)
    btm = sb("btm", [128, 4096])
    ptb = sb_tiles(nc, es, "ptb", 16, [128, 512], BF16)
    ssf = sb("ssf", [128, 512])
    acc = sb_tiles(nc, es, "acc", 4, [128, 1024])
    wjc = sb("wjc_s", [128, 48])
    winv = sb("winv_s", [128, 16])
    lrow = sb("lrow", [1, 512])
    lcol = sb("lcol", [128, 1])
    fac = sb("fac", [128, 4])
    otmp = sb("otmp", [128, 512])
    outt = sb("outt", [128, 1024])
    ps = [es.enter_context(nc.psum_tensor(f"ps{i}", [128, 512], F32)) for i in range(8)]

    P.dma("y", idf[:], idf_d)
    P.dma("y", btm[:], btm_d)
    P.dma("y", winv[:], winv_d)
    for J in range(3):
        P.dma("y", wjc[:, J * 16:(J + 1) * 16], wjc_d[J])
    P.memset("v", onecol_b[:], 1.0)
    P.memset("v", c32[:], -32.0)
    for j in range(8):
        P.dma("y", stage[:], xfmo_d[j * 128:(j + 1) * 128, :])
        P.cp("v", xob[j][:], stage[:])

    xgb = sb_tiles(nc, es, "xgb", 8, [128, 512], BF16)
    gtf = sb_tiles(nc, es, "gtf", 8, [128, 256], BF16)
    gtt = sb_tiles(nc, es, "gtt", 2, [128, 1024], BF16)
    bcc = sb("bcc_s", [128, 8])
    aw = sb("aw", [128, 256])
    gt32 = sb("gt32", [128, 256])
    awT = sb_tiles(nc, es, "awT", 2, [128, 512], BF16)
    mrow = sb("mrow", [128, 1])
    lsum = sb("lsum", [128, 1])
    P.dma("y", bcc[:], bcc_d)
    for j in range(8):
        P.dma("y", stage[:, 0:512], xg_d[j * 128:(j + 1) * 128, :])
        P.cp("v", xgb[j][:], stage[:, 0:512])
    wcf_c = wcf_d.rearrange("(c p) o -> c p o", p=128)
    for oc in range(8):
        pk = ps[0]
        for c in range(32):
            P.dma("y", stage[:, 0:128], wcf_c[c][:, oc * 128:(oc + 1) * 128])
            P.cp("v", ssf[0:128, 0:128], stage[:, 0:128], rk=["stage"], wk=["ssf"])
            P.cp("v", kfmb[0][:, 0:128], ssf[0:128, 0:128], rk=["ssf"],
                 wk=[_apkey(kfmb[0].ap())])
            k, dsub = c // 8, c % 8
            rhs = bass.AP(tensor=xgb[dsub].ap().tensor, offset=k,
                          ap=[[xgb[dsub].ap().ap[0][0], 128], [4, 128]])
            P.mm(pk[:, 0:128], kfmb[0][:, 0:128], rhs, start=(c == 0), stop=(c == 31),
                 rk=[_apkey(kfmb[0].ap()), _apkey(xgb[dsub].ap())])
        P.act(gt32[:, 0:128], pk[:, 0:128], AF.Identity, bias=bcc[:, oc:oc + 1], e="s")
        P.dma("y", stage[:, 0:128], gmf_d[oc * 128:(oc + 1) * 128, :])
        P.cp("v", gt32[:, 128:256], stage[:, 0:128], rk=["stage"], wk=["gt32"])
        P.cp("v", gtf[oc][:], gt32[:], rk=["gt32"], wk=[_apkey(gtf[oc].ap())])
        for kc in range(2):
            P.transpose(ps[1][:, 0:128], gt32[:, kc * 128:(kc + 1) * 128], idf[:],
                        rk=["gt32", "idf"])
            P.cp("s", gtt[kc][:, oc * 128:(oc + 1) * 128], ps[1][:, 0:128])
    for qb in range(4):
        for i in range(4):
            qt = qb * 4 + i
            pl = ps[2]
            for dj in range(8):
                P.mm(pl[:, 0:256], xob[dj][:, qt * 128:(qt + 1) * 128], gtf[dj][:],
                     start=(dj == 0), stop=(dj == 7))
            P.red("v", mrow[:], pl[:, 0:256], ALU.max, rk=[_apkey(pl)], wk=["mrow"])
            P.ts("v", mrow[:], mrow[:], -1.0 / 32.0, None, ALU.mult)
            P.act(aw[:], pl[:, 0:256], AF.Exp, scale=1.0 / 32.0, bias=mrow[:],
                  accum_out=lsum[:], e="s")
            P.op("v", lambda eng: eng.reciprocal(lsum[:], lsum[:]), reads=["lsum"], writes=["lsum"])
            P.ts("v", aw[:], aw[:], lsum[:], None, ALU.mult)
            for kc in range(2):
                P.transpose(ps[1][:, 0:128], aw[:, kc * 128:(kc + 1) * 128], idf[:])
                P.cp("s", awT[kc][:, i * 128:(i + 1) * 128], ps[1][:, 0:128])
        for dc in range(8):
            pg = ps[3]
            for kc in range(2):
                P.mm(pg[:], gtt[kc][:, dc * 128:(dc + 1) * 128], awT[kc][:],
                     start=(kc == 0), stop=(kc == 1))
            P.cp("s", otmp[:], pg[:], rk=[_apkey(pg)], wk=["otmp"])
            for i in range(4):
                P.transpose(ps[1][:, 0:128], otmp[:, i * 128:(i + 1) * 128], idf[:])
                P.cp("s", outt[:, dc * 128:(dc + 1) * 128], ps[1][:, 0:128],
                     rk=["ps1"], wk=["outt"])
                P.dma("g", glob_d[(qb * 4 + i) * 128:(qb * 4 + i + 1) * 128,
                                  dc * 128:(dc + 1) * 128], outt[:, dc * 128:(dc + 1) * 128])

    for qb in range(4):
        for i in range(4):
            P.memset("v", acc[i][:], 0.0)
        Js = [J for J in range(3) if (J == 1) or (J == 0 and qb < 2) or (J == 2 and qb >= 2)]
        for J in Js:
            for j in range(8):
                P.dma("y", stage[:], wkfm_d[J, j * 128:(j + 1) * 128, :])
                P.cp("v", kfmb[j][:], stage[:])
            for kt in range(16):
                P.dma("y", stage[:, 0:1024], wktm_d[J, kt * 128:(kt + 1) * 128, :])
                P.cp("v", wvtb[kt][:], stage[:, 0:1024])
            qoff = qb * 512 + 1024 * (1 - J)
            kts = [kt for kt in range(16) if kt * 128 <= qoff + 511]
            for n, kt in enumerate(kts):
                psS = ps[4 + n % 2]
                for dj in range(8):
                    P.mm(psS[:], kfmb[dj][:, kt * 128:(kt + 1) * 128],
                         xob[dj][:, qb * 512:(qb + 1) * 512],
                         start=(dj == 0), stop=(dj == 7))
                dl = qoff - kt * 128
                sl0 = 512 + dl if dl < 3072 else 3583
                P.tt("v", ssf[:], psS[:], btm[:, sl0:sl0 + 512], ALU.add)
                P.act(ptb[kt][:], ssf[:], AF.Exp, scale=1.0 / 32.0, bias=c32[:], e="s")
                P.mm(ps[6][0:1, :], onecol_b[:], ptb[kt][:],
                     start=(n == 0), stop=(n == len(kts) - 1), wk=["psl"])
            P.cp("s", lrow[:], ps[6][0:1, :], rk=["psl"], wk=["lrow"])
            P.ts("v", lrow[:], lrow[:], 1e-30, None, ALU.add)
            P.op("v", lambda eng: eng.reciprocal(lrow[:], lrow[:]), reads=["lrow"], writes=["lrow"])
            for i in range(4):
                P.transpose(ps[2][:, 0:1], lrow[0:1, i * 128:(i + 1) * 128], idf[0:1, 0:1])
                P.cp("v", lcol[:], ps[2][:, 0:1], rk=[_apkey(ps[2])], wk=["lcol"])
                P.tt("v", fac[:, i:i + 1], lcol[:], wjc[:, J * 16 + qb * 4 + i:J * 16 + qb * 4 + i + 1],
                     ALU.mult, rk=["lcol", "wjc_s"], wk=["fac"])
            for dc in range(8):
                pv = ps[7]
                for n, kt in enumerate(kts):
                    P.mm(pv[:], wvtb[kt][:, dc * 128:(dc + 1) * 128], ptb[kt][:],
                         start=(n == 0), stop=(n == len(kts) - 1))
                P.cp("s", otmp[:], pv[:], rk=[_apkey(pv)], wk=["otmp"])
                for i in range(4):
                    P.transpose(ps[1][:, 0:128], otmp[:, i * 128:(i + 1) * 128], idf[:])
                    P.stt("v", acc[i][:, dc * 128:(dc + 1) * 128], ps[1][:, 0:128],
                          fac[:, i:i + 1], acc[i][:, dc * 128:(dc + 1) * 128],
                          ALU.mult, ALU.add, rk=[_apkey(ps[1]), "fac", _apkey(acc[i].ap())],
                          wk=[_apkey(acc[i].ap())])
        for i in range(4):
            P.ts("v", outt[:], acc[i][:], winv[:, qb * 4 + i:qb * 4 + i + 1], None, ALU.mult,
                 rk=[_apkey(acc[i].ap()), "winv_s"], wk=["outt"])
            P.dma("g", loc_d[(qb * 4 + i) * 128:(qb * 4 + i + 1) * 128, :], outt[:])


def build_l3a2(nc, P, es):
    x_d = nc.dram_tensor("x", [TOK, D], F32, kind="ExternalInput").ap()
    loc_d = nc.dram_tensor("local", [TOK, D], F32, kind="ExternalInput").ap()
    glob_d = nc.dram_tensor("glob", [TOK, D], F32, kind="ExternalInput").ap()
    bch_d = nc.dram_tensor("bcho", [TOK, D], F32, kind="ExternalInput").ap()
    wmt_d = nc.dram_tensor("wmt", [2 * D, D], F32, kind="ExternalInput").ap()
    bmr_d = nc.dram_tensor("bmr", [1, D], F32, kind="ExternalInput").ap()
    wot_d = nc.dram_tensor("wot", [D, D], F32, kind="ExternalInput").ap()
    bor_d = nc.dram_tensor("bor", [1, D], F32, kind="ExternalInput").ap()
    idf_d = nc.dram_tensor("identf", [128, 128], F32, kind="ExternalInput").ap()
    xo_d = nc.dram_tensor("xo", [TOK, D], F32, kind="ExternalOutput").ap()

    sb = lambda n, shp, dt=F32: es.enter_context(nc.sbuf_tensor(n, shp, dt))
    idf = sb("idf", [128, 128])
    onerow_b = es.enter_context(nc.sbuf_tensor("onerow_b", [1, 128], BF16))
    stage = sb("stage", [128, 1024])
    wm = sb_tiles(nc, es, "wm", 16, [128, 1024], BF16)
    wo3 = sb_tiles(nc, es, "wo3", 8, [128, 1024], BF16)
    lt = sb("lt", [128, 1024])
    gt_ = sb("gt_", [128, 1024])
    bt = sb("bt", [128, 1024])
    xt = sb("xt", [128, 1024])
    lgf = sb_tiles(nc, es, "lgf", 16, [128, 128], BF16)
    mw = sb("mw", [128, 1024])
    mixf = sb_tiles(nc, es, "mixf", 8, [128, 128], BF16)
    res = sb("res", [128, 1024])
    outt = sb("outt", [128, 1024])
    ln_sc = {"mean": sb("ln_mean", [128, 1]), "var": sb("ln_var", [128, 1]),
             "xc": sb("ln_xc", [128, 1024]), "sq": sb("ln_sq", [128, 1024]),
             "eps": sb("ln_eps", [128, 1])}
    P.memset("v", ln_sc["eps"], 1e-5)
    P.memset("v", onerow_b[:], 1.0)
    ps = [es.enter_context(nc.psum_tensor(f"ps{i}", [128, 512], F32)) for i in range(6)]

    P.dma("y", idf[:], idf_d)
    bmhi, bmlo = emit_bias_hilo(P, sb, "bm", bmr_d, D)
    bohi, bolo = emit_bias_hilo(P, sb, "bo3", bor_d, D)
    for j in range(16):
        P.dma("y", stage[:], wmt_d[j * 128:(j + 1) * 128, :])
        P.cp("v", wm[j][:], stage[:])
    for j in range(8):
        P.dma("y", stage[:], wot_d[j * 128:(j + 1) * 128, :])
        P.cp("v", wo3[j][:], stage[:])

    for ti in range(NT):
        P.dma("y", lt[:], loc_d[ti * 128:(ti + 1) * 128, :])
        P.dma("y", gt_[:], glob_d[ti * 128:(ti + 1) * 128, :])
        P.dma("y", bt[:], bch_d[ti * 128:(ti + 1) * 128, :])
        P.dma("y", xt[:], x_d[ti * 128:(ti + 1) * 128, :])
        for j in range(8):
            P.transpose(ps[5][:, 0:128], lt[:, j * 128:(j + 1) * 128], idf[:])
            P.cp("s", lgf[j][:], ps[5][:, 0:128], rk=[_apkey(ps[5])], wk=[_apkey(lgf[j].ap())])
            P.transpose(ps[5][:, 0:128], gt_[:, j * 128:(j + 1) * 128], idf[:])
            P.cp("s", lgf[8 + j][:], ps[5][:, 0:128], rk=[_apkey(ps[5])],
                 wk=[_apkey(lgf[8 + j].ap())])
        for ob in range(2):
            pm = ps[ob]
            for j in range(16):
                P.mm(pm[:], lgf[j][:], wm[j][:, ob * 512:(ob + 1) * 512],
                     start=(j == 0), stop=False)
            P.mm(pm[:], onerow_b[:], bmhi[0:1, ob * 512:(ob + 1) * 512],
                 start=False, stop=False)
            P.mm(pm[:], onerow_b[:], bmlo[0:1, ob * 512:(ob + 1) * 512],
                 start=False, stop=True)
            P.act(mw[:, ob * 512:(ob + 1) * 512], pm[:], AF.Sigmoid, e="s")
        P.tt("v", res[:], lt[:], gt_[:], ALU.subtract)
        P.tt("v", res[:], res[:], mw[:], ALU.mult)
        P.tt("v", res[:], res[:], gt_[:], ALU.add)
        P.tt("v", res[:], res[:], bt[:], ALU.add)
        for j in range(8):
            P.transpose(ps[5][:, 0:128], res[:, j * 128:(j + 1) * 128], idf[:])
            P.cp("s", mixf[j][:], ps[5][:, 0:128], rk=[_apkey(ps[5])], wk=[_apkey(mixf[j].ap())])
        for ob in range(2):
            po = ps[2 + ob]
            for j in range(8):
                P.mm(po[:], mixf[j][:], wo3[j][:, ob * 512:(ob + 1) * 512],
                     start=(j == 0), stop=False)
            P.mm(po[:], onerow_b[:], bohi[0:1, ob * 512:(ob + 1) * 512],
                 start=False, stop=False)
            P.mm(po[:], onerow_b[:], bolo[0:1, ob * 512:(ob + 1) * 512],
                 start=False, stop=True)
            P.tt("v", res[:, ob * 512:(ob + 1) * 512], po[:], xt[:, ob * 512:(ob + 1) * 512],
                 ALU.add)
        emit_layernorm_tile(P, res[:], outt[:], D, ln_sc)
        P.dma("g", xo_d[ti * 128:(ti + 1) * 128, :], outt[:])


def _run_l3a(x_flat, gm, Wc, bc, Wm, bm, Wo, bo):
    xs = _shards_tm(x_flat)
    xb = x_flat.reshape(B, S, D)
    WINDOW, STRIDE = 2048, 1024
    nw = (S - WINDOW) // STRIDE + 1
    tri = np.linspace(0.5, 1.5, WINDOW)
    wsum = np.zeros(S)
    for w in range(nw):
        wsum[w * STRIDE:w * STRIDE + WINDOW] += tri
    btm = np.full((128, 4096), -1e30, np.float32)
    ii = np.arange(128)[:, None]
    cc = np.arange(4096)[None, :]
    btm[ii <= cc - 512] = 0.0
    wcf = np.ascontiguousarray(Wc.transpose(2, 1, 0).reshape(4096, D))
    bc_c = np.ascontiguousarray(bc.reshape(8, 128).T)
    in1 = []
    in0 = []
    in2 = []
    for c in range(NCORES):
        b = c // (NCORES // B)
        r0 = 2048 * (c % (NCORES // B))
        m = r0 // 1024
        wk = np.zeros((3, 2048, D), x_flat.dtype)
        wjc = np.zeros((3, 128, 16))
        for J in range(3):
            w = m - 1 + J
            if 0 <= w <= nw - 1:
                wk[J] = xb[b, w * STRIDE:w * STRIDE + WINDOW]
                t = np.arange(TOK)
                qoff = t + 1024 * (1 - J)
                wt = np.where((qoff >= 0) & (qoff < WINDOW), tri[np.clip(qoff, 0, WINDOW - 1)], 0.0)
                wjc[J] = wt.reshape(16, 128).T
        winv = (1.0 / (wsum[r0:r0 + TOK] + 1e-6)).reshape(16, 128).T
        wkfm = np.ascontiguousarray(wk.transpose(0, 2, 1))
        xfmo = np.ascontiguousarray(xb[b, r0:r0 + TOK].T)
        xbc_ = np.ascontiguousarray(xb[b].T[(c % 4) * 256:(c % 4) * 256 + 256])
        in0.append({"xbc": xbc_})
        in1.append({"xfmo": xfmo, "wkfm": wkfm, "wktm": np.ascontiguousarray(wk),
                    "wjc": _f32c(wjc), "winv": _f32c(winv),
                    "btmask": btm, "xg512": np.ascontiguousarray(xb[b, 0:512].T),
                    "wcf": wcf, "bc_c": bc_c,
                    "gmf": np.ascontiguousarray(gm.reshape(128, D).T),
                    "identf": _IDENT32})
    r0_ = _launch(build_l3a0, in0, ["bch"])
    r1 = _launch(build_l3a1, in1, ["local", "glob"])
    bch_full = np.zeros((B, S, D), np.float32)
    for c in range(NCORES):
        b = c // (NCORES // B)
        dsl = (c % 4) * 256
        bch_full[b, :, dsl:dsl + 256] = r0_[c]["bch"].T
    bch_flat = bch_full.reshape(NTOK, D)
    wmt = np.ascontiguousarray(Wm.T)
    wot = np.ascontiguousarray(Wo.T)
    for c in range(NCORES):
        in2.append({"x": xs[c], "local": r1[c]["local"], "glob": r1[c]["glob"],
                    "bcho": np.ascontiguousarray(bch_flat[c * TOK:(c + 1) * TOK]),
                    "wmt": wmt, "bmr": bm.reshape(1, D), "wot": wot,
                    "bor": bo.reshape(1, D), "identf": _IDENT32})
    return _launch(build_l3a2, in2, ["xo"])



def build_l3f(nc, P, es):
    E, DF = 8, 4096
    x_d = nc.dram_tensor("x", [TOK, D], F32, kind="ExternalInput").ap()
    wg8_d = nc.dram_tensor("wg8", [D, E], F32, kind="ExternalInput").ap()
    wgall_d = nc.dram_tensor("wg_all", [E, D, DF], F32, kind="ExternalInput").ap()
    wuall_d = nc.dram_tensor("wu_all", [E, D, DF], F32, kind="ExternalInput").ap()
    wdall_d = nc.dram_tensor("wd_all", [E, DF, D], F32, kind="ExternalInput").ap()
    bg_d = nc.dram_tensor("bg", [E, 128, 32], F32, kind="ExternalInput").ap()
    bu_d = nc.dram_tensor("bu", [E, 128, 32], F32, kind="ExternalInput").ap()
    bd_d = nc.dram_tensor("bd", [E, D], F32, kind="ExternalInput").ap()
    idf_d = nc.dram_tensor("identf", [128, 128], F32, kind="ExternalInput").ap()
    xo_d = nc.dram_tensor("xo", [TOK, D], F32, kind="ExternalOutput").ap()
    wgc_d = nc.dram_tensor("wgc", [E, 8, 128, DF], BF16, kind="Internal").ap()
    wuc_d = nc.dram_tensor("wuc", [E, 8, 128, DF], BF16, kind="Internal").ap()
    wdc_d = nc.dram_tensor("wdc", [E, 32, 128, D], BF16, kind="Internal").ap()

    sb = lambda n, shp, dt=F32: es.enter_context(nc.sbuf_tensor(n, shp, dt))
    idf = sb("idf", [128, 128])
    onerow = sb("onerow", [1, 128])
    stage = sb("stage", [128, 2048])
    cbuf = sb("cbuf", [128, 2048], BF16)
    xt = sb("xt", [128, 1024])
    xfb = sb_tiles(nc, es, "xfb", 8, [128, TOK], BF16)
    wg8b = sb_tiles(nc, es, "wg8b", 8, [128, E])
    xff32 = sb("xff32", [128, 1024])
    bgs = sb("bgs", [128, 32])
    bus = sb("bus", [128, 32])
    bds = sb("bds", [8, 1024])
    cmb = sb("cmb", [128, 8])
    esb = sb("esb", [128, 8])
    m1 = sb("m1c", [128, 1])
    v1 = sb("v1c", [128, 1])
    v2 = sb("v2c", [128, 1])
    den = sb("denc", [128, 1])
    msk = sb("mskc", [128, 8])
    cmbT8 = sb("cmbT8", [8, TOK])
    cbrow = sb("cbrow", [1, TOK])
    cbc = sb("cbc", [128, 512])
    wgb = sb_tiles(nc, es, "wgb", 2, [128, 1024], BF16)
    wub = sb_tiles(nc, es, "wub", 2, [128, 1024], BF16)
    wdb = sb_tiles(nc, es, "wdb", 1, [128, 4096], BF16)
    gev = sb_tiles(nc, es, "gev", 1, [128, 512])
    hfb = sb_tiles(nc, es, "hfb", 32, [128, 512], BF16)
    yac = sb_tiles(nc, es, "yac", 8, [128, TOK])
    tmp5 = sb("tmp5", [128, 512])
    res = sb("res", [128, 1024])
    outt = sb("outt", [128, 1024])
    ln_sc = {"mean": sb("ln_mean", [128, 1]), "var": sb("ln_var", [128, 1]),
             "xc": sb("ln_xc", [128, 1024]), "sq": sb("ln_sq", [128, 1024]),
             "eps": sb("ln_eps", [128, 1])}
    P.memset("v", ln_sc["eps"], 1e-5)
    P.memset("v", onerow[:], 1.0)
    ps = [es.enter_context(nc.psum_tensor(f"ps{i}", [128, 512], F32)) for i in range(8)]

    P.dma("y", idf[:], idf_d)
    P.dma("y", bds[:], bd_d)
    prebufs = [(stage, cbuf), (yac[0], xfb[0])]
    pk_ = [0]

    def precast(src_ap, dst_ap, width):
        st_, cb_ = prebufs[pk_[0] % 2]
        eng = "v" if pk_[0] % 2 == 0 else "s"
        pk_[0] += 1
        P.dma("y", st_[:, 0:width], src_ap)
        P.cp(eng, cb_[:, 0:width], st_[:, 0:width])
        P.dma("g", dst_ap, cb_[:, 0:width])

    for e in range(E):
        for c in range(8):
            for half in range(2):
                precast(wgall_d[e, c * 128:(c + 1) * 128, half * 2048:(half + 1) * 2048],
                        wgc_d[e, c][:, half * 2048:(half + 1) * 2048], 2048)
                precast(wuall_d[e, c * 128:(c + 1) * 128, half * 2048:(half + 1) * 2048],
                        wuc_d[e, c][:, half * 2048:(half + 1) * 2048], 2048)
        for c in range(32):
            precast(wdall_d[e, c * 128:(c + 1) * 128, :], wdc_d[e, c], 1024)
    for j in range(8):
        P.dma("y", stage[:, 0:E], wg8_d[j * 128:(j + 1) * 128, :])
        P.cp("v", wg8b[j][:], stage[:, 0:E])

    for ti in range(NT):
        P.dma("y", xt[:], x_d[ti * 128:(ti + 1) * 128, :])
        for j in range(8):
            P.transpose(ps[6][:, 0:128], xt[:, j * 128:(j + 1) * 128], idf[:])
            P.cp("s", xfb[j][:, ti * 128:(ti + 1) * 128], ps[6][:, 0:128])
            P.cp("s", xff32[:, j * 128:(j + 1) * 128], ps[6][:, 0:128])
        pl = ps[7]
        for dj in range(8):
            P.mm(pl[:, 0:E], xff32[:, dj * 128:(dj + 1) * 128], wg8b[dj][:],
                 start=(dj == 0), stop=(dj == 7))
        P.red("v", m1[:], pl[:, 0:E], ALU.max, rk=[_apkey(pl)], wk=["m1c"])
        P.ts("v", m1[:], m1[:], -1.0, None, ALU.mult)
        P.act(esb[:], pl[:, 0:E], AF.Exp, bias=m1[:], e="s",
              rk=[_apkey(pl), "m1c"], wk=["esb"])
        P.red("v", v1[:], esb[:], ALU.max)
        P.ts("v", msk[:], esb[:], v1[:], None, ALU.is_ge)
        P.ts("v", msk[:], msk[:], -1.0, 1.0, ALU.mult, ALU.add)
        P.tt("v", cmb[:], esb[:], msk[:], ALU.mult)
        P.red("v", v2[:], cmb[:], ALU.max, rk=["cmb"], wk=["v2c"])
        P.ts("v", msk[:], esb[:], v2[:], None, ALU.is_ge)
        P.tt("v", den[:], v1[:], v2[:], ALU.add)
        P.op("v", lambda eng: eng.reciprocal(den[:], den[:]), reads=["denc"], writes=["denc"])
        P.tt("v", cmb[:], esb[:], msk[:], ALU.mult)
        P.ts("v", cmb[:], cmb[:], den[:], None, ALU.mult)
        P.transpose(ps[6][0:8, 0:128], cmb[:], idf[:])
        P.cp("s", cmbT8[:, ti * 128:(ti + 1) * 128], ps[6][0:8, 0:128])
    for e in range(E):
        P.dma("g", cbrow[:], cmbT8[e:e + 1, :])
        P.dma("y", bgs[:], bg_d[e])
        P.dma("y", bus[:], bu_d[e])
        for tb in range(4):
            P.mm(ps[5][:], onerow[:], cbrow[0:1, tb * 512:(tb + 1) * 512],
                 start=True, stop=True)
            P.cp("s", cbc[:], ps[5][:], rk=[_apkey(ps[5])], wk=["cbc"])
            for fc in range(32):
                P.dma("y", wgb[fc % 2][:].rearrange("p (c f) -> p c f", c=8),
                      wgc_d[e, :, :, fc * 128:(fc + 1) * 128].rearrange("c p f -> p c f"))
                P.dma("g", wub[fc % 2][:].rearrange("p (c f) -> p c f", c=8),
                      wuc_d[e, :, :, fc * 128:(fc + 1) * 128].rearrange("c p f -> p c f"))
                pg = ps[fc % 2]
                pu = ps[2 + fc % 2]
                for dj in range(8):
                    P.mm(pg[:], wgb[fc % 2][:, dj * 128:(dj + 1) * 128],
                         xfb[dj][:, tb * 512:(tb + 1) * 512], start=(dj == 0), stop=(dj == 7))
                for dj in range(8):
                    P.mm(pu[:], wub[fc % 2][:, dj * 128:(dj + 1) * 128],
                         xfb[dj][:, tb * 512:(tb + 1) * 512], start=(dj == 0), stop=(dj == 7))
                g_ = gev[0]
                P.act(g_[:], pg[:], AF.Silu, bias=bgs[:, fc:fc + 1], e="s")
                P.act(tmp5[:], pu[:], AF.Identity, bias=bus[:, fc:fc + 1], e="s")
                P.tt("v", hfb[fc][:], g_[:], tmp5[:], ALU.mult)
            for dc in range(8):
                P.dma("g", wdb[0][:].rearrange("p (c f) -> p c f", c=32),
                      wdc_d[e, :, :, dc * 128:(dc + 1) * 128].rearrange("c p f -> p c f"))
                pv = ps[4 + dc % 2]
                for fc in range(32):
                    P.mm(pv[:], wdb[0][:, fc * 128:(fc + 1) * 128], hfb[fc][:],
                         start=(fc == 0), stop=(fc == 31))
                P.tt("v", tmp5[:], pv[:], cbc[:], ALU.mult,
                     rk=[_apkey(pv), "cbc"], wk=["tmp5"])
                if e == 0:
                    P.cp("v", yac[dc][:, tb * 512:(tb + 1) * 512], tmp5[:])
                else:
                    P.tt("v", yac[dc][:, tb * 512:(tb + 1) * 512],
                         yac[dc][:, tb * 512:(tb + 1) * 512], tmp5[:], ALU.add)

    for ti in range(NT):
        P.dma("y", xt[:], x_d[ti * 128:(ti + 1) * 128, :])
        for ob in range(2):
            po = ps[6 + ob]
            for q in range(4):
                dc = ob * 4 + q
                P.transpose(po[:, q * 128:(q + 1) * 128],
                            yac[dc][:, ti * 128:(ti + 1) * 128], idf[:],
                            rk=[_apkey(yac[dc].ap()), "idf"], wk=[_apkey(po)])
            P.mm(ps[5][:], cmbT8[:, ti * 128:(ti + 1) * 128],
                 bds[:, ob * 512:(ob + 1) * 512], start=True, stop=True,
                 rk=["cmbT8", "bds"], wk=[_apkey(ps[5])])
            P.tt("v", res[:, ob * 512:(ob + 1) * 512], po[:], xt[:, ob * 512:(ob + 1) * 512],
                 ALU.add)
            P.tt("v", res[:, ob * 512:(ob + 1) * 512], res[:, ob * 512:(ob + 1) * 512],
                 ps[5][:], ALU.add)
        emit_layernorm_tile(P, res[:], outt[:], D, ln_sc)
        P.dma("g", xo_d[ti * 128:(ti + 1) * 128, :], outt[:])


def _run_l3f(x_shards, Wg, w_gate, b_gate, w_up, b_up, w_down, b_down):
    bg = np.ascontiguousarray(b_gate.reshape(8, 32, 128).transpose(0, 2, 1))
    bu = np.ascontiguousarray(b_up.reshape(8, 32, 128).transpose(0, 2, 1))
    in_maps = [{"x": x_shards[c], "wg8": Wg, "wg_all": w_gate, "wu_all": w_up,
                "wd_all": w_down, "bg": bg, "bu": bu, "bd": b_down,
                "identf": _IDENT32} for c in range(NCORES)]
    return _launch(build_l3f, in_maps, ["xo"])



def kernel(L0_attn_x, L0_attn_Wq, L0_attn_Wk, L0_attn_Wv, L0_attn_Wo,
           L0_ffn_w_gate, L0_ffn_w_up, L0_ffn_w_down,
           L1_attn_Wqkv, L1_attn_Wproj, L1_attn_bproj,
           L1_ffn_w1, L1_ffn_b1, L1_ffn_w2, L1_ffn_b2,
           L2_attn_Wqkv, L2_attn_bqkv, L2_attn_Wout, L2_attn_bout,
           L2_attn_gamma, L2_attn_beta, L2_attn_W1, L2_attn_b1, L2_attn_W2,
           L2_attn_b2, L2_ffn_clip_x, L2_ffn_clip_down, L2_ffn_w_gate,
           L2_ffn_w_up, L2_ffn_w_down,
           L3_attn_global_memory, L3_attn_Wc, L3_attn_bc, L3_attn_Wm,
           L3_attn_bm, L3_attn_Wo, L3_attn_bo,
           L3_ffn_Wg, L3_ffn_w_gate, L3_ffn_b_gate, L3_ffn_w_up, L3_ffn_b_up,
           L3_ffn_w_down, L3_ffn_b_down):
    x = np.ascontiguousarray(np.asarray(L0_attn_x).reshape(NTOK, D))
    r = _run_l0a(x, np.asarray(L0_attn_Wq), np.asarray(L0_attn_Wk),
                 np.asarray(L0_attn_Wv), np.asarray(L0_attn_Wo))
    x = np.concatenate([np.asarray(t["xo"]) for t in r], 0)
    r = _run_l0f(_shards_tm(x), np.asarray(L0_ffn_w_gate),
                 np.asarray(L0_ffn_w_up), np.asarray(L0_ffn_w_down))
    x = np.concatenate([np.asarray(t["xo"]) for t in r], 0)
    r = _run_l1a(_shards_tm(x), np.asarray(L1_attn_Wqkv),
                 np.asarray(L1_attn_Wproj), np.asarray(L1_attn_bproj))
    x = np.concatenate([np.asarray(t["xo"]) for t in r], 0)
    r = _run_l1f(_shards_tm(x), np.asarray(L1_ffn_w1), np.asarray(L1_ffn_b1),
                 np.asarray(L1_ffn_w2), np.asarray(L1_ffn_b2))
    x = np.concatenate([np.asarray(t["xo"]) for t in r], 0)
    r = _run_l2a(x, np.asarray(L2_attn_Wqkv), np.asarray(L2_attn_bqkv),
                 np.asarray(L2_attn_Wout), np.asarray(L2_attn_bout),
                 np.asarray(L2_attn_gamma), np.asarray(L2_attn_beta),
                 np.asarray(L2_attn_W1), np.asarray(L2_attn_b1),
                 np.asarray(L2_attn_W2), np.asarray(L2_attn_b2))
    x = np.concatenate([np.asarray(t["xo"]) for t in r], 0)
    r = _run_l2f(_shards_tm(x), np.asarray(L2_ffn_clip_x),
                 np.asarray(L2_ffn_clip_down), np.asarray(L2_ffn_w_gate),
                 np.asarray(L2_ffn_w_up), np.asarray(L2_ffn_w_down))
    x = np.concatenate([np.asarray(t["xo"]) for t in r], 0)
    r = _run_l3a(x, np.asarray(L3_attn_global_memory), np.asarray(L3_attn_Wc),
                 np.asarray(L3_attn_bc), np.asarray(L3_attn_Wm),
                 np.asarray(L3_attn_bm), np.asarray(L3_attn_Wo),
                 np.asarray(L3_attn_bo))
    x = np.concatenate([np.asarray(t["xo"]) for t in r], 0)
    r = _run_l3f(_shards_tm(x), np.asarray(L3_ffn_Wg), np.asarray(L3_ffn_w_gate),
                 np.asarray(L3_ffn_b_gate), np.asarray(L3_ffn_w_up),
                 np.asarray(L3_ffn_b_up), np.asarray(L3_ffn_w_down),
                 np.asarray(L3_ffn_b_down))
    x = np.concatenate([np.asarray(t["xo"]) for t in r], 0)
    return x.reshape(B, S, D)
